# version 08
# baseline (speedup 1.0000x reference)
import contextlib
import math

import numpy as np

import concourse.bass as bass
import concourse.mybir as mybir
from concourse.alu_op_type import AluOpType
from concourse.bass_utils import run_bass_kernel_spmd

F32 = mybir.dt.float32
BF16 = mybir.dt.bfloat16
AF = mybir.ActivationFunctionType
ADD = AluOpType.add
SUB = AluOpType.subtract
MULT = AluOpType.mult
MAXOP = AluOpType.max
AXL = mybir.AxisListType.X

S = 4096
D = 1024
TOK = 2048
NCH = 8
CH = 512
HPC = 8
NKT = 32
RELB = 32
RG = [[0, 1], [2, 3], [4, 5], [6, 7]]
VRING = 4
KRING = 4
CHORD = [0, 4, 1, 5, 2, 6, 3, 7]


def _tile_w(w):
    din, dout = w.shape
    dt_ = min(128, dout)
    ndi, ndo = din // 128, dout // dt_
    return np.ascontiguousarray(
        w.reshape(ndi, 128, ndo, dt_).transpose(2, 0, 1, 3))


def _rel_bucket(d):
    n = max(int(d), 0)
    me = RELB // 2
    if n < me:
        return n
    v = me + int(math.log(max(n, 1) / me) / math.log(128 / me) * (RELB - me))
    return min(v, RELB - 1)


def _build_onehot():
    n = 2 * 16384 + 2
    oh = np.zeros((RELB, n), dtype=np.float32)
    for t in range(2):
        for kr in range(128):
            base = t * 16384 + kr * 128
            for qc in range(128):
                d = 128 * t + qc - kr
                if d < 0:
                    continue
                oh[_rel_bucket(d), base + qc] += 8.0
                oh[31, base + qc] -= 8.0
    oh[31, n - 2] = 8.0
    return oh


class Ctr:
    __slots__ = ("sem", "n", "name")

    def __init__(self, sem, name):
        self.sem = sem
        self.n = 0
        self.name = name

    def inc(self, k=1):
        self.n += k
        return self.n


class K:
    ENGS = ("tensor", "vector", "scalar", "sync", "gpsimd")

    def __init__(self, nc, block, es):
        self.nc = nc
        self.block = block
        self.es = es
        self.hw = {e: {} for e in self.ENGS}
        self.bar = []
        self.allc = []
        self.pend = []
        for n in ("pe", "dv", "ac", "gp", "col", "dout"):
            c = Ctr(es.enter_context(nc.semaphore("s_" + n)), n)
            setattr(self, n, c)
            self.allc.append(c)

    def ds(self, name):
        c = Ctr(self.es.enter_context(self.nc.semaphore("d_" + name)), name)
        self.allc.append(c)
        return c

    def barrier(self):
        self.flush()
        self.bar = [(c, c.n) for c in self.allc]

    def _sect(self, eng, waits, fn, tick):
        allw = list(self.bar) + list(waits or ())
        if tick is not None and eng in ("vector", "scalar", "gpsimd") \
                and tick.n > 0:
            allw.append((tick, tick.n))
        tv = tick.inc() if tick is not None else None
        self.pend.append((eng, allw, fn, tick))
        return tv

    def flush(self):
        pend = self.pend
        self.pend = []
        for eng in self.ENGS:
            items = [it for it in pend if it[0] == eng]
            if not items:
                continue
            hw = self.hw[eng]

            def body(e, items=items, hw=hw, eng=eng):
                for _, allw, fn, tick in items:
                    for w in allw:
                        if w is None:
                            continue
                        c, v = w
                        if v is None or v <= 0 or hw.get(c.name, 0) >= v:
                            continue
                        e.wait_ge(c.sem, v)
                        hw[c.name] = v
                    last = fn(e)
                    if tick is not None:
                        if last is None:
                            assert eng in ("gpsimd", "vector", "scalar"), eng
                            last = e.nop()
                        last.then_inc(tick.sem, 1)
            getattr(self.block, eng)(body)

    def t(self, fn, waits=None):
        return self._sect("tensor", waits, fn, self.pe)

    def v(self, fn, waits=None):
        return self._sect("vector", waits, fn, self.dv)

    def a(self, fn, waits=None):
        return self._sect("scalar", waits, fn, self.ac)

    def g(self, fn, waits=None):
        return self._sect("gpsimd", waits, fn, self.gp)

    def g_(self, fn, waits=None):
        return self._sect("gpsimd", waits, fn, None)

    def sy(self, fn, waits=None):
        return self._sect("sync", waits, fn, None)

    def adma(self, fn, waits=None):
        return self._sect("sync", waits, fn, None)


def build_program(taps=()):
    nc = bass.Bass("TRN2", target_bir_lowering=False)
    dt = nc.dram_tensor
    tap_outs = {}

    x_in = dt("x_in", [S, D], F32, kind="ExternalInput").ap()
    x_my = dt("x_my", [TOK, D], F32, kind="ExternalInput").ap()
    ident = dt("ident", [128, 128], F32, kind="ExternalInput").ap()
    onehot = dt("onehot", [RELB, 2 * 16384 + 2], F32, kind="ExternalInput").ap()
    wdq = dt("wdq", [2, 8, 128, 128], F32, kind="ExternalInput").ap()
    wuq = dt("wuq", [4, 2, 128, 128], F32, kind="ExternalInput").ap()
    wdkv = dt("wdkv", [2, 8, 128, 128], F32, kind="ExternalInput").ap()
    wuk = dt("wuk", [4, 2, 128, 128], F32, kind="ExternalInput").ap()
    wuv = dt("wuv", [4, 2, 128, 128], F32, kind="ExternalInput").ap()
    wo = dt("wo", [8, 4, 128, 128], F32, kind="ExternalInput").ap()
    w1f = dt("w1f", [32, 8, 128, 128], F32, kind="ExternalInput").ap()
    w2f = dt("w2f", [32, 8, 128, 128], F32, kind="ExternalInput").ap()
    h1f = dt("h1f", [32, 32, 128, 128], F32, kind="ExternalInput").ap()
    h2f = dt("h2f", [32, 32, 128, 128], F32, kind="ExternalInput").ap()
    w3f = dt("w3f", [8, 32, 128, 128], F32, kind="ExternalInput").ap()
    scl = dt("scl", [128, 136], F32, kind="ExternalInput").ap()
    wq1 = dt("wq1", [4, 8, 128, 128], F32, kind="ExternalInput").ap()
    wk1 = dt("wk1", [8, 128, 64], F32, kind="ExternalInput").ap()
    wv1 = dt("wv1", [8, 128, 64], F32, kind="ExternalInput").ap()
    wo1 = dt("wo1", [8, 4, 128, 128], F32, kind="ExternalInput").ap()
    bout1 = dt("bout1", [128, 8], F32, kind="ExternalInput").ap()
    rel = dt("rel", [RELB, HPC], F32, kind="ExternalInput").ap()
    wf1 = dt("wf1", [8, 8, 128, 128], F32, kind="ExternalInput").ap()
    wf2 = dt("wf2", [8, 8, 128, 128], F32, kind="ExternalInput").ap()
    wf3 = dt("wf3", [8, 8, 128, 128], F32, kind="ExternalInput").ap()
    bias1 = dt("bias1", [128, 8], F32, kind="ExternalInput").ap()
    y_out = dt("y_out", [TOK, D], F32, kind="ExternalOutput").ap()

    w1b = dt("w1b", [32, 8, 128, 128], BF16, kind="Internal").ap()
    w2b = dt("w2b", [32, 8, 128, 128], BF16, kind="Internal").ap()
    h1b = dt("h1b", [32, 32, 128, 128], BF16, kind="Internal").ap()
    h2b = dt("h2b", [32, 32, 128, 128], BF16, kind="Internal").ap()
    w3b = dt("w3b", [8, 32, 128, 128], BF16, kind="Internal").ap()
    v0h = dt("v0h", [NKT, 128, HPC, 65], F32, kind="Internal").ap()
    kfh = dt("kfh", [NKT, 128, 4, 2, 128], BF16, kind="Internal").ap()
    cqh = dt("cqh", [2, 128, S], F32, kind="Internal").ap()
    strh = dt("strh", [HPC, 2 * 16384 + 2], F32, kind="Internal").ap()
    x2h = dt("x2h", [TOK, D], F32, kind="Internal").ap()
    rs0i = [dt(f"rs0i{j}", [2, D, CH], F32, kind="Internal").ap() for j in range(4)]
    rs0o = [dt(f"rs0o{j}", [D, CH], F32, kind="Internal").ap() for j in range(4)]
    agi = [dt(f"agi{j}", [CH, D], F32, kind="Internal").ap() for j in range(4)]
    ago = [dt(f"ago{j}", [2, CH, D], F32, kind="Internal").ap() for j in range(4)]
    rs1i = [dt(f"rs1i{j}", [2, D, CH], F32, kind="Internal").ap() for j in range(4)]
    rs1o = [dt(f"rs1o{j}", [D, CH], F32, kind="Internal").ap() for j in range(4)]

    es = contextlib.ExitStack()

    def mksb(stack):
        def sbuf(name, shape, d=F32):
            return stack.enter_context(nc.sbuf_tensor(name, list(shape), d))
        return sbuf

    with nc.Block() as block, es:
        sbuf = mksb(es)
        k = K(nc, block, es)
        pb = [es.enter_context(nc.psum_tensor(f"pb{i}", [128, 512], F32))
              for i in range(8)]
        mm_free = [(k.dv, 0) for _ in range(8)]
        ds_consts = k.ds("consts")
        ds_conv = k.ds("conv")
        ds_oh = [k.ds(f"oh{i}") for i in range(2)]
        ds_st = [k.ds(f"st{i}") for i in range(2)]
        ds_str = k.ds("str")
        ds_scg = k.ds("scg")
        ds_x = k.ds("x")
        ds_cq = k.ds("cq")
        ds_ks = k.ds("ks")
        ds_vst = k.ds("vst")
        ds_wa = k.ds("wa")
        ds_wo = k.ds("wo")
        ds_cqb = k.ds("cqb")
        ds_vr = [k.ds(f"vr{i}") for i in range(VRING)]
        ds_kr = [k.ds(f"kr{i}") for i in range(KRING)]
        ds_rs = [k.ds(f"rs{i}") for i in range(2)]
        ds_ag = [k.ds(f"ag{i}") for i in range(2)]
        ds_stg = [k.ds("stg0"), k.ds("stg1")]
        ds_wr = [k.ds(f"wr{i}") for i in range(2)]
        ds_b01 = k.ds("b01")
        ds_w4 = k.ds("w4")
        ds_dup = k.ds("dup")
        ds_w5 = k.ds("w5")
        ds_o2 = k.ds("o2")

        identsb = sbuf("identsb", [128, 128])
        sc = sbuf("sc", [128, 64])
        onescol = sbuf("onescol", [128, 1])
        onesrow = sbuf("onesrow", [1, 128])
        epsv = sbuf("epsv", [128, 1])
        zerov = sbuf("zerov", [1, 1])
        bvec = sbuf("bvec", [128, 8])
        boutsb = sbuf("boutsb", [128, 8])
        bias1sb = sbuf("bias1sb", [128, 8])
        sclsb = sbuf("sclsb", [128, 136])
        lnws = sbuf("lnws", [128, 128])
        relsb = sbuf("relsb", [RELB, HPC])
        maxcsb = sbuf("maxcsb", [HPC, 1])
        xch = sbuf("xch", [128, 4, D])
        vts = sbuf("vts", [128, 512])
        stg = sbuf("stg", [128, 2, 512])
        stg64 = sbuf("stg64", [128, 512])
        rcrow = sbuf("rcrow", [128, 2, 512])
        rbs = sbuf("rbs", [128, 512])
        onesr64 = sbuf("onesr64", [128, 64])
        masks = sbuf("masks", [128, 4, 512], BF16)
        wuq_sb = sbuf("wuq_sb", [128, 4, 2, 128])

        stg_read = [0, 0]
        stg_bo = [0, 0]
        vts_rd = [0]
        cell = {"x_free": 0, "x_dma": None}

        ds_consts.inc(80)
        def _(s):
            s.dma_start(out=identsb[:], in_=ident[:]).then_inc(ds_consts.sem, 16)
            s.dma_start(out=boutsb[:], in_=bout1[:]).then_inc(ds_consts.sem, 16)
            s.dma_start(out=bias1sb[:], in_=bias1[:]).then_inc(ds_consts.sem, 16)
            s.dma_start(out=sclsb[:], in_=scl[:]).then_inc(ds_consts.sem, 16)
            s.dma_start(out=relsb[:], in_=rel[:]).then_inc(ds_consts.sem, 16)
            return None
        k.sy(_)
        ld_consts = ds_consts.n

        ds_conv.inc(80)
        def _(g):
            for src, dst in ((w1f, w1b), (w2f, w2b), (h1f, h1b), (h2f, h2b),
                             (w3f, w3b)):
                g.dma_start(out=dst[:], in_=src[:]).then_inc(ds_conv.sem, 16)
            return None
        k.g_(_)
        t_conv = ds_conv.n

        def _(v):
            v.memset(onescol[:], 1.0)
            v.memset(onesrow[:], 1.0)
            v.memset(onesr64[64:65, :], 1.0)
            v.memset(epsv[:], 1e-5)
            v.memset(zerov[:], 0.0)
            return v.memset(sc[:], 0.0)
        t_mem = k.v(_)

        def _(g):
            r = None
            for m in range(4):
                r = g.memset(masks[:, m, :], 1.0)
            return r
        t_msk0 = k.g(_)

        def _(g):
            r = None
            for m in range(4):
                r = g.affine_select(masks[:, m, :], masks[:, m, :],
                                    [[1, 512]], AluOpType.is_ge, 0.0,
                                    base=-128 * m, channel_multiplier=-1)
            return r
        t_masks = k.g(_, waits=[(k.gp, t_msk0)])

        NCOLS = 2 * 16384 + 2
        t_relu = 0

        def emit_ln_stats(xin, nt, wait_list, ws=0):
            def _(v):
                r = None
                for tt in range(nt):
                    o = ws + 16 * tt
                    v.bn_stats(lnws[:, o:o + 6], xin[:, tt, 0:512])
                    r = v.bn_stats(lnws[:, o + 6:o + 12], xin[:, tt, 512:1024])
                return r
            t1 = k.v(_, waits=wait_list)

            def _(v):
                r = None
                for tt in range(nt):
                    o = ws + 16 * tt
                    r = v.bn_aggr(lnws[:, o + 12:o + 14], lnws[:, o:o + 12])
                return r
            t2 = k.v(_, waits=[(k.dv, t1)])

            def _(v):
                r = None
                for tt in range(nt):
                    o = ws + 16 * tt
                    r = v.tensor_scalar(lnws[:, o + 14:o + 15],
                                        lnws[:, o + 12:o + 13], -1.0, None,
                                        MULT)
                return r
            t3 = k.v(_, waits=[(k.dv, t2)])

            def _(a):
                r = None
                for tt in range(nt):
                    o = ws + 16 * tt
                    r = a.activation(lnws[:, o + 15:o + 16],
                                     lnws[:, o + 13:o + 14], AF.Sqrt,
                                     bias=epsv[:], scale=1.0)
                return r
            t4 = k.a(_, waits=[(k.dv, t3)])

            def _(v):
                r = None
                for tt in range(nt):
                    o = ws + 16 * tt
                    r = v.reciprocal(lnws[:, o + 15:o + 16],
                                     lnws[:, o + 15:o + 16])
                return r
            return k.v(_, waits=[(k.ac, t4)])

        def emit_ln_fm(xin, nt, t_stats, dst_hi, dst_lo, dst_f32, ndc=8,
                       ws=0):
            last = 0
            for dc in range(ndc):
                sl_ = dc % 2

                def _sl(v, dc=dc, sl_=sl_):
                    r = None
                    for tt in range(nt):
                        o = ws + 16 * tt
                        r = v.tensor_scalar(
                            stg[:, sl_, 128 * tt:128 * (tt + 1)],
                            xin[:, tt, 128 * dc:128 * (dc + 1)],
                            lnws[:, o + 14:o + 15], lnws[:, o + 15:o + 16],
                            ADD, op1=MULT)
                    return r
                bi = 6 + (dc % 2)
                tsl = k.v(_sl, waits=[(k.dv, t_stats), (k.pe, stg_read[sl_])])

                def _t(te, bi=bi, sl_=sl_):
                    r = None
                    for tt in range(nt):
                        r = te.transpose(pb[bi][:, tt * 128:(tt + 1) * 128],
                                         stg[:, sl_, 128 * tt:128 * (tt + 1)],
                                         identsb[:])
                    return r
                tmm = k.t(_t, waits=[(k.dv, tsl), mm_free[bi]])
                stg_read[sl_] = tmm

                if dst_f32 is not None:
                    t = k.v(lambda v, dc=dc, bi=bi: v.tensor_copy(
                        dst_f32[:, dc, 0:nt * 128], pb[bi][:, 0:nt * 128]),
                        waits=[(k.pe, tmm)])
                else:
                    t0 = k.v(lambda v, dc=dc, bi=bi: v.tensor_copy(
                        dst_hi[:, dc, 0:nt * 128], pb[bi][:, 0:nt * 128]),
                        waits=[(k.pe, tmm)])
                    t = k.v(lambda v, dc=dc, bi=bi: v.tensor_tensor(
                        dst_lo[:, dc, 0:nt * 128], pb[bi][:, 0:nt * 128],
                        dst_hi[:, dc, 0:nt * 128], SUB),
                        waits=[(k.dv, t0)])
                mm_free[bi] = (k.dv, t)
                last = t
            return last

        def mmgroup(bi, emit, wait_list):
            return k.t(lambda te: emit(te, pb[bi]),
                       waits=list(wait_list) + [mm_free[bi]])

        def bank_evac(bi, t_pe, fn, eng="v", extra=None):
            f = {"v": k.v, "a": k.a}[eng]
            w = [(k.pe, t_pe)]
            if extra:
                w += list(extra)
            t = f(fn, waits=w)
            mm_free[bi] = ({"v": k.dv, "a": k.ac}[eng], t)
            return t

        ph_a = contextlib.ExitStack()
        sba = mksb(ph_a)
        lnTa = sba("lnTa", [128, 8, CH])
        lnTa2 = sba("lnTa2", [128, 8, CH])
        xch2 = sba("xch2", [128, 4, D])
        ohtile = sba("ohtile", [RELB, 2, 2048])
        sttile = sba("sttile", [HPC, 2, 2048])
        ckvT = sba("ckvT", [128, 2, CH])
        cqst = sba("cqst", [128, 2, CH])
        ksst = sba("ksst", [128, 4, 2, CH], BF16)
        vstage = sba("vstage", [128, 4, HPC, 65])
        wdq_sb = sba("wdq_sb", [128, 2, 8, 128])
        wdkv_sb = sba("wdkv_sb", [128, 2, 8, 128])
        wuk_sb = sba("wuk_sb", [128, 4, 2, 128])
        wuv_sb = sba("wuv_sb", [128, 4, 2, 128])


        BL = 2048
        strip_blocks = [(i, c0, min(BL, NCOLS - c0))
                        for i, c0 in enumerate(range(0, NCOLS, BL))]
        strip_state = {"oh_mm": {}, "st_wr": {}, "tev": 0, "idx": 0}

        def strip_round():
            if strip_state["idx"] >= len(strip_blocks):
                return
            i, c0, n = strip_blocks[strip_state["idx"]]
            strip_state["idx"] += 1
            oh_mm = strip_state["oh_mm"]
            st_wr = strip_state["st_wr"]

            ds_oh[i % 2].inc(16)
            def _(s, i=i, c0=c0, n=n):
                s.dma_start(out=ohtile[:, i % 2, 0:n],
                            in_=onehot[:, c0:c0 + n]
                            ).then_inc(ds_oh[i % 2].sem, 16)
                return None
            k.sy(_, waits=[(k.pe, oh_mm.get(i - 2, 0))])
            t_oh = ds_oh[i % 2].n
            for b0 in range(0, n, 512):
                nb = min(512, n - b0)
                tmm = k.t(lambda te, i=i, b0=b0, nb=nb: te.matmul(
                    pb[i % 2][0:HPC, 0:nb], relsb[:],
                    ohtile[:, i % 2, b0:b0 + nb], start=True, stop=True),
                    waits=[(ds_oh[i % 2], t_oh), mm_free[i % 2],
                           (ds_consts, ld_consts)])
                oh_mm[i] = tmm

                def _ev(v, i=i, b0=b0, nb=nb):
                    v.tensor_copy(sttile[:, i % 2, b0:b0 + nb],
                                  pb[i % 2][0:HPC, 0:nb])
                    return v.tensor_reduce(sc[0:8, 49:50],
                                           pb[i % 2][0:HPC, 0:nb], AXL, MAXOP)
                tev0 = k.v(_ev, waits=[(k.pe, tmm), (k.dv, t_mem),
                                       (ds_st[i % 2], st_wr.get(i % 2, 0))])
                tev = k.v(lambda v, i=i, b0=b0: v.tensor_tensor(
                    maxcsb[:, 0:1],
                    (maxcsb[:, 0:1] if (i or b0) else sc[0:8, 49:50]),
                    sc[0:8, 49:50], MAXOP), waits=[(k.dv, tev0)])
                mm_free[i % 2] = (k.dv, tev)
                strip_state["tev"] = tev

            ds_st[i % 2].inc(16)
            def _(s, i=i, c0=c0, n=n):
                s.dma_start(out=strh[:, c0:c0 + n],
                            in_=sttile[:, i % 2, 0:n]
                            ).then_inc(ds_st[i % 2].sem, 16)
                return None
            k.sy(_, waits=[(k.dv, strip_state["tev"])])
            st_wr[i % 2] = ds_st[i % 2].n

        def strip_finish():
            while strip_state["idx"] < len(strip_blocks):
                strip_round()
            ds_scg.inc(32)
            def _(g):
                g.dma_start(out=sc[0:1, 0:8],
                            in_=strh[:, NCOLS - 2:NCOLS - 1]
                            .rearrange("h o -> o h"),
                            allow_slow_non_contiguous=True
                            ).then_inc(ds_scg.sem, 16)
                g.dma_start(out=sc[0:1, 8:16], in_=maxcsb[:],
                            allow_slow_non_contiguous=True
                            ).then_inc(ds_scg.sem, 16)
                return None
            k.g_(_, waits=[(ds_st[0], ds_st[0].n), (ds_st[1], ds_st[1].n),
                           (k.dv, strip_state["tev"])])
            return k.v(lambda v: v.tensor_scalar(
                sc[0:1, 8:16], sc[0:1, 8:16], 0.0, None, MAXOP),
                waits=[(ds_scg, ds_scg.n)])

        ds_wa.inc(80)
        def _(s):
            for dst, src in ((wdq_sb, wdq), (wdkv_sb, wdkv), (wuk_sb, wuk),
                             (wuv_sb, wuv), (wuq_sb, wuq)):
                s.dma_start(out=dst[:], in_=src.rearrange("a b p n -> p a b n")
                            ).then_inc(ds_wa.sem, 16)
            return None
        k.adma(_)
        t_p1w = ds_wa.n

        v0_wr = [0] * NCH
        vst_dma = 0
        cq_dma = 0
        ks_dma = 0

        xfree_p = [0, 0]
        for j in range(NCH):
            strip_round()
            strip_round()
            par = j % 2
            xv = xch if par == 0 else xch2
            lv = lnTa if par == 0 else lnTa2
            ds_x.inc(16)
            def _(s, j=j, xv=xv):
                s.dma_start(out=xv[:],
                            in_=x_in[CH * j:CH * (j + 1), :]
                            .rearrange("(t p) d -> p t d", p=128)
                            ).then_inc(ds_x.sem, 16)
                return None
            k.sy(_, waits=[(k.dv, xfree_p[par])])
            t_x = ds_x.n

            t_st = emit_ln_stats(xv, 4, [(ds_x, t_x)], ws=64 * par)
            t_fm = emit_ln_fm(xv, 4, t_st, None, None, lv, ws=64 * par)
            xfree_p[par] = t_fm

            for do in range(2):
                tmm = mmgroup(do, lambda te, bank, do=do, lv=lv: [
                    te.matmul(bank[:, 0:512], wdq_sb[:, do, di, :],
                              lv[:, di, :], start=(di == 0), stop=(di == 7))
                    for di in range(8)][-1],
                    [(k.dv, t_fm), (ds_wa, t_p1w)])
                bank_evac(do, tmm, lambda v, do=do: v.tensor_copy(
                    cqst[:, do, :], pb[do][:, 0:512]),
                    extra=[(ds_cq, cq_dma)])
            t_cq = k.dv.n

            ds_cq.inc(16)
            def _(s, j=j):
                s.dma_start(out=cqh[:, :, CH * j:CH * (j + 1)]
                            .rearrange("a p n -> p a n"),
                            in_=cqst[:]).then_inc(ds_cq.sem, 16)
                return None
            k.sy(_, waits=[(k.dv, t_cq)])
            cq_dma = ds_cq.n

            for do in range(2):
                tmm = mmgroup(2 + do, lambda te, bank, do=do, lv=lv: [
                    te.matmul(bank[:, 0:512], wdkv_sb[:, do, di, :],
                              lv[:, di, :], start=(di == 0), stop=(di == 7))
                    for di in range(8)][-1],
                    [(k.dv, t_fm), (ds_wa, t_p1w)])
                bank_evac(2 + do, tmm, lambda v, do=do: v.tensor_copy(
                    ckvT[:, do, :], pb[2 + do][:, 0:512]))
            t_ckv = k.dv.n

            for do in range(4):
                bi = 4 + (do % 2)
                tmm = mmgroup(bi, lambda te, bank, do=do: [
                    te.matmul(bank[:, 0:512], wuk_sb[:, do, di, :],
                              ckvT[:, di, :], start=(di == 0), stop=(di == 1))
                    for di in range(2)][-1], [(k.dv, t_ckv)])
                t_hi = k.v(lambda v, do=do, bi=bi: v.tensor_copy(
                    ksst[:, do, 0, :], pb[bi][:, 0:512]),
                    waits=[(k.pe, tmm), (ds_ks, ks_dma)])
                t_ev = k.v(lambda v, do=do, bi=bi: v.tensor_tensor(
                    ksst[:, do, 1, :], pb[bi][:, 0:512],
                    ksst[:, do, 0, :], SUB), waits=[(k.dv, t_hi)])
                mm_free[bi] = (k.dv, t_ev)
                t_sq = k.v(lambda v, do=do: v.tensor_tensor(
                    vts[:], ksst[:, do, 0, :], ksst[:, do, 0, :], MULT),
                    waits=[(k.dv, t_hi), (k.pe, vts_rd[0])])
                tnm = mmgroup(6, lambda te, bank: te.matmul(
                    bank[0:1, 0:512], onescol[:], vts[:], start=True,
                    stop=True), [(k.dv, t_sq)])
                vts_rd[0] = tnm

                trd = bank_evac(6, tnm, lambda v: v.tensor_reduce(
                    sc[0:1, 48:49], pb[6][0:1, 0:512], AXL, MAXOP),
                    extra=[(k.dv, t_relu)])
                k.v(lambda v, do=do: v.tensor_tensor(
                    sc[0:1, 16 + do:17 + do], sc[0:1, 16 + do:17 + do],
                    sc[0:1, 48:49], MAXOP), waits=[(k.dv, trd)])
            t_ks = k.dv.n

            ds_ks.inc(16)
            def _(s, j=j):
                s.dma_start(out=kfh[4 * j:4 * (j + 1)]
                            .rearrange("t p a b n -> p a b t n"),
                            in_=ksst[:].rearrange("p a b (t n) -> p a b t n",
                                                  n=128)
                            ).then_inc(ds_ks.sem, 16)
                return None
            k.sy(_, waits=[(k.dv, t_ks)])
            ks_dma = ds_ks.n

            for do in range(4):
                bi = 4 + (do % 2)
                tmm = mmgroup(bi, lambda te, bank, do=do: [
                    te.matmul(bank[:, 0:512], wuv_sb[:, do, di, :],
                              ckvT[:, di, :], start=(di == 0), stop=(di == 1))
                    for di in range(2)][-1], [(k.dv, t_ckv)])
                t_ev = bank_evac(bi, tmm, lambda v, bi=bi: v.tensor_copy(
                    vts[:], pb[bi][:, 0:512]))
                for tt in range(4):
                    ttr = k.t(lambda te, tt=tt: te.transpose(
                        pb[7][:, 0:128], vts[:, tt * 128:(tt + 1) * 128],
                        identsb[:]), waits=[(k.dv, t_ev), mm_free[7]])

                    def _vv(v, tt=tt, do=do):
                        v.tensor_copy(vstage[:, tt, 2 * do, 0:64],
                                      pb[7][:, 0:64])
                        return v.tensor_copy(vstage[:, tt, 2 * do + 1, 0:64],
                                             pb[7][:, 64:128])
                    bank_evac(7, ttr, _vv, extra=[(ds_vst, vst_dma)])
            t_vst = k.v(lambda v: v.memset(vstage[:, :, :, 64:65], 1.0),
                        waits=[mm_free[7]])

            ds_vst.inc(16)
            def _(s, j=j):
                s.dma_start(out=v0h[4 * j:4 * (j + 1)]
                            .rearrange("t p h c -> p t h c"),
                            in_=vstage[:]).then_inc(ds_vst.sem, 16)
                return None
            k.sy(_, waits=[(k.dv, t_vst)])
            v0_wr[j] = ds_vst.n
            vst_dma = ds_vst.n
            k.flush()

        t_relu = strip_finish()
        k.barrier()
        ph_a.close()

        def attention(tag, get_q, kf_mode, vt_mode, wout_ds, rsi, rso,
                      mc_base, kn_col, use_bias, extra_w, qf_hi, qf_lo,
                      esb, attnT, ystg, wosb, vring, kring, b01_w,
                      rs_ticks, prep=None):
            attn_guard = (k.pe, 0)
            yp_dma = None
            rs_wr = [[0, 0] for _ in range(4)]
            vr_av = {}
            kr_sc = {}
            for ci, c in enumerate(CHORD):
                t_qs, t_qn = get_q(c)
                t_q = t_qs

                def _m(v):
                    r = None
                    for p in range(4):
                        r = v.tensor_tensor(
                            sc[0:1, 44 + p:45 + p], sc[0:1, 40 + p:41 + p],
                            sc[0:1, kn_col(p):kn_col(p) + 1], MULT)
                    return r
                tm1 = k.v(_m, waits=[(k.dv, t_qn)])

                def _s(a):
                    r = None
                    for p in range(4):
                        r = a.activation(sc[0:1, 44 + p:45 + p],
                                         sc[0:1, 44 + p:45 + p], AF.Sqrt,
                                         bias=zerov[:], scale=1.0201)
                    return r
                tm2 = k.a(_s, waits=[(k.dv, tm1)])

                def _b(v):
                    r = None
                    for p in range(4):
                        r = v.tensor_scalar(
                            sc[0:1, 24 + 2 * p:26 + 2 * p],
                            sc[0:1, mc_base + 2 * p:mc_base + 2 * p + 2],
                            sc[0:1, 44 + p:45 + p], -1.0, ADD, op1=MULT)
                    return r
                tm3 = k.v(_b, waits=[(k.ac, tm2)])
                tbc = mmgroup(6, lambda te, bank: te.matmul(
                    bank[0:128, 0:8], onesrow[:], sc[0:1, 24:32],
                    start=True, stop=True), [(k.dv, tm3)])
                t_bv = bank_evac(6, tbc, lambda v: v.tensor_copy(
                    bvec[:], pb[6][0:128, 0:8]))

                nkt = 4 * c + 4
                epi_pend = [None]
                t_at = 0
                for p in range(4):
                    if epi_pend[0] is not None and nkt == 4:
                        t_at = epi_pend[0]()
                        epi_pend[0] = None
                    av_done = {}
                    av_args = {}
                    for t in range(nkt + 1):
                      if t < nkt:
                        sb0 = 2 * (t % 2)
                        if kf_mode is None:
                            sl = t % KRING

                            ds_kr[sl].inc(16)
                            def _kld(s, t=t, sl=sl, p=p):
                                s.dma_start(out=kring[:, sl, :, :],
                                            in_=kfh[t][:, p]
                                            ).then_inc(ds_kr[sl].sem, 16)
                                return None
                            k.sy(_kld, waits=[kr_sc.get(sl, (k.pe, 0))])
                            kw = [(ds_kr[sl], ds_kr[sl].n)]

                            def kf_sl(kv, po, sl=sl):
                                return kring[po:po + 64, sl,
                                             0 if kv == "hi" else 1, :]
                        else:
                            kf2h, kf2l = kf_mode
                            kw = []

                            def kf_sl(kv, po, t_=t):
                                src = kf2h if kv == "hi" else kf2l
                                return src[po:po + 64,
                                           128 * t_:128 * (t_ + 1)]

                        if vt_mode is None:
                            slv = t % VRING

                            ds_vr[slv].inc(16)
                            def _vld(s, t=t, slv=slv, p=p):
                                s.dma_start(out=vring[:, slv, :, :],
                                            in_=v0h[t][:, 2 * p:2 * p + 2, :]
                                            ).then_inc(ds_vr[slv].sem, 16)
                                return None
                            k.sy(_vld, waits=[vr_av.get(slv, (k.pe, 0))])
                            vt_get = (lambda hh, slv=slv:
                                      vring[:, slv, hh, :])
                            vw = [(ds_vr[slv], ds_vr[slv].n)]
                        else:
                            v1sb_, t_v1 = vt_mode
                            vt_get = (lambda hh, t=t: v1sb_[:, t, :])
                            vw = [(k.dv, t_v1)]

                        def _sc(te, p=p, t=t, sb0=sb0, kf_sl=kf_sl):
                            r = None
                            for hh in range(2):
                                po = 64 * hh
                                bank = pb[sb0 + hh]
                                for i, (qop, kv) in enumerate(
                                        ((qf_hi, "hi"), (qf_lo, "hi"),
                                         (qf_hi, "lo"))):
                                    r = te.matmul(
                                        bank[:, 0:512], kf_sl(kv, po),
                                        qop[po:po + 64, p, :],
                                        start=(i == 0), stop=(i == 2),
                                        tile_position=(po, 0))
                            return r
                        t_smm = k.t(_sc, [(k.dv, t_q), mm_free[sb0],
                                          mm_free[sb0 + 1]] + kw +
                                    list(extra_w))
                        if kf_mode is None:
                            kr_sc[sl] = (k.pe, t_smm)

                        t_badd = 0
                        if use_bias:
                            addv = []
                            for qs in range(4):
                                qt = 4 * c + qs
                                if t == qt:
                                    addv.append((qs, b0t))
                                elif t == qt - 1:
                                    addv.append((qs, b1t))
                            if addv:
                                def _ba(v, p=p, addv=addv, sb0=sb0):
                                    r = None
                                    for hh in range(2):
                                        h = 2 * p + hh
                                        for qs, bt in addv:
                                            sl_ = slice(128 * qs,
                                                        128 * (qs + 1))
                                            r = v.tensor_tensor(
                                                pb[sb0 + hh][:, sl_],
                                                pb[sb0 + hh][:, sl_],
                                                bt[:, h, :], ADD)
                                    return r
                                t_badd = k.v(_ba, waits=[(k.pe, t_smm),
                                                         b01_w])

                        def _ex(a, p=p, t=t, sb0=sb0):
                            r = None
                            for hh in range(2):
                                h = 2 * p + hh
                                r = a.activation(
                                    esb[:, hh, t % 2, :],
                                    pb[sb0 + hh][:, 0:512], AF.Exp,
                                    bias=bvec[:, h:h + 1], scale=1.0)
                            return r
                        wex = [(k.pe, t_smm), (k.dv, t_bv)]
                        if t_badd:
                            wex.append((k.dv, t_badd))
                        if t >= 2:
                            wex.append(av_done[t - 2])
                        t_exp = k.a(_ex, waits=wex)
                        mm_free[sb0] = (k.ac, t_exp)
                        mm_free[sb0 + 1] = (k.ac, t_exp)
                        if t == 0 and epi_pend[0] is not None:
                            t_at = epi_pend[0]()
                            epi_pend[0] = None

                        t_mask = 0
                        if t >= 4 * c:
                            mi = t - 4 * c

                            def _mk(v, mi=mi, t=t):
                                r = None
                                for hh in range(2):
                                    r = v.tensor_tensor(
                                        esb[:, hh, t % 2, :],
                                        esb[:, hh, t % 2, :],
                                        masks[:, mi, :], MULT)
                                return r
                            t_mask = k.v(_mk, waits=[(k.ac, t_exp),
                                                     (k.gp, t_masks)])

                        av_args[t] = (vt_get,
                                      list(vw) + [(k.dv, t_mask) if t_mask
                                                  else (k.ac, t_exp)],
                                      slv if vt_mode is None else None)
                      ta = t - 1
                      if ta >= 0:
                        vt_get_a, w_av, slv_a = av_args.pop(ta)

                        def _av(te, ta=ta, nkt=nkt, vt_get_a=vt_get_a):
                            r = None
                            for hh in range(2):
                                r = te.matmul(
                                    pb[4 + hh][0:65, 0:512],
                                    vt_get_a(hh),
                                    esb[:, hh, ta % 2, :],
                                    start=(ta == 0), stop=(ta == nkt - 1))
                            return r
                        if ta == 0:
                            w_av = w_av + [mm_free[4], mm_free[5]]
                        tick_av = k.t(_av, w_av)
                        av_done[ta] = (k.pe, tick_av)
                        if slv_a is not None:
                            vr_av[slv_a] = (k.pe, tick_av)

                    def epilogue(p=p, av_last=av_done[nkt - 1],
                                 ag=attn_guard):
                        t_rc = k.v(lambda v: [
                            v.reciprocal(rcrow[64:65, hh, :],
                                         pb[4 + hh][64:65, 0:512])
                            for hh in range(2)][-1],
                            waits=[av_last])
                        for hh in range(2):
                            t_bcm = mmgroup(6, lambda te, bank, hh=hh:
                                te.matmul(
                                    bank[0:64, 0:512], onesr64[64:65, :],
                                    rcrow[64:65, hh, :], start=True,
                                    stop=True, tile_position=(64, 0)),
                                [(k.dv, t_rc), (k.dv, t_mem)])
                            t_rb = bank_evac(6, t_bcm, lambda v:
                                v.tensor_copy(rbs[0:64, :],
                                              pb[6][0:64, 0:512]))
                            if hh == 0:
                                t_mu = k.v(lambda v, p=p: v.tensor_tensor(
                                    attnT[0:64, p, :], pb[4][0:64, 0:512],
                                    rbs[0:64, :], MULT),
                                    waits=[(k.dv, t_rb), ag])
                                mm_free[4] = (k.dv, t_mu)
                            else:
                                t_mu = k.v(lambda v: v.tensor_tensor(
                                    stg64[0:64, :], pb[5][0:64, 0:512],
                                    rbs[0:64, :], MULT),
                                    waits=[(k.dv, t_rb), (ds_o2, ds_o2.n)])
                                mm_free[5] = (k.dv, t_mu)

                                ds_o2.inc(16)
                                def _od(s, p=p):
                                    s.dma_start(out=attnT[64:128, p, :],
                                                in_=stg64[0:64, :]
                                                ).then_inc(ds_o2.sem, 16)
                                    return None
                                k.sy(_od, waits=[(k.dv, t_mu)])
                        return k.dv.n
                    epi_pend[0] = epilogue
                if epi_pend[0] is not None:
                    t_at = epi_pend[0]()
                    epi_pend[0] = None
                if prep is not None and ci + 1 < len(CHORD):
                    prep(CHORD[ci + 1])
                k.flush()

                for dh in range(4):
                    for do2 in range(2):
                        do = 2 * dh + do2
                        bi = do2
                        tmm = mmgroup(bi, lambda te, bank, do=do: [
                            te.matmul(bank[:, 0:512], wosb[:, do, di, :],
                                      attnT[:, di, :], start=(di == 0),
                                      stop=(di == 3)) for di in range(4)][-1],
                            [(k.dv, t_at), wout_ds, (ds_o2, ds_o2.n)])
                        bank_evac(bi, tmm, lambda v, do2=do2, bi=bi:
                                  v.tensor_copy(ystg[:, do2, :],
                                                pb[bi][:, 0:512]),
                                  extra=[yp_dma])
                    t_y2 = k.dv.n

                    ds_rs[(c % 4) % 2].inc(16)
                    def _(s, c=c, dh=dh):
                        s.dma_start(
                            out=rsi[c % 4][c // 4, 256 * dh:256 * (dh + 1), :]
                            .rearrange("(dc p) n -> p dc n", p=128),
                            in_=ystg[:]).then_inc(ds_rs[(c % 4) % 2].sem, 16)
                        return None
                    k.sy(_, waits=[(k.dv, t_y2)])
                    yp_dma = (ds_rs[(c % 4) % 2], ds_rs[(c % 4) % 2].n)
                attn_guard = (k.pe, k.pe.n)

                if ci % 2 == 1:
                    j = c % 4

                    def _(g, j=j):
                        r = g.collective_compute(
                            "ReduceScatter", ADD, ins=[rsi[j][:]],
                            outs=[rso[j][:]], replica_groups=RG)
                        r.then_inc(k.col.sem, 1)
                        return None
                    k.col.inc()
                    k.g_(_, waits=[(ds_rs[j % 2], ds_rs[j % 2].n)])
                    rs_ticks[(tag, j)] = k.col.n
                k.flush()

        ph_b = contextlib.ExitStack()
        sbb = mksb(ph_b)
        qf_hi_b = sbb("qf_hi", [128, 4, CH], BF16)
        qf_lo_b = sbb("qf_lo", [128, 4, CH], BF16)
        qscr = sbb("qscr", [128, 512])
        esb_b = sbb("esb", [128, 2, 2, 512])
        vring_b = sbb("vring", [128, VRING, 2, 65])
        kring_b = sbb("kring", [128, KRING, 2, 128], BF16)
        attnT_b = sbb("attnT", [128, 4, 512])
        ystg_b = sbb("ystg", [128, 2, 512])
        wosb_b = sbb("wosb", [128, 8, 4, 128])
        cqbuf = sbb("cqbuf", [128, 2, 2, 512])
        rs_ticks = {}

        ds_wo.inc(16)
        def _(s):
            s.dma_start(out=wosb_b[:], in_=wo.rearrange("a b p n -> p a b n")
                        ).then_inc(ds_wo.sem, 16)
            return None
        k.adma(_)
        t_wo = ds_wo.n

        cq_rd = {0: 0, 1: 0}
        qs_tick = [0]
        l0_prep = {}
        l0_pc = [0]

        def l0_prepare(c):
            par = l0_pc[0] % 2
            l0_pc[0] += 1
            ds_cqb.inc(16)
            def _(s, c=c, par=par):
                s.dma_start(out=cqbuf[:, par],
                            in_=cqh[:, :, CH * c:CH * (c + 1)]
                            .rearrange("a p n -> p a n")
                            ).then_inc(ds_cqb.sem, 16)
                return None
            k.sy(_, waits=[(k.pe, cq_rd[par])])
            l0_prep[c] = (ds_cqb.n, par)

        def l0_get_q(c):
            qs_tick[0] = 0
            if c not in l0_prep:
                l0_prepare(c)
            t_cql, par = l0_prep.pop(c)
            for p in range(4):
                bi = p % 2
                tmm = mmgroup(bi, lambda te, bank, p=p, par=par: [
                    te.matmul(bank[:, 0:512], wuq_sb[:, p, di, :],
                              cqbuf[:, par, di, :], start=(di == 0),
                              stop=(di == 1)) for di in range(2)][-1],
                    [(ds_cqb, t_cql), (ds_wa, t_p1w)])
                t_s = k.v(lambda v, bi=bi: v.tensor_scalar(
                    qscr[:], pb[bi][:, 0:512], 0.125, None, MULT),
                    waits=[(k.pe, tmm)])
                t_h = k.v(lambda v, p=p: v.tensor_copy(
                    qf_hi_b[:, p, :], qscr[:]), waits=[(k.dv, t_s)])
                t_ev = k.v(lambda v, p=p: v.tensor_tensor(
                    qf_lo_b[:, p, :], qscr[:], qf_hi_b[:, p, :], SUB),
                    waits=[(k.dv, t_h)])
                mm_free[bi] = (k.dv, t_ev)
                t_sq = k.v(lambda v, p=p: v.tensor_tensor(
                    vts[:], qf_hi_b[:, p, :], qf_hi_b[:, p, :], MULT),
                    waits=[(k.dv, t_h), (k.pe, vts_rd[0])])
                tnm = mmgroup(6, lambda te, bank: te.matmul(
                    bank[0:1, 0:512], onescol[:], vts[:], start=True,
                    stop=True), [(k.dv, t_sq)])
                vts_rd[0] = tnm
                bank_evac(6, tnm, lambda v, p=p: v.tensor_reduce(
                    sc[0:1, 40 + p:41 + p], pb[6][0:1, 0:512], AXL, MAXOP))
                qs_tick[0] = max(qs_tick[0], t_ev)
            cq_rd[par] = k.pe.n
            return qs_tick[0], k.dv.n

        attention("l0", l0_get_q, None, None, (ds_wo, t_wo), rs0i, rs0o,
                  32, lambda p: 16 + p, False, [], qf_hi_b, qf_lo_b,
                  esb_b, attnT_b, ystg_b, wosb_b, vring_b, kring_b,
                  None, rs_ticks, prep=l0_prepare)

        k.barrier()
        ph_b.close()

        ph_c = contextlib.ExitStack()
        sbc = mksb(ph_c)
        lnThi = sbc("lnThi", [128, 8, CH], BF16)
        lnTlo = sbc("lnTlo", [128, 8, CH], BF16)
        uhi = sbc("uhi", [128, 32, CH], BF16)
        ulo = sbc("ulo", [128, 32, CH], BF16)
        ghi = sbc("ghi", [128, 32, CH], BF16)
        glo = sbc("glo", [128, 32, CH], BF16)
        fsc = sbc("fsc", [128, 512])
        qscr2 = sbc("qscr2", [128, 512])
        wring = sbc("wring", [128, 2, 32, 128], BF16)

        wr_free = [(k.pe, 0), (k.pe, 0)]
        fsc_rd = [0]
        ag_ticks = {}

        def stream_wcol(hbm_tiled, do, ndi, slot):
            ds_wr[slot].inc(16)
            def _(s, do=do, ndi=ndi, slot=slot):
                s.dma_start(out=wring[:, slot, 0:ndi, :],
                            in_=hbm_tiled[do].rearrange("b p n -> p b n")
                            ).then_inc(ds_wr[slot].sem, 16)
                return None
            k.adma(_, waits=[wr_free[slot], (ds_conv, t_conv)])
            return (ds_wr[slot], ds_wr[slot].n)

        for kk in range(4):
            ds_x.inc(16)
            def _(s, kk=kk):
                s.dma_start(out=xch[:],
                            in_=x_my[CH * kk:CH * (kk + 1), :]
                            .rearrange("(t p) d -> p t d", p=128)
                            ).then_inc(ds_x.sem, 16)
                return None
            k.sy(_, waits=[(k.col, rs_ticks[("l0", kk)]),
                           (k.dv, cell["x_free"]), cell["x_dma"]])
            t_xld = ds_x.n

            for dc in range(8):
                bi = 6 + (dc % 2)

                sl_ = dc % 2
                ds_stg[sl_].inc(16)
                def _(s, kk=kk, dc=dc, sl_=sl_):
                    s.dma_start(out=stg[:, sl_, :],
                                in_=rs0o[kk][128 * dc:128 * (dc + 1), :]
                                ).then_inc(ds_stg[sl_].sem, 16)
                    return None
                k.sy(_, waits=[(k.pe, stg_read[sl_])])
                t_sld = ds_stg[sl_].n

                def _t(te, bi=bi, sl_=sl_):
                    r = None
                    for tt in range(4):
                        r = te.transpose(pb[bi][:, tt * 128:(tt + 1) * 128],
                                         stg[:, sl_, tt * 128:(tt + 1) * 128],
                                         identsb[:])
                    return r
                tmm = k.t(_t, waits=[(ds_stg[sl_], t_sld), mm_free[bi]])
                stg_read[sl_] = tmm

                def _v(v, dc=dc, bi=bi):
                    r = None
                    for tt in range(4):
                        r = v.tensor_tensor(
                            xch[:, tt, 128 * dc:128 * (dc + 1)],
                            pb[bi][:, tt * 128:(tt + 1) * 128],
                            xch[:, tt, 128 * dc:128 * (dc + 1)], ADD)
                    return r
                bank_evac(bi, tmm, _v, extra=[(ds_x, t_xld)])
            t_x1 = k.dv.n

            t_stt = emit_ln_stats(xch, 4, [(k.dv, t_x1)])
            t_sp = emit_ln_fm(xch, 4, t_stt, lnThi, lnTlo, None)

            for do in range(32):
                sl = do % 2
                s1 = stream_wcol(w1b, do, 8, sl)
                tm1 = mmgroup(0, lambda te, bank, sl=sl: [
                    te.matmul(bank[:, 0:512], wring[:, sl, i % 8, :],
                              (lnThi if i < 8 else lnTlo)[:, i % 8, :],
                              start=(i == 0), stop=(i == 15))
                    for i in range(16)][-1], [(k.dv, t_sp), s1])
                wr_free[sl] = (k.pe, tm1)
                s2 = stream_wcol(w2b, do, 8, 1 - sl)
                tm2 = mmgroup(1, lambda te, bank, sl=sl: [
                    te.matmul(bank[:, 0:512], wring[:, 1 - sl, i % 8, :],
                              (lnThi if i < 8 else lnTlo)[:, i % 8, :],
                              start=(i == 0), stop=(i == 15))
                    for i in range(16)][-1], [(k.dv, t_sp), s2])
                wr_free[1 - sl] = (k.pe, tm2)
                t_si = k.a(lambda a, do=do: a.activation(
                    fsc[:], pb[0][:, 0:512], AF.Silu,
                    scale=sclsb[:, do:do + 1]),
                    waits=[(k.pe, tm1), (k.dv, fsc_rd[0])])
                mm_free[0] = (k.ac, t_si)

                t_u0 = k.v(lambda v, do=do: v.scalar_tensor_tensor(
                    qscr2[:], pb[1][:, 0:512], sclsb[:, 32 + do:33 + do],
                    fsc[:], MULT, MULT), waits=[(k.ac, t_si), (k.pe, tm2)])
                t_u1 = k.v(lambda v, do=do: v.tensor_copy(
                    uhi[:, do, :], qscr2[:]), waits=[(k.dv, t_u0)])
                t_u = k.v(lambda v, do=do: v.tensor_tensor(
                    ulo[:, do, :], qscr2[:], uhi[:, do, :], SUB),
                    waits=[(k.dv, t_u1)])
                mm_free[1] = (k.dv, t_u0)
                fsc_rd[0] = t_u0
                if do % 8 == 7:
                    k.flush()
            t_uall = k.dv.n

            for do in range(32):
                sl = do % 2
                s1 = stream_wcol(h1b, do, 32, sl)
                bi = 4 + (do % 2)
                tm = mmgroup(bi, lambda te, bank, sl=sl: [
                    te.matmul(bank[:, 0:512], wring[:, sl, i % 32, :],
                              (uhi if i < 32 else ulo)[:, i % 32, :],
                              start=(i == 0), stop=(i == 63))
                    for i in range(64)][-1], [(k.dv, t_uall), s1])
                wr_free[sl] = (k.pe, tm)
                t_ge = k.a(lambda a, do=do, bi=bi: a.activation(
                    fsc[:], pb[bi][:, 0:512], AF.Gelu,
                    scale=sclsb[:, 64 + do:65 + do]),
                    waits=[(k.pe, tm), (k.dv, fsc_rd[0])])
                mm_free[bi] = (k.ac, t_ge)

                t_g0 = k.v(lambda v, do=do: v.tensor_copy(
                    ghi[:, do, :], fsc[:]), waits=[(k.ac, t_ge)])
                fsc_rd[0] = k.v(lambda v, do=do: v.tensor_tensor(
                    glo[:, do, :], fsc[:], ghi[:, do, :], SUB),
                    waits=[(k.dv, t_g0)])
                if do % 8 == 7:
                    k.flush()
            t_gall = k.dv.n

            for do in range(32):
                sl = do % 2
                s1 = stream_wcol(h2b, do, 32, sl)
                bi = 4 + (do % 2)
                tm = mmgroup(bi, lambda te, bank, sl=sl: [
                    te.matmul(bank[:, 0:512], wring[:, sl, i % 32, :],
                              (ghi if i < 32 else glo)[:, i % 32, :],
                              start=(i == 0), stop=(i == 63))
                    for i in range(64)][-1], [(k.dv, t_gall), s1])
                wr_free[sl] = (k.pe, tm)
                t_ge = k.a(lambda a, do=do, bi=bi: a.activation(
                    fsc[:], pb[bi][:, 0:512], AF.Gelu,
                    scale=sclsb[:, 96 + do:97 + do]),
                    waits=[(k.pe, tm), (k.dv, fsc_rd[0])])
                mm_free[bi] = (k.ac, t_ge)

                t_g0 = k.v(lambda v, do=do: v.tensor_copy(
                    uhi[:, do, :], fsc[:]), waits=[(k.ac, t_ge)])
                fsc_rd[0] = k.v(lambda v, do=do: v.tensor_tensor(
                    ulo[:, do, :], fsc[:], uhi[:, do, :], SUB),
                    waits=[(k.dv, t_g0)])
                if do % 8 == 7:
                    k.flush()
            t_g2 = k.dv.n

            for do in range(8):
                sl = do % 2
                s1 = stream_wcol(w3b, do, 32, sl)
                bi = do % 2
                tm = mmgroup(bi, lambda te, bank, sl=sl: [
                    te.matmul(bank[:, 0:512], wring[:, sl, i % 32, :],
                              (uhi if i < 32 else ulo)[:, i % 32, :],
                              start=(i == 0), stop=(i == 63))
                    for i in range(64)][-1], [(k.dv, t_g2), s1])
                wr_free[sl] = (k.pe, tm)
                sl_ = do % 2
                t_cp = bank_evac(bi, tm, lambda a, do=do, bi=bi, sl_=sl_:
                    a.activation(
                        stg[:, sl_, :], pb[bi][:, 0:512], AF.Copy, bias=0.0,
                        scale=sclsb[:, 128 + do:129 + do]), eng="a",
                    extra=[(k.pe, stg_read[sl_])])

                bj = 6 + (do % 2)

                def _t(te, bj=bj, sl_=sl_):
                    r = None
                    for tt in range(4):
                        r = te.transpose(pb[bj][:, tt * 128:(tt + 1) * 128],
                                         stg[:, sl_, tt * 128:(tt + 1) * 128],
                                         identsb[:])
                    return r
                tmm = k.t(_t, waits=[(k.ac, t_cp), mm_free[bj]])
                stg_read[sl_] = tmm

                def _v(v, do=do, bj=bj):
                    r = None
                    for tt in range(4):
                        r = v.tensor_tensor(
                            xch[:, tt, 128 * do:128 * (do + 1)],
                            pb[bj][:, tt * 128:(tt + 1) * 128],
                            xch[:, tt, 128 * do:128 * (do + 1)], ADD)
                    return r
                bank_evac(bj, tmm, _v)
            t_x2 = k.dv.n

            ds_ag[kk % 2].inc(32)
            def _(s, kk=kk):
                s.dma_start(out=agi[kk].rearrange("(t p) d -> p t d", p=128),
                            in_=xch[:]).then_inc(ds_ag[kk % 2].sem, 16)
                s.dma_start(out=x2h[CH * kk:CH * (kk + 1), :]
                            .rearrange("(t p) d -> p t d", p=128),
                            in_=xch[:]).then_inc(ds_ag[kk % 2].sem, 16)
                return None
            k.sy(_, waits=[(k.dv, t_x2)])
            cell["x_free"] = t_x2
            cell["x_dma"] = (ds_ag[kk % 2], ds_ag[kk % 2].n)

            def _(g, kk=kk):
                r = g.collective_compute("AllGather", AluOpType.bypass,
                                         ins=[agi[kk][:]], outs=[ago[kk][:]],
                                         replica_groups=RG)
                r.then_inc(k.col.sem, 1)
                return None
            k.col.inc()
            k.g_(_, waits=[(ds_ag[kk % 2], ds_ag[kk % 2].n)])
            ag_ticks[kk] = k.col.n
            k.flush()

        k.barrier()
        ph_c.close()

        ph_d = contextlib.ExitStack()
        sbd = mksb(ph_d)
        b0t = sbd("b0t", [128, HPC, 128])
        b1t = sbd("b1t", [128, HPC, 128])
        lnTd = sbd("lnTd", [128, 8, CH])
        lnTd2 = sbd("lnTd2", [128, 8, CH])
        xchd2 = sbd("xchd2", [128, 4, D])
        kf2_hi = sbd("kf2_hi", [128, S], BF16)
        kf2_lo = sbd("kf2_lo", [128, S], BF16)
        v1sb = sbd("v1sb", [128, NKT, 65])
        wq1_sb = sbd("wq1_sb", [128, 4, 8, 128])
        wk1_sb = sbd("wk1_sb", [128, 8, 64])
        wv1_sb = sbd("wv1_sb", [128, 8, 64])
        qf_hi_d = sbd("qf_hi_d", [128, 4, CH], BF16)
        qf_lo_d = sbd("qf_lo_d", [128, 4, CH], BF16)
        qscr_d = sbd("qscr_d", [128, 512])
        esb_d = sbd("esb_d", [128, 2, 2, 512])
        attnT_d = sbd("attnT_d", [128, 4, 512])
        ystg_d = sbd("ystg_d", [128, 2, 512])
        wosb_d = sbd("wosb_d", [128, 8, 4, 128])

        ds_b01.inc(32 * HPC)
        def _(s):
            for h in range(HPC):
                s.dma_start(out=b0t[:, h, :],
                            in_=strh[h:h + 1, 0:16384]
                            .rearrange("o (p n) -> (o p) n", p=128)
                            ).then_inc(ds_b01.sem, 16)
                s.dma_start(out=b1t[:, h, :],
                            in_=strh[h:h + 1, 16384:32768]
                            .rearrange("o (p n) -> (o p) n", p=128)
                            ).then_inc(ds_b01.sem, 16)
            return None
        k.sy(_)
        t_b01 = ds_b01.n

        ds_w4.inc(64)
        def _(s):
            s.dma_start(out=wq1_sb[:], in_=wq1.rearrange("a b p n -> p a b n")
                        ).then_inc(ds_w4.sem, 16)
            s.dma_start(out=wk1_sb[:], in_=wk1.rearrange("b p n -> p b n")
                        ).then_inc(ds_w4.sem, 16)
            s.dma_start(out=wv1_sb[:], in_=wv1.rearrange("b p n -> p b n")
                        ).then_inc(ds_w4.sem, 16)
            s.dma_start(out=wosb_d[:], in_=wo1.rearrange("a b p n -> p a b n")
                        ).then_inc(ds_w4.sem, 16)
            return None
        k.adma(_)
        t_p4w = ds_w4.n

        def load_ago_chunk(j):
            g = j // 4
            kkk = j % 4

            ds_x.inc(16)
            def _(s, g=g, kkk=kkk):
                s.dma_start(out=xch[:], in_=ago[kkk][g]
                            .rearrange("(t p) d -> p t d", p=128)
                            ).then_inc(ds_x.sem, 16)
                return None
            k.sy(_, waits=[(k.col, ag_ticks[kkk]), (k.dv, cell["x_free"]),
                           cell["x_dma"]])
            return ds_x.n

        for j in range(NCH):
            t_ld = load_ago_chunk(j)
            t_stt = emit_ln_stats(xch, 4, [(ds_x, t_ld)])
            t_fm = emit_ln_fm(xch, 4, t_stt, None, None, lnTd)
            cell["x_free"] = t_fm
            cell["x_dma"] = None

            tmk = mmgroup(0, lambda te, bank: [
                te.matmul(bank[0:64, 0:512], wk1_sb[:, di, :], lnTd[:, di, :],
                          start=(di == 0), stop=(di == 7))
                for di in range(8)][-1], [(k.dv, t_fm), (ds_w4, t_p4w)])
            t_kh = k.v(lambda v, j=j: v.tensor_copy(
                kf2_hi[0:64, CH * j:CH * (j + 1)], pb[0][0:64, 0:512]),
                waits=[(k.pe, tmk)])
            t_ek = k.v(lambda v, j=j: v.tensor_tensor(
                kf2_lo[0:64, CH * j:CH * (j + 1)], pb[0][0:64, 0:512],
                kf2_hi[0:64, CH * j:CH * (j + 1)], SUB),
                waits=[(k.dv, t_kh)])
            mm_free[0] = (k.dv, t_ek)
            t_sq = k.v(lambda v, j=j: v.tensor_tensor(
                vts[0:64, :], kf2_hi[0:64, CH * j:CH * (j + 1)],
                kf2_hi[0:64, CH * j:CH * (j + 1)], MULT),
                waits=[(k.dv, t_kh), (k.pe, vts_rd[0])])
            tnm = mmgroup(6, lambda te, bank: te.matmul(
                bank[0:1, 0:512], onescol[0:64, :], vts[0:64, :], start=True,
                stop=True), [(k.dv, t_sq)])
            vts_rd[0] = tnm

            trd = bank_evac(6, tnm, lambda v: v.tensor_reduce(
                sc[0:1, 48:49], pb[6][0:1, 0:512], AXL, MAXOP),
                extra=[(k.dv, t_relu)])
            k.v(lambda v: v.tensor_tensor(
                sc[0:1, 20:21], sc[0:1, 20:21], sc[0:1, 48:49], MAXOP),
                waits=[(k.dv, trd)])

            tmv = mmgroup(1, lambda te, bank: [
                te.matmul(bank[0:64, 0:512], wv1_sb[:, di, :], lnTd[:, di, :],
                          start=(di == 0), stop=(di == 7))
                for di in range(8)][-1], [(k.dv, t_fm), (ds_w4, t_p4w)])
            t_ev = bank_evac(1, tmv, lambda v: v.tensor_copy(
                vts[0:64, :], pb[1][0:64, 0:512]))
            for tt in range(4):
                ttr = k.t(lambda te, tt=tt: te.transpose(
                    pb[7][:, 0:64], vts[0:64, tt * 128:(tt + 1) * 128],
                    identsb[0:64, 0:64]),
                    waits=[(k.dv, t_ev), mm_free[7]])
                bank_evac(7, ttr, lambda v, tt=tt, j=j: v.tensor_copy(
                    v1sb[:, 4 * j + tt, 0:64], pb[7][:, 0:64]))
            k.flush()

        t_v1o = k.v(lambda v: v.memset(v1sb[:, :, 64:65], 1.0))

        ds_dup.inc(32)
        def _(g):
            g.dma_start(out=kf2_hi[64:128, :], in_=kf2_hi[0:64, :]
                        ).then_inc(ds_dup.sem, 16)
            g.dma_start(out=kf2_lo[64:128, :], in_=kf2_lo[0:64, :]
                        ).then_inc(ds_dup.sem, 16)
            return None
        k.g_(_, waits=[(k.dv, k.dv.n)])
        t_dup = ds_dup.n

        l1_prep = {}
        xfree_d = {0: cell["x_free"], 1: 0}
        l1_pc = [0]

        def l1_prepare(c):
            par = l1_pc[0] % 2
            l1_pc[0] += 1
            xv = xch if par == 0 else xchd2
            lv = lnTd if par == 0 else lnTd2
            g = c // 4
            kkk = c % 4
            ds_x.inc(16)

            def _(s, g=g, kkk=kkk, xv=xv):
                s.dma_start(out=xv[:], in_=ago[kkk][g]
                            .rearrange("(t p) d -> p t d", p=128)
                            ).then_inc(ds_x.sem, 16)
                return None
            k.sy(_, waits=[(k.col, ag_ticks[kkk]), (k.dv, xfree_d[par]),
                           cell["x_dma"]])
            t_ld = ds_x.n
            t_stt = emit_ln_stats(xv, 4, [(ds_x, t_ld)], ws=64 * par)
            t_fm = emit_ln_fm(xv, 4, t_stt, None, None, lv, ws=64 * par)
            xfree_d[par] = t_fm
            l1_prep[c] = (t_fm, lv)

        def l1_get_q(c):
            qs_tick[0] = 0
            if c not in l1_prep:
                l1_prepare(c)
            t_fm, lv = l1_prep.pop(c)
            for p in range(4):
                bi = p % 2
                tmm = mmgroup(bi, lambda te, bank, p=p, lv=lv: [
                    te.matmul(bank[:, 0:512], wq1_sb[:, p, di, :],
                              lv[:, di, :], start=(di == 0), stop=(di == 7))
                    for di in range(8)][-1], [(k.dv, t_fm), (ds_w4, t_p4w)])
                t_s = k.v(lambda v, bi=bi: v.tensor_scalar(
                    qscr_d[:], pb[bi][:, 0:512], 0.125, None, MULT),
                    waits=[(k.pe, tmm)])
                t_h = k.v(lambda v, p=p: v.tensor_copy(
                    qf_hi_d[:, p, :], qscr_d[:]), waits=[(k.dv, t_s)])
                t_ev = k.v(lambda v, p=p: v.tensor_tensor(
                    qf_lo_d[:, p, :], qscr_d[:], qf_hi_d[:, p, :], SUB),
                    waits=[(k.dv, t_h)])
                mm_free[bi] = (k.dv, t_ev)
                t_sq = k.v(lambda v, p=p: v.tensor_tensor(
                    vts[:], qf_hi_d[:, p, :], qf_hi_d[:, p, :], MULT),
                    waits=[(k.dv, t_h), (k.pe, vts_rd[0])])
                tnm = mmgroup(6, lambda te, bank: te.matmul(
                    bank[0:1, 0:512], onescol[:], vts[:], start=True,
                    stop=True), [(k.dv, t_sq)])
                vts_rd[0] = tnm
                bank_evac(6, tnm, lambda v, p=p: v.tensor_reduce(
                    sc[0:1, 40 + p:41 + p], pb[6][0:1, 0:512], AXL, MAXOP))
                qs_tick[0] = max(qs_tick[0], t_ev)
            return qs_tick[0], k.dv.n

        attention("l1", l1_get_q, (kf2_hi, kf2_lo), (v1sb, t_v1o),
                  (ds_w4, t_p4w), rs1i, rs1o, 8, lambda p: 20, True,
                  [(ds_dup, t_dup)], qf_hi_d, qf_lo_d, esb_d,
                  attnT_d, ystg_d, wosb_d, None, None, (ds_b01, t_b01),
                  rs_ticks, prep=l1_prepare)

        k.barrier()
        ph_d.close()

        ph_e = contextlib.ExitStack()
        sbe = mksb(ph_e)
        lnTe = sbe("lnTe", [128, 8, CH])
        h1T = sbe("h1T", [128, 8, CH])
        h2T = sbe("h2T", [128, 8, CH])
        wf1_sb = sbe("wf1_sb", [128, 8, 8, 128])
        wf2_sb = sbe("wf2_sb", [128, 8, 8, 128])
        wf3_sb = sbe("wf3_sb", [128, 8, 8, 128])

        ds_w5.inc(48)
        def _(s):
            for dst, src in ((wf1_sb, wf1), (wf2_sb, wf2), (wf3_sb, wf3)):
                s.dma_start(out=dst[:], in_=src.rearrange("a b p n -> p a b n")
                            ).then_inc(ds_w5.sem, 16)
            return None
        k.adma(_)
        t_p5w = ds_w5.n

        for kk in range(4):
            ds_x.inc(16)
            def _(s, kk=kk):
                s.dma_start(out=xch[:], in_=x2h[CH * kk:CH * (kk + 1), :]
                            .rearrange("(t p) d -> p t d", p=128)
                            ).then_inc(ds_x.sem, 16)
                return None
            k.sy(_, waits=[(k.col, rs_ticks[("l1", kk)]),
                           (k.dv, cell["x_free"]), cell["x_dma"]])
            t_xld = ds_x.n
            cell["x_dma"] = None

            for dc in range(8):
                bi = 6 + (dc % 2)

                sl_ = dc % 2
                ds_stg[sl_].inc(16)
                def _(s, kk=kk, dc=dc, sl_=sl_):
                    s.dma_start(out=stg[:, sl_, :],
                                in_=rs1o[kk][128 * dc:128 * (dc + 1), :]
                                ).then_inc(ds_stg[sl_].sem, 16)
                    return None
                k.sy(_, waits=[(k.pe, stg_read[sl_]), (k.dv, stg_bo[sl_])])
                t_sld = ds_stg[sl_].n
                t_bo = k.v(lambda v, dc=dc, sl_=sl_: v.tensor_scalar(
                    stg[:, sl_, :], stg[:, sl_, :], boutsb[:, dc:dc + 1],
                    None, ADD), waits=[(ds_stg[sl_], t_sld)])
                stg_bo[sl_] = t_bo

                def _t(te, bi=bi, sl_=sl_):
                    r = None
                    for tt in range(4):
                        r = te.transpose(pb[bi][:, tt * 128:(tt + 1) * 128],
                                         stg[:, sl_, tt * 128:(tt + 1) * 128],
                                         identsb[:])
                    return r
                tmm = k.t(_t, waits=[(k.dv, t_bo), mm_free[bi]])
                stg_read[sl_] = tmm

                def _v(v, dc=dc, bi=bi):
                    r = None
                    for tt in range(4):
                        r = v.tensor_tensor(
                            xch[:, tt, 128 * dc:128 * (dc + 1)],
                            pb[bi][:, tt * 128:(tt + 1) * 128],
                            xch[:, tt, 128 * dc:128 * (dc + 1)], ADD)
                    return r
                bank_evac(bi, tmm, _v, extra=[(ds_x, t_xld)])
            t_x3 = k.dv.n

            t_stt = emit_ln_stats(xch, 4, [(k.dv, t_x3)])
            t_fm = emit_ln_fm(xch, 4, t_stt, None, None, lnTe)

            def chain(win, xin, xout, wait_in, t_w):
                t_last = 0
                for do in range(8):
                    bi = do % 2
                    tmm = mmgroup(bi, lambda te, bank, do=do: [
                        te.matmul(bank[:, 0:512], win[:, do, di, :],
                                  xin[:, di, :], start=(di == 0),
                                  stop=(di == 7)) for di in range(8)][-1],
                        [wait_in, (ds_w5, t_w)])
                    t_last = bank_evac(bi, tmm, lambda v, do=do, bi=bi:
                                       v.tensor_copy(xout[:, do, :],
                                                     pb[bi][:, 0:512]))
                return t_last
            t_h1 = chain(wf1_sb, lnTe, h1T, (k.dv, t_fm), t_p5w)
            t_h2 = chain(wf2_sb, h1T, h2T, (k.dv, t_h1), t_p5w)

            for do in range(8):
                bi = do % 2
                tmm = mmgroup(bi, lambda te, bank, do=do: [
                    te.matmul(bank[:, 0:512], wf3_sb[:, do, di, :],
                              h2T[:, di, :], start=(di == 0), stop=(di == 7))
                    for di in range(8)][-1], [(k.dv, t_h2),
                                              (ds_w5, t_p5w)])
                sl_ = do % 2
                t_y = bank_evac(bi, tmm, lambda a, do=do, bi=bi, sl_=sl_:
                    a.activation(
                        stg[:, sl_, :], pb[bi][:, 0:512], AF.Tanh,
                        bias=bias1sb[:, do:do + 1], scale=1.0), eng="a",
                    extra=[(k.pe, stg_read[sl_]), (k.dv, stg_bo[sl_])])

                bj = 6 + (do % 2)

                def _t(te, bj=bj, sl_=sl_):
                    r = None
                    for tt in range(4):
                        r = te.transpose(pb[bj][:, tt * 128:(tt + 1) * 128],
                                         stg[:, sl_, tt * 128:(tt + 1) * 128],
                                         identsb[:])
                    return r
                tmm2 = k.t(_t, waits=[(k.ac, t_y), mm_free[bj]])
                stg_read[sl_] = tmm2

                def _v(v, do=do, bj=bj):
                    r = None
                    for tt in range(4):
                        r = v.tensor_tensor(
                            xch[:, tt, 128 * do:128 * (do + 1)],
                            pb[bj][:, tt * 128:(tt + 1) * 128],
                            xch[:, tt, 128 * do:128 * (do + 1)], ADD)
                    return r
                bank_evac(bj, tmm2, _v)
            t_out = k.dv.n

            k.dout.inc(16)
            def _(s, kk=kk):
                s.dma_start(out=y_out[CH * kk:CH * (kk + 1), :]
                            .rearrange("(t p) d -> p t d", p=128),
                            in_=xch[:]).then_inc(k.dout.sem, 16)
                return None
            k.sy(_, waits=[(k.dv, t_out)])
            out_tick = k.dout.n
            cell["x_free"] = k.v(lambda v: None, waits=[(k.dout, out_tick)])
            k.flush()

        ph_e.close()

        def _(s):
            s.wait_ge(k.dout.sem, k.dout.n)
            return None
        k.sy(_)
        k.flush()

    return nc, tap_outs


_prog_cache = {}


def _get_prog(taps=()):
    key = tuple(sorted(taps))
    if key not in _prog_cache:
        _prog_cache[key] = build_program(taps=key)
    return _prog_cache[key]


def make_in_maps(inputs):
    f32c = lambda a: np.ascontiguousarray(np.asarray(a), dtype=np.float32)
    x = f32c(inputs["L0_attn_x"])
    W_dq = f32c(inputs["L0_attn_W_dq"])
    W_uq = f32c(inputs["L0_attn_W_uq"])
    W_dkv = f32c(inputs["L0_attn_W_dkv"])
    W_uk = f32c(inputs["L0_attn_W_uk"])
    W_uv = f32c(inputs["L0_attn_W_uv"])
    W_o = f32c(inputs["L0_attn_W_o"])
    scl = np.zeros((128, 136), dtype=np.float32)
    scl[:, 0:32] = f32c(inputs["L0_ffn_w1_s"]).reshape(32, 128).T
    scl[:, 32:64] = f32c(inputs["L0_ffn_w2_s"]).reshape(32, 128).T
    scl[:, 64:96] = f32c(inputs["L0_ffn_h1_s"]).reshape(32, 128).T
    scl[:, 96:128] = f32c(inputs["L0_ffn_h2_s"]).reshape(32, 128).T
    scl[:, 128:136] = f32c(inputs["L0_ffn_w3_s"]).reshape(8, 128).T
    Wq1 = f32c(inputs["L1_attn_Wq"])
    Wkv1 = f32c(inputs["L1_attn_Wkv"])
    Wout1 = f32c(inputs["L1_attn_Wout"])
    bout = f32c(inputs["L1_attn_bout"]).reshape(8, 128).T
    rel_emb = f32c(inputs["L1_attn_rel_emb"])
    bias = f32c(inputs["L1_ffn_bias"]).reshape(8, 128).T

    shared = {
        "ident": np.eye(128, dtype=np.float32),
        "onehot": _build_onehot(),
        "wdq": _tile_w(W_dq), "wdkv": _tile_w(W_dkv),
        "w1f": _tile_w(np.ascontiguousarray(f32c(inputs["L0_ffn_w1_w"]).T)),
        "w2f": _tile_w(np.ascontiguousarray(f32c(inputs["L0_ffn_w2_w"]).T)),
        "h1f": _tile_w(np.ascontiguousarray(f32c(inputs["L0_ffn_h1_w"]).T)),
        "h2f": _tile_w(np.ascontiguousarray(f32c(inputs["L0_ffn_h2_w"]).T)),
        "w3f": _tile_w(np.ascontiguousarray(f32c(inputs["L0_ffn_w3_w"]).T)),
        "scl": scl, "bout1": bout, "bias1": bias,
        "wf1": _tile_w(f32c(inputs["L1_ffn_weight"])),
        "wf2": _tile_w(f32c(inputs["L1_ffn_aux_weight1"])),
        "wf3": _tile_w(f32c(inputs["L1_ffn_aux_weight2"])),
    }
    in_maps = []
    for c in range(8):
        b, h = c // 2, c % 2
        hs = slice(512 * h, 512 * (h + 1))
        m = dict(shared)
        m["x_in"] = np.ascontiguousarray(x[b])
        m["x_my"] = np.ascontiguousarray(x[b, TOK * h:TOK * (h + 1)])
        m["wuq"] = _tile_w(np.ascontiguousarray(W_uq[:, hs]))
        m["wuk"] = _tile_w(np.ascontiguousarray(W_uk[:, hs]))
        m["wuv"] = _tile_w(np.ascontiguousarray(W_uv[:, hs]))
        m["wo"] = _tile_w(np.ascontiguousarray(W_o[hs, :]))
        m["wq1"] = _tile_w(np.ascontiguousarray(Wq1[:, hs]))
        m["wk1"] = _tile_w(np.ascontiguousarray(Wkv1[:, 0:64]))[0]
        m["wv1"] = _tile_w(np.ascontiguousarray(Wkv1[:, 64:128]))[0]
        m["wo1"] = _tile_w(np.ascontiguousarray(Wout1[hs, :]))
        m["rel"] = np.ascontiguousarray(rel_emb[:, 8 * h:8 * (h + 1)])
        in_maps.append(m)
    return in_maps


def kernel(**inputs):
    nc, _ = _get_prog()
    in_maps = make_in_maps(inputs)
    res = run_bass_kernel_spmd(nc, in_maps, core_ids=list(range(8)))
    out = np.zeros((4, S, D), dtype=np.float32)
    for c in range(8):
        b, h = c // 2, c % 2
        out[b, TOK * h:TOK * (h + 1)] = res.results[c]["y_out"]
    return out


# version 09
# speedup vs baseline: 1.0348x; 1.0348x over previous
import contextlib
import math

import numpy as np

import concourse.bass as bass
import concourse.mybir as mybir
from concourse.alu_op_type import AluOpType
from concourse.bass_utils import run_bass_kernel_spmd

F32 = mybir.dt.float32
BF16 = mybir.dt.bfloat16
AF = mybir.ActivationFunctionType
ADD = AluOpType.add
SUB = AluOpType.subtract
MULT = AluOpType.mult
MAXOP = AluOpType.max
AXL = mybir.AxisListType.X

S = 4096
D = 1024
TOK = 2048
NCH = 8
CH = 512
HPC = 8
NKT = 32
RELB = 32
RG = [[0, 1], [2, 3], [4, 5], [6, 7]]
VRING = 4
KRING = 4
CHORD = [0, 4, 1, 5, 2, 6, 3, 7]


def _tile_w(w):
    din, dout = w.shape
    dt_ = min(128, dout)
    ndi, ndo = din // 128, dout // dt_
    return np.ascontiguousarray(
        w.reshape(ndi, 128, ndo, dt_).transpose(2, 0, 1, 3))


def _rel_bucket(d):
    n = max(int(d), 0)
    me = RELB // 2
    if n < me:
        return n
    v = me + int(math.log(max(n, 1) / me) / math.log(128 / me) * (RELB - me))
    return min(v, RELB - 1)


def _build_onehot():
    n = 2 * 16384 + 2
    oh = np.zeros((RELB, n), dtype=np.float32)
    for t in range(2):
        for kr in range(128):
            base = t * 16384 + kr * 128
            for qc in range(128):
                d = 128 * t + qc - kr
                if d < 0:
                    continue
                oh[_rel_bucket(d), base + qc] += 8.0
                oh[31, base + qc] -= 8.0
    oh[31, n - 2] = 8.0
    return oh


class Ctr:
    __slots__ = ("sem", "n", "name")

    def __init__(self, sem, name):
        self.sem = sem
        self.n = 0
        self.name = name

    def inc(self, k=1):
        self.n += k
        return self.n


class K:
    ENGS = ("tensor", "vector", "scalar", "sync", "gpsimd")

    def __init__(self, nc, block, es):
        self.nc = nc
        self.block = block
        self.es = es
        self.hw = {e: {} for e in self.ENGS}
        self.bar = []
        self.allc = []
        self.pend = []
        for n in ("pe", "dv", "ac", "gp", "col", "dout"):
            c = Ctr(es.enter_context(nc.semaphore("s_" + n)), n)
            setattr(self, n, c)
            self.allc.append(c)

    def ds(self, name):
        c = Ctr(self.es.enter_context(self.nc.semaphore("d_" + name)), name)
        self.allc.append(c)
        return c

    def barrier(self):
        self.flush()
        self.bar = [(c, c.n) for c in self.allc]

    def _sect(self, eng, waits, fn, tick):
        allw = list(self.bar) + list(waits or ())
        if tick is not None and eng in ("vector", "scalar", "gpsimd") \
                and tick.n > 0:
            allw.append((tick, tick.n))
        tv = tick.inc() if tick is not None else None
        self.pend.append((eng, allw, fn, tick))
        return tv

    def flush(self):
        pend = self.pend
        self.pend = []
        for eng in self.ENGS:
            items = [it for it in pend if it[0] == eng]
            if not items:
                continue
            hw = self.hw[eng]

            def body(e, items=items, hw=hw, eng=eng):
                for _, allw, fn, tick in items:
                    for w in allw:
                        if w is None:
                            continue
                        c, v = w
                        if v is None or v <= 0 or hw.get(c.name, 0) >= v:
                            continue
                        e.wait_ge(c.sem, v)
                        hw[c.name] = v
                    last = fn(e)
                    if tick is not None:
                        if last is None:
                            assert eng in ("gpsimd", "vector", "scalar"), eng
                            last = e.nop()
                        last.then_inc(tick.sem, 1)
            getattr(self.block, eng)(body)

    def t(self, fn, waits=None):
        return self._sect("tensor", waits, fn, self.pe)

    def v(self, fn, waits=None):
        return self._sect("vector", waits, fn, self.dv)

    def a(self, fn, waits=None):
        return self._sect("scalar", waits, fn, self.ac)

    def g(self, fn, waits=None):
        return self._sect("gpsimd", waits, fn, self.gp)

    def g_(self, fn, waits=None):
        return self._sect("gpsimd", waits, fn, None)

    def sy(self, fn, waits=None):
        return self._sect("sync", waits, fn, None)

    def adma(self, fn, waits=None):
        return self._sect("sync", waits, fn, None)


def build_program(taps=()):
    nc = bass.Bass("TRN2", target_bir_lowering=False)
    dt = nc.dram_tensor
    tap_outs = {}

    x_in = dt("x_in", [S, D], F32, kind="ExternalInput").ap()
    x_my = dt("x_my", [TOK, D], F32, kind="ExternalInput").ap()
    ident = dt("ident", [128, 128], F32, kind="ExternalInput").ap()
    onehot = dt("onehot", [RELB, 2 * 16384 + 2], F32, kind="ExternalInput").ap()
    wdq = dt("wdq", [2, 8, 128, 128], F32, kind="ExternalInput").ap()
    wuq = dt("wuq", [4, 2, 128, 128], F32, kind="ExternalInput").ap()
    wdkv = dt("wdkv", [2, 8, 128, 128], F32, kind="ExternalInput").ap()
    wuk = dt("wuk", [4, 2, 128, 128], F32, kind="ExternalInput").ap()
    wuv = dt("wuv", [4, 2, 128, 128], F32, kind="ExternalInput").ap()
    wo = dt("wo", [8, 4, 128, 128], F32, kind="ExternalInput").ap()
    w1f = dt("w1f", [32, 8, 128, 128], F32, kind="ExternalInput").ap()
    w2f = dt("w2f", [32, 8, 128, 128], F32, kind="ExternalInput").ap()
    h1f = dt("h1f", [32, 32, 128, 128], F32, kind="ExternalInput").ap()
    h2f = dt("h2f", [32, 32, 128, 128], F32, kind="ExternalInput").ap()
    w3f = dt("w3f", [8, 32, 128, 128], F32, kind="ExternalInput").ap()
    scl = dt("scl", [128, 136], F32, kind="ExternalInput").ap()
    wq1 = dt("wq1", [4, 8, 128, 128], F32, kind="ExternalInput").ap()
    wk1 = dt("wk1", [8, 128, 64], F32, kind="ExternalInput").ap()
    wv1 = dt("wv1", [8, 128, 64], F32, kind="ExternalInput").ap()
    wo1 = dt("wo1", [8, 4, 128, 128], F32, kind="ExternalInput").ap()
    bout1 = dt("bout1", [128, 8], F32, kind="ExternalInput").ap()
    rel = dt("rel", [RELB, HPC], F32, kind="ExternalInput").ap()
    wf1 = dt("wf1", [8, 8, 128, 128], F32, kind="ExternalInput").ap()
    wf2 = dt("wf2", [8, 8, 128, 128], F32, kind="ExternalInput").ap()
    wf3 = dt("wf3", [8, 8, 128, 128], F32, kind="ExternalInput").ap()
    bias1 = dt("bias1", [128, 8], F32, kind="ExternalInput").ap()
    y_out = dt("y_out", [TOK, D], F32, kind="ExternalOutput").ap()

    w1b = dt("w1b", [32, 8, 128, 128], BF16, kind="Internal").ap()
    w2b = dt("w2b", [32, 8, 128, 128], BF16, kind="Internal").ap()
    h1b = dt("h1b", [32, 32, 128, 128], BF16, kind="Internal").ap()
    h2b = dt("h2b", [32, 32, 128, 128], BF16, kind="Internal").ap()
    w3b = dt("w3b", [8, 32, 128, 128], BF16, kind="Internal").ap()
    v0h = dt("v0h", [NKT, 128, HPC, 65], F32, kind="Internal").ap()
    kfh = dt("kfh", [NKT, 128, 4, 2, 128], BF16, kind="Internal").ap()
    cqh = dt("cqh", [2, 128, S], F32, kind="Internal").ap()
    strh = dt("strh", [HPC, 2 * 16384 + 2], F32, kind="Internal").ap()
    x2h = dt("x2h", [TOK, D], F32, kind="Internal").ap()
    rs0i = [dt(f"rs0i{j}", [2, D, CH], F32, kind="Internal").ap() for j in range(4)]
    rs0o = [dt(f"rs0o{j}", [D, CH], F32, kind="Internal").ap() for j in range(4)]
    agi = [dt(f"agi{j}", [CH, D], F32, kind="Internal").ap() for j in range(4)]
    ago = [dt(f"ago{j}", [2, CH, D], F32, kind="Internal").ap() for j in range(4)]
    rs1i = [dt(f"rs1i{j}", [2, D, CH], F32, kind="Internal").ap() for j in range(4)]
    rs1o = [dt(f"rs1o{j}", [D, CH], F32, kind="Internal").ap() for j in range(4)]

    es = contextlib.ExitStack()

    def mksb(stack):
        def sbuf(name, shape, d=F32):
            return stack.enter_context(nc.sbuf_tensor(name, list(shape), d))
        return sbuf

    with nc.Block() as block, es:
        sbuf = mksb(es)
        k = K(nc, block, es)
        pb = [es.enter_context(nc.psum_tensor(f"pb{i}", [128, 512], F32))
              for i in range(8)]
        mm_free = [(k.dv, 0) for _ in range(8)]
        ds_consts = k.ds("consts")
        ds_conv = k.ds("conv")
        ds_oh = [k.ds(f"oh{i}") for i in range(2)]
        ds_st = [k.ds(f"st{i}") for i in range(2)]
        ds_str = k.ds("str")
        ds_scg = k.ds("scg")
        ds_x = k.ds("x")
        ds_cq = k.ds("cq")
        ds_ks = k.ds("ks")
        ds_vst = k.ds("vst")
        ds_wa = k.ds("wa")
        ds_wo = k.ds("wo")
        ds_cqb = k.ds("cqb")
        ds_vr = [k.ds(f"vr{i}") for i in range(VRING)]
        ds_kr = [k.ds(f"kr{i}") for i in range(KRING)]
        ds_rs = [k.ds(f"rs{i}") for i in range(2)]
        ds_ag = [k.ds(f"ag{i}") for i in range(2)]
        ds_stg = [k.ds("stg0"), k.ds("stg1")]
        ds_wr = [k.ds(f"wr{i}") for i in range(2)]
        ds_b01 = k.ds("b01")
        ds_w4 = k.ds("w4")
        ds_dup = k.ds("dup")
        ds_w5 = k.ds("w5")
        ds_o2 = k.ds("o2")

        identsb = sbuf("identsb", [128, 128])
        sc = sbuf("sc", [128, 64])
        onescol = sbuf("onescol", [128, 1])
        onesrow = sbuf("onesrow", [1, 128])
        epsv = sbuf("epsv", [128, 1])
        zerov = sbuf("zerov", [1, 1])
        bvec = sbuf("bvec", [128, 8])
        boutsb = sbuf("boutsb", [128, 8])
        bias1sb = sbuf("bias1sb", [128, 8])
        sclsb = sbuf("sclsb", [128, 136])
        lnws = sbuf("lnws", [128, 128])
        relsb = sbuf("relsb", [RELB, HPC])
        maxcsb = sbuf("maxcsb", [HPC, 1])
        xch = sbuf("xch", [128, 4, D])
        vts = sbuf("vts", [128, 512])
        stg = sbuf("stg", [128, 2, 512])
        stg64 = sbuf("stg64", [128, 512])
        rcrow = sbuf("rcrow", [128, 2, 512])
        rbs = sbuf("rbs", [128, 512])
        onesr64 = sbuf("onesr64", [128, 64])
        masks = sbuf("masks", [128, 4, 512], BF16)
        wuq_sb = sbuf("wuq_sb", [128, 4, 2, 128])

        stg_read = [0, 0]
        stg_bo = [0, 0]
        vts_rd = [0]
        cell = {"x_free": 0, "x_dma": None}

        ds_consts.inc(80)
        def _(s):
            s.dma_start(out=identsb[:], in_=ident[:]).then_inc(ds_consts.sem, 16)
            s.dma_start(out=boutsb[:], in_=bout1[:]).then_inc(ds_consts.sem, 16)
            s.dma_start(out=bias1sb[:], in_=bias1[:]).then_inc(ds_consts.sem, 16)
            s.dma_start(out=sclsb[:], in_=scl[:]).then_inc(ds_consts.sem, 16)
            s.dma_start(out=relsb[:], in_=rel[:]).then_inc(ds_consts.sem, 16)
            return None
        k.sy(_)
        ld_consts = ds_consts.n

        def _spin(n_mm):
            def _(te):
                r = None
                for j in range(n_mm):
                    r = te.matmul(pb[0][:, (j % 4) * 128:(j % 4) * 128 + 128],
                                  identsb[:], identsb[:], start=True,
                                  stop=True)
                return r
            return _

        k.t(_spin(800), waits=[(ds_consts, ld_consts)])

        ds_conv.inc(80)
        def _(g):
            for src, dst in ((w1f, w1b), (w2f, w2b), (h1f, h1b), (h2f, h2b),
                             (w3f, w3b)):
                g.dma_start(out=dst[:], in_=src[:]).then_inc(ds_conv.sem, 16)
            return None
        k.g_(_)
        t_conv = ds_conv.n

        def _(v):
            v.memset(onescol[:], 1.0)
            v.memset(onesrow[:], 1.0)
            v.memset(onesr64[64:65, :], 1.0)
            v.memset(epsv[:], 1e-5)
            v.memset(zerov[:], 0.0)
            return v.memset(sc[:], 0.0)
        t_mem = k.v(_)

        def _(g):
            r = None
            for m in range(4):
                r = g.memset(masks[:, m, :], 1.0)
            return r
        t_msk0 = k.g(_)

        def _(g):
            r = None
            for m in range(4):
                r = g.affine_select(masks[:, m, :], masks[:, m, :],
                                    [[1, 512]], AluOpType.is_ge, 0.0,
                                    base=-128 * m, channel_multiplier=-1)
            return r
        t_masks = k.g(_, waits=[(k.gp, t_msk0)])

        NCOLS = 2 * 16384 + 2
        t_relu = 0

        def emit_ln_stats(xin, nt, wait_list, ws=0):
            def _(v):
                r = None
                for tt in range(nt):
                    o = ws + 16 * tt
                    v.bn_stats(lnws[:, o:o + 6], xin[:, tt, 0:512])
                    r = v.bn_stats(lnws[:, o + 6:o + 12], xin[:, tt, 512:1024])
                return r
            t1 = k.v(_, waits=wait_list)

            def _(v):
                r = None
                for tt in range(nt):
                    o = ws + 16 * tt
                    r = v.bn_aggr(lnws[:, o + 12:o + 14], lnws[:, o:o + 12])
                return r
            t2 = k.v(_, waits=[(k.dv, t1)])

            def _(v):
                r = None
                for tt in range(nt):
                    o = ws + 16 * tt
                    r = v.tensor_scalar(lnws[:, o + 14:o + 15],
                                        lnws[:, o + 12:o + 13], -1.0, None,
                                        MULT)
                return r
            t3 = k.v(_, waits=[(k.dv, t2)])

            def _(a):
                r = None
                for tt in range(nt):
                    o = ws + 16 * tt
                    r = a.activation(lnws[:, o + 15:o + 16],
                                     lnws[:, o + 13:o + 14], AF.Sqrt,
                                     bias=epsv[:], scale=1.0)
                return r
            t4 = k.a(_, waits=[(k.dv, t3)])

            def _(v):
                r = None
                for tt in range(nt):
                    o = ws + 16 * tt
                    r = v.reciprocal(lnws[:, o + 15:o + 16],
                                     lnws[:, o + 15:o + 16])
                return r
            return k.v(_, waits=[(k.ac, t4)])

        def emit_ln_fm(xin, nt, t_stats, dst_hi, dst_lo, dst_f32, ndc=8,
                       ws=0):
            last = 0
            for dc in range(ndc):
                sl_ = dc % 2

                def _sl(v, dc=dc, sl_=sl_):
                    r = None
                    for tt in range(nt):
                        o = ws + 16 * tt
                        r = v.tensor_scalar(
                            stg[:, sl_, 128 * tt:128 * (tt + 1)],
                            xin[:, tt, 128 * dc:128 * (dc + 1)],
                            lnws[:, o + 14:o + 15], lnws[:, o + 15:o + 16],
                            ADD, op1=MULT)
                    return r
                bi = 6 + (dc % 2)
                tsl = k.v(_sl, waits=[(k.dv, t_stats), (k.pe, stg_read[sl_])])

                def _t(te, bi=bi, sl_=sl_):
                    r = None
                    for tt in range(nt):
                        r = te.transpose(pb[bi][:, tt * 128:(tt + 1) * 128],
                                         stg[:, sl_, 128 * tt:128 * (tt + 1)],
                                         identsb[:])
                    return r
                tmm = k.t(_t, waits=[(k.dv, tsl), mm_free[bi]])
                stg_read[sl_] = tmm

                if dst_f32 is not None:
                    t = k.v(lambda v, dc=dc, bi=bi: v.tensor_copy(
                        dst_f32[:, dc, 0:nt * 128], pb[bi][:, 0:nt * 128]),
                        waits=[(k.pe, tmm)])
                else:
                    t0 = k.v(lambda v, dc=dc, bi=bi: v.tensor_copy(
                        dst_hi[:, dc, 0:nt * 128], pb[bi][:, 0:nt * 128]),
                        waits=[(k.pe, tmm)])
                    t = k.v(lambda v, dc=dc, bi=bi: v.tensor_tensor(
                        dst_lo[:, dc, 0:nt * 128], pb[bi][:, 0:nt * 128],
                        dst_hi[:, dc, 0:nt * 128], SUB),
                        waits=[(k.dv, t0)])
                mm_free[bi] = (k.dv, t)
                last = t
            return last

        def mmgroup(bi, emit, wait_list):
            return k.t(lambda te: emit(te, pb[bi]),
                       waits=list(wait_list) + [mm_free[bi]])

        def bank_evac(bi, t_pe, fn, eng="v", extra=None):
            f = {"v": k.v, "a": k.a}[eng]
            w = [(k.pe, t_pe)]
            if extra:
                w += list(extra)
            t = f(fn, waits=w)
            mm_free[bi] = ({"v": k.dv, "a": k.ac}[eng], t)
            return t

        ph_a = contextlib.ExitStack()
        sba = mksb(ph_a)
        lnTa = sba("lnTa", [128, 8, CH])
        lnTa2 = sba("lnTa2", [128, 8, CH])
        xch2 = sba("xch2", [128, 4, D])
        ohtile = sba("ohtile", [RELB, 2, 2048])
        sttile = sba("sttile", [HPC, 2, 2048])
        ckvT = sba("ckvT", [128, 2, CH])
        cqst = sba("cqst", [128, 2, CH])
        ksst = sba("ksst", [128, 4, 2, CH], BF16)
        vstage = sba("vstage", [128, 4, HPC, 65])
        wdq_sb = sba("wdq_sb", [128, 2, 8, 128])
        wdkv_sb = sba("wdkv_sb", [128, 2, 8, 128])
        wuk_sb = sba("wuk_sb", [128, 4, 2, 128])
        wuv_sb = sba("wuv_sb", [128, 4, 2, 128])


        BL = 2048
        strip_blocks = [(i, c0, min(BL, NCOLS - c0))
                        for i, c0 in enumerate(range(0, NCOLS, BL))]
        strip_state = {"oh_mm": {}, "st_wr": {}, "tev": 0, "idx": 0}

        def strip_round():
            if strip_state["idx"] >= len(strip_blocks):
                return
            i, c0, n = strip_blocks[strip_state["idx"]]
            strip_state["idx"] += 1
            oh_mm = strip_state["oh_mm"]
            st_wr = strip_state["st_wr"]

            ds_oh[i % 2].inc(16)
            def _(s, i=i, c0=c0, n=n):
                s.dma_start(out=ohtile[:, i % 2, 0:n],
                            in_=onehot[:, c0:c0 + n]
                            ).then_inc(ds_oh[i % 2].sem, 16)
                return None
            k.sy(_, waits=[(k.pe, oh_mm.get(i - 2, 0))])
            t_oh = ds_oh[i % 2].n
            for b0 in range(0, n, 512):
                nb = min(512, n - b0)
                tmm = k.t(lambda te, i=i, b0=b0, nb=nb: te.matmul(
                    pb[i % 2][0:HPC, 0:nb], relsb[:],
                    ohtile[:, i % 2, b0:b0 + nb], start=True, stop=True),
                    waits=[(ds_oh[i % 2], t_oh), mm_free[i % 2],
                           (ds_consts, ld_consts)])
                oh_mm[i] = tmm

                def _ev(v, i=i, b0=b0, nb=nb):
                    v.tensor_copy(sttile[:, i % 2, b0:b0 + nb],
                                  pb[i % 2][0:HPC, 0:nb])
                    return v.tensor_reduce(sc[0:8, 49:50],
                                           pb[i % 2][0:HPC, 0:nb], AXL, MAXOP)
                tev0 = k.v(_ev, waits=[(k.pe, tmm), (k.dv, t_mem),
                                       (ds_st[i % 2], st_wr.get(i % 2, 0))])
                tev = k.v(lambda v, i=i, b0=b0: v.tensor_tensor(
                    maxcsb[:, 0:1],
                    (maxcsb[:, 0:1] if (i or b0) else sc[0:8, 49:50]),
                    sc[0:8, 49:50], MAXOP), waits=[(k.dv, tev0)])
                mm_free[i % 2] = (k.dv, tev)
                strip_state["tev"] = tev

            ds_st[i % 2].inc(16)
            def _(s, i=i, c0=c0, n=n):
                s.dma_start(out=strh[:, c0:c0 + n],
                            in_=sttile[:, i % 2, 0:n]
                            ).then_inc(ds_st[i % 2].sem, 16)
                return None
            k.sy(_, waits=[(k.dv, strip_state["tev"])])
            st_wr[i % 2] = ds_st[i % 2].n

        def strip_finish():
            while strip_state["idx"] < len(strip_blocks):
                strip_round()
            ds_scg.inc(32)
            def _(g):
                g.dma_start(out=sc[0:1, 0:8],
                            in_=strh[:, NCOLS - 2:NCOLS - 1]
                            .rearrange("h o -> o h"),
                            allow_slow_non_contiguous=True
                            ).then_inc(ds_scg.sem, 16)
                g.dma_start(out=sc[0:1, 8:16], in_=maxcsb[:],
                            allow_slow_non_contiguous=True
                            ).then_inc(ds_scg.sem, 16)
                return None
            k.g_(_, waits=[(ds_st[0], ds_st[0].n), (ds_st[1], ds_st[1].n),
                           (k.dv, strip_state["tev"])])
            return k.v(lambda v: v.tensor_scalar(
                sc[0:1, 8:16], sc[0:1, 8:16], 0.0, None, MAXOP),
                waits=[(ds_scg, ds_scg.n)])

        ds_wa.inc(80)
        def _(s):
            for dst, src in ((wdq_sb, wdq), (wdkv_sb, wdkv), (wuk_sb, wuk),
                             (wuv_sb, wuv), (wuq_sb, wuq)):
                s.dma_start(out=dst[:], in_=src.rearrange("a b p n -> p a b n")
                            ).then_inc(ds_wa.sem, 16)
            return None
        k.adma(_)
        t_p1w = ds_wa.n

        v0_wr = [0] * NCH
        vst_dma = 0
        cq_dma = 0
        ks_dma = 0

        xfree_p = [0, 0]
        for j in range(NCH):
            strip_round()
            strip_round()
            par = j % 2
            xv = xch if par == 0 else xch2
            lv = lnTa if par == 0 else lnTa2
            ds_x.inc(16)
            def _(s, j=j, xv=xv):
                s.dma_start(out=xv[:],
                            in_=x_in[CH * j:CH * (j + 1), :]
                            .rearrange("(t p) d -> p t d", p=128)
                            ).then_inc(ds_x.sem, 16)
                return None
            k.sy(_, waits=[(k.dv, xfree_p[par])])
            t_x = ds_x.n

            t_st = emit_ln_stats(xv, 4, [(ds_x, t_x)], ws=64 * par)
            t_fm = emit_ln_fm(xv, 4, t_st, None, None, lv, ws=64 * par)
            xfree_p[par] = t_fm

            for do in range(2):
                tmm = mmgroup(do, lambda te, bank, do=do, lv=lv: [
                    te.matmul(bank[:, 0:512], wdq_sb[:, do, di, :],
                              lv[:, di, :], start=(di == 0), stop=(di == 7))
                    for di in range(8)][-1],
                    [(k.dv, t_fm), (ds_wa, t_p1w)])
                bank_evac(do, tmm, lambda v, do=do: v.tensor_copy(
                    cqst[:, do, :], pb[do][:, 0:512]),
                    extra=[(ds_cq, cq_dma)])
            t_cq = k.dv.n

            ds_cq.inc(16)
            def _(s, j=j):
                s.dma_start(out=cqh[:, :, CH * j:CH * (j + 1)]
                            .rearrange("a p n -> p a n"),
                            in_=cqst[:]).then_inc(ds_cq.sem, 16)
                return None
            k.sy(_, waits=[(k.dv, t_cq)])
            cq_dma = ds_cq.n

            for do in range(2):
                tmm = mmgroup(2 + do, lambda te, bank, do=do, lv=lv: [
                    te.matmul(bank[:, 0:512], wdkv_sb[:, do, di, :],
                              lv[:, di, :], start=(di == 0), stop=(di == 7))
                    for di in range(8)][-1],
                    [(k.dv, t_fm), (ds_wa, t_p1w)])
                bank_evac(2 + do, tmm, lambda v, do=do: v.tensor_copy(
                    ckvT[:, do, :], pb[2 + do][:, 0:512]))
            t_ckv = k.dv.n

            for do in range(4):
                bi = 4 + (do % 2)
                tmm = mmgroup(bi, lambda te, bank, do=do: [
                    te.matmul(bank[:, 0:512], wuk_sb[:, do, di, :],
                              ckvT[:, di, :], start=(di == 0), stop=(di == 1))
                    for di in range(2)][-1], [(k.dv, t_ckv)])
                t_hi = k.v(lambda v, do=do, bi=bi: v.tensor_copy(
                    ksst[:, do, 0, :], pb[bi][:, 0:512]),
                    waits=[(k.pe, tmm), (ds_ks, ks_dma)])
                t_ev = k.v(lambda v, do=do, bi=bi: v.tensor_tensor(
                    ksst[:, do, 1, :], pb[bi][:, 0:512],
                    ksst[:, do, 0, :], SUB), waits=[(k.dv, t_hi)])
                mm_free[bi] = (k.dv, t_ev)
                t_sq = k.v(lambda v, do=do: v.tensor_tensor(
                    vts[:], ksst[:, do, 0, :], ksst[:, do, 0, :], MULT),
                    waits=[(k.dv, t_hi), (k.pe, vts_rd[0])])
                tnm = mmgroup(6, lambda te, bank: te.matmul(
                    bank[0:1, 0:512], onescol[:], vts[:], start=True,
                    stop=True), [(k.dv, t_sq)])
                vts_rd[0] = tnm

                trd = bank_evac(6, tnm, lambda v: v.tensor_reduce(
                    sc[0:1, 48:49], pb[6][0:1, 0:512], AXL, MAXOP),
                    extra=[(k.dv, t_relu)])
                k.v(lambda v, do=do: v.tensor_tensor(
                    sc[0:1, 16 + do:17 + do], sc[0:1, 16 + do:17 + do],
                    sc[0:1, 48:49], MAXOP), waits=[(k.dv, trd)])
            t_ks = k.dv.n

            ds_ks.inc(16)
            def _(s, j=j):
                s.dma_start(out=kfh[4 * j:4 * (j + 1)]
                            .rearrange("t p a b n -> p a b t n"),
                            in_=ksst[:].rearrange("p a b (t n) -> p a b t n",
                                                  n=128)
                            ).then_inc(ds_ks.sem, 16)
                return None
            k.sy(_, waits=[(k.dv, t_ks)])
            ks_dma = ds_ks.n

            for do in range(4):
                bi = 4 + (do % 2)
                tmm = mmgroup(bi, lambda te, bank, do=do: [
                    te.matmul(bank[:, 0:512], wuv_sb[:, do, di, :],
                              ckvT[:, di, :], start=(di == 0), stop=(di == 1))
                    for di in range(2)][-1], [(k.dv, t_ckv)])
                t_ev = bank_evac(bi, tmm, lambda v, bi=bi: v.tensor_copy(
                    vts[:], pb[bi][:, 0:512]))
                for tt in range(4):
                    ttr = k.t(lambda te, tt=tt: te.transpose(
                        pb[7][:, 0:128], vts[:, tt * 128:(tt + 1) * 128],
                        identsb[:]), waits=[(k.dv, t_ev), mm_free[7]])

                    def _vv(v, tt=tt, do=do):
                        v.tensor_copy(vstage[:, tt, 2 * do, 0:64],
                                      pb[7][:, 0:64])
                        return v.tensor_copy(vstage[:, tt, 2 * do + 1, 0:64],
                                             pb[7][:, 64:128])
                    bank_evac(7, ttr, _vv, extra=[(ds_vst, vst_dma)])
            t_vst = k.v(lambda v: v.memset(vstage[:, :, :, 64:65], 1.0),
                        waits=[mm_free[7]])

            ds_vst.inc(16)
            def _(s, j=j):
                s.dma_start(out=v0h[4 * j:4 * (j + 1)]
                            .rearrange("t p h c -> p t h c"),
                            in_=vstage[:]).then_inc(ds_vst.sem, 16)
                return None
            k.sy(_, waits=[(k.dv, t_vst)])
            v0_wr[j] = ds_vst.n
            vst_dma = ds_vst.n
            k.flush()

        t_relu = strip_finish()
        k.barrier()
        ph_a.close()

        def attention(tag, get_q, kf_mode, vt_mode, wout_ds, rsi, rso,
                      mc_base, kn_col, use_bias, extra_w, qf_hi, qf_lo,
                      esb, attnT, ystg, wosb, vring, kring, b01_w,
                      rs_ticks):
            attn_guard = (k.pe, 0)
            yp_dma = None
            rs_wr = [[0, 0] for _ in range(4)]
            vr_av = {}
            kr_sc = {}
            for ci, c in enumerate(CHORD):
                t_qs, t_qn = get_q(c)
                t_q = t_qs

                def _m(v):
                    r = None
                    for p in range(4):
                        r = v.tensor_tensor(
                            sc[0:1, 44 + p:45 + p], sc[0:1, 40 + p:41 + p],
                            sc[0:1, kn_col(p):kn_col(p) + 1], MULT)
                    return r
                tm1 = k.v(_m, waits=[(k.dv, t_qn)])

                def _s(a):
                    r = None
                    for p in range(4):
                        r = a.activation(sc[0:1, 44 + p:45 + p],
                                         sc[0:1, 44 + p:45 + p], AF.Sqrt,
                                         bias=zerov[:], scale=1.0201)
                    return r
                tm2 = k.a(_s, waits=[(k.dv, tm1)])

                def _b(v):
                    r = None
                    for p in range(4):
                        r = v.tensor_scalar(
                            sc[0:1, 24 + 2 * p:26 + 2 * p],
                            sc[0:1, mc_base + 2 * p:mc_base + 2 * p + 2],
                            sc[0:1, 44 + p:45 + p], -1.0, ADD, op1=MULT)
                    return r
                tm3 = k.v(_b, waits=[(k.ac, tm2)])
                tbc = mmgroup(6, lambda te, bank: te.matmul(
                    bank[0:128, 0:8], onesrow[:], sc[0:1, 24:32],
                    start=True, stop=True), [(k.dv, tm3)])
                t_bv = bank_evac(6, tbc, lambda v: v.tensor_copy(
                    bvec[:], pb[6][0:128, 0:8]))

                nkt = 4 * c + 4
                epi_pend = [None]
                t_at = 0
                for p in range(4):
                    if epi_pend[0] is not None and nkt == 4:
                        t_at = epi_pend[0]()
                        epi_pend[0] = None
                    av_done = {}
                    av_args = {}
                    for t in range(nkt + 1):
                      if t < nkt:
                        sb0 = 2 * (t % 2)
                        if kf_mode is None:
                            sl = t % KRING

                            ds_kr[sl].inc(16)
                            def _kld(s, t=t, sl=sl, p=p):
                                s.dma_start(out=kring[:, sl, :, :],
                                            in_=kfh[t][:, p]
                                            ).then_inc(ds_kr[sl].sem, 16)
                                return None
                            k.sy(_kld, waits=[kr_sc.get(sl, (k.pe, 0))])
                            kw = [(ds_kr[sl], ds_kr[sl].n)]

                            def kf_sl(kv, po, sl=sl):
                                return kring[po:po + 64, sl,
                                             0 if kv == "hi" else 1, :]
                        else:
                            kf2h, kf2l = kf_mode
                            kw = []

                            def kf_sl(kv, po, t_=t):
                                src = kf2h if kv == "hi" else kf2l
                                return src[po:po + 64,
                                           128 * t_:128 * (t_ + 1)]

                        if vt_mode is None:
                            slv = t % VRING

                            ds_vr[slv].inc(16)
                            def _vld(s, t=t, slv=slv, p=p):
                                s.dma_start(out=vring[:, slv, :, :],
                                            in_=v0h[t][:, 2 * p:2 * p + 2, :]
                                            ).then_inc(ds_vr[slv].sem, 16)
                                return None
                            k.sy(_vld, waits=[vr_av.get(slv, (k.pe, 0))])
                            vt_get = (lambda hh, slv=slv:
                                      vring[:, slv, hh, :])
                            vw = [(ds_vr[slv], ds_vr[slv].n)]
                        else:
                            v1sb_, t_v1 = vt_mode
                            vt_get = (lambda hh, t=t: v1sb_[:, t, :])
                            vw = [(k.dv, t_v1)]

                        def _sc(te, p=p, t=t, sb0=sb0, kf_sl=kf_sl):
                            r = None
                            for hh in range(2):
                                po = 64 * hh
                                bank = pb[sb0 + hh]
                                for i, (qop, kv) in enumerate(
                                        ((qf_hi, "hi"), (qf_lo, "hi"),
                                         (qf_hi, "lo"))):
                                    r = te.matmul(
                                        bank[:, 0:512], kf_sl(kv, po),
                                        qop[po:po + 64, p, :],
                                        start=(i == 0), stop=(i == 2),
                                        tile_position=(po, 0))
                            return r
                        t_smm = k.t(_sc, [(k.dv, t_q), mm_free[sb0],
                                          mm_free[sb0 + 1]] + kw +
                                    list(extra_w))
                        if kf_mode is None:
                            kr_sc[sl] = (k.pe, t_smm)

                        t_badd = 0
                        if use_bias:
                            addv = []
                            for qs in range(4):
                                qt = 4 * c + qs
                                if t == qt:
                                    addv.append((qs, b0t))
                                elif t == qt - 1:
                                    addv.append((qs, b1t))
                            if addv:
                                def _ba(v, p=p, addv=addv, sb0=sb0):
                                    r = None
                                    for hh in range(2):
                                        h = 2 * p + hh
                                        for qs, bt in addv:
                                            sl_ = slice(128 * qs,
                                                        128 * (qs + 1))
                                            r = v.tensor_tensor(
                                                pb[sb0 + hh][:, sl_],
                                                pb[sb0 + hh][:, sl_],
                                                bt[:, h, :], ADD)
                                    return r
                                t_badd = k.v(_ba, waits=[(k.pe, t_smm),
                                                         b01_w])

                        def _ex(a, p=p, t=t, sb0=sb0):
                            r = None
                            for hh in range(2):
                                h = 2 * p + hh
                                r = a.activation(
                                    esb[:, hh, t % 2, :],
                                    pb[sb0 + hh][:, 0:512], AF.Exp,
                                    bias=bvec[:, h:h + 1], scale=1.0)
                            return r
                        wex = [(k.pe, t_smm), (k.dv, t_bv)]
                        if t_badd:
                            wex.append((k.dv, t_badd))
                        if t >= 2:
                            wex.append(av_done[t - 2])
                        t_exp = k.a(_ex, waits=wex)
                        mm_free[sb0] = (k.ac, t_exp)
                        mm_free[sb0 + 1] = (k.ac, t_exp)
                        if t == 0 and epi_pend[0] is not None:
                            t_at = epi_pend[0]()
                            epi_pend[0] = None

                        t_mask = 0
                        if t >= 4 * c:
                            mi = t - 4 * c

                            def _mk(v, mi=mi, t=t):
                                r = None
                                for hh in range(2):
                                    r = v.tensor_tensor(
                                        esb[:, hh, t % 2, :],
                                        esb[:, hh, t % 2, :],
                                        masks[:, mi, :], MULT)
                                return r
                            t_mask = k.v(_mk, waits=[(k.ac, t_exp),
                                                     (k.gp, t_masks)])

                        av_args[t] = (vt_get,
                                      list(vw) + [(k.dv, t_mask) if t_mask
                                                  else (k.ac, t_exp)],
                                      slv if vt_mode is None else None)
                      ta = t - 1
                      if ta >= 0:
                        vt_get_a, w_av, slv_a = av_args.pop(ta)

                        def _av(te, ta=ta, nkt=nkt, vt_get_a=vt_get_a):
                            r = None
                            for hh in range(2):
                                r = te.matmul(
                                    pb[4 + hh][0:65, 0:512],
                                    vt_get_a(hh),
                                    esb[:, hh, ta % 2, :],
                                    start=(ta == 0), stop=(ta == nkt - 1))
                            return r
                        if ta == 0:
                            w_av = w_av + [mm_free[4], mm_free[5]]
                        tick_av = k.t(_av, w_av)
                        av_done[ta] = (k.pe, tick_av)
                        if slv_a is not None:
                            vr_av[slv_a] = (k.pe, tick_av)

                    def epilogue(p=p, av_last=av_done[nkt - 1],
                                 ag=attn_guard):
                        t_rc = k.v(lambda v: [
                            v.reciprocal(rcrow[64:65, hh, :],
                                         pb[4 + hh][64:65, 0:512])
                            for hh in range(2)][-1],
                            waits=[av_last])
                        for hh in range(2):
                            t_bcm = mmgroup(6, lambda te, bank, hh=hh:
                                te.matmul(
                                    bank[0:64, 0:512], onesr64[64:65, :],
                                    rcrow[64:65, hh, :], start=True,
                                    stop=True, tile_position=(64, 0)),
                                [(k.dv, t_rc), (k.dv, t_mem)])
                            t_rb = bank_evac(6, t_bcm, lambda v:
                                v.tensor_copy(rbs[0:64, :],
                                              pb[6][0:64, 0:512]))
                            if hh == 0:
                                t_mu = k.v(lambda v, p=p: v.tensor_tensor(
                                    attnT[0:64, p, :], pb[4][0:64, 0:512],
                                    rbs[0:64, :], MULT),
                                    waits=[(k.dv, t_rb), ag])
                                mm_free[4] = (k.dv, t_mu)
                            else:
                                t_mu = k.v(lambda v: v.tensor_tensor(
                                    stg64[0:64, :], pb[5][0:64, 0:512],
                                    rbs[0:64, :], MULT),
                                    waits=[(k.dv, t_rb), (ds_o2, ds_o2.n)])
                                mm_free[5] = (k.dv, t_mu)

                                ds_o2.inc(16)
                                def _od(s, p=p):
                                    s.dma_start(out=attnT[64:128, p, :],
                                                in_=stg64[0:64, :]
                                                ).then_inc(ds_o2.sem, 16)
                                    return None
                                k.sy(_od, waits=[(k.dv, t_mu)])
                        return k.dv.n
                    epi_pend[0] = epilogue
                if epi_pend[0] is not None:
                    t_at = epi_pend[0]()
                    epi_pend[0] = None
                k.flush()

                for dh in range(4):
                    for do2 in range(2):
                        do = 2 * dh + do2
                        bi = do2
                        tmm = mmgroup(bi, lambda te, bank, do=do: [
                            te.matmul(bank[:, 0:512], wosb[:, do, di, :],
                                      attnT[:, di, :], start=(di == 0),
                                      stop=(di == 3)) for di in range(4)][-1],
                            [(k.dv, t_at), wout_ds, (ds_o2, ds_o2.n)])
                        bank_evac(bi, tmm, lambda v, do2=do2, bi=bi:
                                  v.tensor_copy(ystg[:, do2, :],
                                                pb[bi][:, 0:512]),
                                  extra=[yp_dma])
                    t_y2 = k.dv.n

                    ds_rs[(c % 4) % 2].inc(16)
                    def _(s, c=c, dh=dh):
                        s.dma_start(
                            out=rsi[c % 4][c // 4, 256 * dh:256 * (dh + 1), :]
                            .rearrange("(dc p) n -> p dc n", p=128),
                            in_=ystg[:]).then_inc(ds_rs[(c % 4) % 2].sem, 16)
                        return None
                    k.sy(_, waits=[(k.dv, t_y2)])
                    yp_dma = (ds_rs[(c % 4) % 2], ds_rs[(c % 4) % 2].n)
                attn_guard = (k.pe, k.pe.n)

                if ci % 2 == 1:
                    j = c % 4

                    def _(g, j=j):
                        r = g.collective_compute(
                            "ReduceScatter", ADD, ins=[rsi[j][:]],
                            outs=[rso[j][:]], replica_groups=RG)
                        r.then_inc(k.col.sem, 1)
                        return None
                    k.col.inc()
                    k.g_(_, waits=[(ds_rs[j % 2], ds_rs[j % 2].n)])
                    rs_ticks[(tag, j)] = k.col.n
                k.flush()

        ph_b = contextlib.ExitStack()
        sbb = mksb(ph_b)
        qf_hi_b = sbb("qf_hi", [128, 4, CH], BF16)
        qf_lo_b = sbb("qf_lo", [128, 4, CH], BF16)
        qscr = sbb("qscr", [128, 512])
        esb_b = sbb("esb", [128, 2, 2, 512])
        vring_b = sbb("vring", [128, VRING, 2, 65])
        kring_b = sbb("kring", [128, KRING, 2, 128], BF16)
        attnT_b = sbb("attnT", [128, 4, 512])
        ystg_b = sbb("ystg", [128, 2, 512])
        wosb_b = sbb("wosb", [128, 8, 4, 128])
        cqbuf = sbb("cqbuf", [128, 2, 512])
        rs_ticks = {}

        ds_wo.inc(16)
        def _(s):
            s.dma_start(out=wosb_b[:], in_=wo.rearrange("a b p n -> p a b n")
                        ).then_inc(ds_wo.sem, 16)
            return None
        k.adma(_)
        t_wo = ds_wo.n

        cq_rd = [0]
        qs_tick = [0]

        def l0_get_q(c):
            qs_tick[0] = 0
            ds_cqb.inc(16)
            def _(s, c=c):
                s.dma_start(out=cqbuf[:],
                            in_=cqh[:, :, CH * c:CH * (c + 1)]
                            .rearrange("a p n -> p a n")
                            ).then_inc(ds_cqb.sem, 16)
                return None
            k.sy(_, waits=[(k.pe, cq_rd[0])])
            t_cql = ds_cqb.n
            for p in range(4):
                bi = p % 2
                tmm = mmgroup(bi, lambda te, bank, p=p: [
                    te.matmul(bank[:, 0:512], wuq_sb[:, p, di, :],
                              cqbuf[:, di, :], start=(di == 0),
                              stop=(di == 1)) for di in range(2)][-1],
                    [(ds_cqb, t_cql), (ds_wa, t_p1w)])
                t_s = k.v(lambda v, bi=bi: v.tensor_scalar(
                    qscr[:], pb[bi][:, 0:512], 0.125, None, MULT),
                    waits=[(k.pe, tmm)])
                t_h = k.v(lambda v, p=p: v.tensor_copy(
                    qf_hi_b[:, p, :], qscr[:]), waits=[(k.dv, t_s)])
                t_ev = k.v(lambda v, p=p: v.tensor_tensor(
                    qf_lo_b[:, p, :], qscr[:], qf_hi_b[:, p, :], SUB),
                    waits=[(k.dv, t_h)])
                mm_free[bi] = (k.dv, t_ev)
                t_sq = k.v(lambda v, p=p: v.tensor_tensor(
                    vts[:], qf_hi_b[:, p, :], qf_hi_b[:, p, :], MULT),
                    waits=[(k.dv, t_h), (k.pe, vts_rd[0])])
                tnm = mmgroup(6, lambda te, bank: te.matmul(
                    bank[0:1, 0:512], onescol[:], vts[:], start=True,
                    stop=True), [(k.dv, t_sq)])
                vts_rd[0] = tnm
                bank_evac(6, tnm, lambda v, p=p: v.tensor_reduce(
                    sc[0:1, 40 + p:41 + p], pb[6][0:1, 0:512], AXL, MAXOP))
                qs_tick[0] = max(qs_tick[0], t_ev)
            cq_rd[0] = k.pe.n
            return qs_tick[0], k.dv.n

        attention("l0", l0_get_q, None, None, (ds_wo, t_wo), rs0i, rs0o,
                  32, lambda p: 16 + p, False, [], qf_hi_b, qf_lo_b,
                  esb_b, attnT_b, ystg_b, wosb_b, vring_b, kring_b,
                  None, rs_ticks)

        k.barrier()
        ph_b.close()

        ph_c = contextlib.ExitStack()
        sbc = mksb(ph_c)
        lnThi = sbc("lnThi", [128, 8, CH], BF16)
        lnTlo = sbc("lnTlo", [128, 8, CH], BF16)
        uhi = sbc("uhi", [128, 32, CH], BF16)
        ulo = sbc("ulo", [128, 32, CH], BF16)
        ghi = sbc("ghi", [128, 32, CH], BF16)
        glo = sbc("glo", [128, 32, CH], BF16)
        fsc = sbc("fsc", [128, 512])
        qscr2 = sbc("qscr2", [128, 512])
        wring = sbc("wring", [128, 2, 32, 128], BF16)

        wr_free = [(k.pe, 0), (k.pe, 0)]
        fsc_rd = [0]
        ag_ticks = {}

        def stream_wcol(hbm_tiled, do, ndi, slot):
            ds_wr[slot].inc(16)
            def _(s, do=do, ndi=ndi, slot=slot):
                s.dma_start(out=wring[:, slot, 0:ndi, :],
                            in_=hbm_tiled[do].rearrange("b p n -> p b n")
                            ).then_inc(ds_wr[slot].sem, 16)
                return None
            k.adma(_, waits=[wr_free[slot], (ds_conv, t_conv)])
            return (ds_wr[slot], ds_wr[slot].n)

        for kk in range(4):
            ds_x.inc(16)
            def _(s, kk=kk):
                s.dma_start(out=xch[:],
                            in_=x_my[CH * kk:CH * (kk + 1), :]
                            .rearrange("(t p) d -> p t d", p=128)
                            ).then_inc(ds_x.sem, 16)
                return None
            k.sy(_, waits=[(k.col, rs_ticks[("l0", kk)]),
                           (k.dv, cell["x_free"]), cell["x_dma"]])
            t_xld = ds_x.n

            for dc in range(8):
                bi = 6 + (dc % 2)

                sl_ = dc % 2
                ds_stg[sl_].inc(16)
                def _(s, kk=kk, dc=dc, sl_=sl_):
                    s.dma_start(out=stg[:, sl_, :],
                                in_=rs0o[kk][128 * dc:128 * (dc + 1), :]
                                ).then_inc(ds_stg[sl_].sem, 16)
                    return None
                k.sy(_, waits=[(k.pe, stg_read[sl_])])
                t_sld = ds_stg[sl_].n

                def _t(te, bi=bi, sl_=sl_):
                    r = None
                    for tt in range(4):
                        r = te.transpose(pb[bi][:, tt * 128:(tt + 1) * 128],
                                         stg[:, sl_, tt * 128:(tt + 1) * 128],
                                         identsb[:])
                    return r
                tmm = k.t(_t, waits=[(ds_stg[sl_], t_sld), mm_free[bi]])
                stg_read[sl_] = tmm

                def _v(v, dc=dc, bi=bi):
                    r = None
                    for tt in range(4):
                        r = v.tensor_tensor(
                            xch[:, tt, 128 * dc:128 * (dc + 1)],
                            pb[bi][:, tt * 128:(tt + 1) * 128],
                            xch[:, tt, 128 * dc:128 * (dc + 1)], ADD)
                    return r
                bank_evac(bi, tmm, _v, extra=[(ds_x, t_xld)])
            t_x1 = k.dv.n

            t_stt = emit_ln_stats(xch, 4, [(k.dv, t_x1)])
            t_sp = emit_ln_fm(xch, 4, t_stt, lnThi, lnTlo, None)

            for do in range(32):
                sl = do % 2
                s1 = stream_wcol(w1b, do, 8, sl)
                tm1 = mmgroup(0, lambda te, bank, sl=sl: [
                    te.matmul(bank[:, 0:512], wring[:, sl, i % 8, :],
                              (lnThi if i < 8 else lnTlo)[:, i % 8, :],
                              start=(i == 0), stop=(i == 15))
                    for i in range(16)][-1], [(k.dv, t_sp), s1])
                wr_free[sl] = (k.pe, tm1)
                s2 = stream_wcol(w2b, do, 8, 1 - sl)
                tm2 = mmgroup(1, lambda te, bank, sl=sl: [
                    te.matmul(bank[:, 0:512], wring[:, 1 - sl, i % 8, :],
                              (lnThi if i < 8 else lnTlo)[:, i % 8, :],
                              start=(i == 0), stop=(i == 15))
                    for i in range(16)][-1], [(k.dv, t_sp), s2])
                wr_free[1 - sl] = (k.pe, tm2)
                t_si = k.a(lambda a, do=do: a.activation(
                    fsc[:], pb[0][:, 0:512], AF.Silu,
                    scale=sclsb[:, do:do + 1]),
                    waits=[(k.pe, tm1), (k.dv, fsc_rd[0])])
                mm_free[0] = (k.ac, t_si)

                t_u0 = k.v(lambda v, do=do: v.scalar_tensor_tensor(
                    qscr2[:], pb[1][:, 0:512], sclsb[:, 32 + do:33 + do],
                    fsc[:], MULT, MULT), waits=[(k.ac, t_si), (k.pe, tm2)])
                t_u1 = k.v(lambda v, do=do: v.tensor_copy(
                    uhi[:, do, :], qscr2[:]), waits=[(k.dv, t_u0)])
                t_u = k.v(lambda v, do=do: v.tensor_tensor(
                    ulo[:, do, :], qscr2[:], uhi[:, do, :], SUB),
                    waits=[(k.dv, t_u1)])
                mm_free[1] = (k.dv, t_u0)
                fsc_rd[0] = t_u0
                if do % 8 == 7:
                    k.flush()
            t_uall = k.dv.n

            for do in range(32):
                sl = do % 2
                s1 = stream_wcol(h1b, do, 32, sl)
                bi = 4 + (do % 2)
                tm = mmgroup(bi, lambda te, bank, sl=sl: [
                    te.matmul(bank[:, 0:512], wring[:, sl, i % 32, :],
                              (uhi if i < 32 else ulo)[:, i % 32, :],
                              start=(i == 0), stop=(i == 63))
                    for i in range(64)][-1], [(k.dv, t_uall), s1])
                wr_free[sl] = (k.pe, tm)
                t_ge = k.a(lambda a, do=do, bi=bi: a.activation(
                    fsc[:], pb[bi][:, 0:512], AF.Gelu,
                    scale=sclsb[:, 64 + do:65 + do]),
                    waits=[(k.pe, tm), (k.dv, fsc_rd[0])])
                mm_free[bi] = (k.ac, t_ge)

                t_g0 = k.v(lambda v, do=do: v.tensor_copy(
                    ghi[:, do, :], fsc[:]), waits=[(k.ac, t_ge)])
                fsc_rd[0] = k.v(lambda v, do=do: v.tensor_tensor(
                    glo[:, do, :], fsc[:], ghi[:, do, :], SUB),
                    waits=[(k.dv, t_g0)])
                if do % 8 == 7:
                    k.flush()
            t_gall = k.dv.n

            for do in range(32):
                sl = do % 2
                s1 = stream_wcol(h2b, do, 32, sl)
                bi = 4 + (do % 2)
                tm = mmgroup(bi, lambda te, bank, sl=sl: [
                    te.matmul(bank[:, 0:512], wring[:, sl, i % 32, :],
                              (ghi if i < 32 else glo)[:, i % 32, :],
                              start=(i == 0), stop=(i == 63))
                    for i in range(64)][-1], [(k.dv, t_gall), s1])
                wr_free[sl] = (k.pe, tm)
                t_ge = k.a(lambda a, do=do, bi=bi: a.activation(
                    fsc[:], pb[bi][:, 0:512], AF.Gelu,
                    scale=sclsb[:, 96 + do:97 + do]),
                    waits=[(k.pe, tm), (k.dv, fsc_rd[0])])
                mm_free[bi] = (k.ac, t_ge)

                t_g0 = k.v(lambda v, do=do: v.tensor_copy(
                    uhi[:, do, :], fsc[:]), waits=[(k.ac, t_ge)])
                fsc_rd[0] = k.v(lambda v, do=do: v.tensor_tensor(
                    ulo[:, do, :], fsc[:], uhi[:, do, :], SUB),
                    waits=[(k.dv, t_g0)])
                if do % 8 == 7:
                    k.flush()
            t_g2 = k.dv.n

            for do in range(8):
                sl = do % 2
                s1 = stream_wcol(w3b, do, 32, sl)
                bi = do % 2
                tm = mmgroup(bi, lambda te, bank, sl=sl: [
                    te.matmul(bank[:, 0:512], wring[:, sl, i % 32, :],
                              (uhi if i < 32 else ulo)[:, i % 32, :],
                              start=(i == 0), stop=(i == 63))
                    for i in range(64)][-1], [(k.dv, t_g2), s1])
                wr_free[sl] = (k.pe, tm)
                sl_ = do % 2
                t_cp = bank_evac(bi, tm, lambda a, do=do, bi=bi, sl_=sl_:
                    a.activation(
                        stg[:, sl_, :], pb[bi][:, 0:512], AF.Copy, bias=0.0,
                        scale=sclsb[:, 128 + do:129 + do]), eng="a",
                    extra=[(k.pe, stg_read[sl_])])

                bj = 6 + (do % 2)

                def _t(te, bj=bj, sl_=sl_):
                    r = None
                    for tt in range(4):
                        r = te.transpose(pb[bj][:, tt * 128:(tt + 1) * 128],
                                         stg[:, sl_, tt * 128:(tt + 1) * 128],
                                         identsb[:])
                    return r
                tmm = k.t(_t, waits=[(k.ac, t_cp), mm_free[bj]])
                stg_read[sl_] = tmm

                def _v(v, do=do, bj=bj):
                    r = None
                    for tt in range(4):
                        r = v.tensor_tensor(
                            xch[:, tt, 128 * do:128 * (do + 1)],
                            pb[bj][:, tt * 128:(tt + 1) * 128],
                            xch[:, tt, 128 * do:128 * (do + 1)], ADD)
                    return r
                bank_evac(bj, tmm, _v)
            t_x2 = k.dv.n

            ds_ag[kk % 2].inc(32)
            def _(s, kk=kk):
                s.dma_start(out=agi[kk].rearrange("(t p) d -> p t d", p=128),
                            in_=xch[:]).then_inc(ds_ag[kk % 2].sem, 16)
                s.dma_start(out=x2h[CH * kk:CH * (kk + 1), :]
                            .rearrange("(t p) d -> p t d", p=128),
                            in_=xch[:]).then_inc(ds_ag[kk % 2].sem, 16)
                return None
            k.sy(_, waits=[(k.dv, t_x2)])
            cell["x_free"] = t_x2
            cell["x_dma"] = (ds_ag[kk % 2], ds_ag[kk % 2].n)

            def _(g, kk=kk):
                r = g.collective_compute("AllGather", AluOpType.bypass,
                                         ins=[agi[kk][:]], outs=[ago[kk][:]],
                                         replica_groups=RG)
                r.then_inc(k.col.sem, 1)
                return None
            k.col.inc()
            k.g_(_, waits=[(ds_ag[kk % 2], ds_ag[kk % 2].n)])
            ag_ticks[kk] = k.col.n
            k.flush()

        k.t(_spin(220), waits=[(k.dv, k.dv.n), (k.ac, k.ac.n)])
        k.barrier()
        ph_c.close()

        ph_d = contextlib.ExitStack()
        sbd = mksb(ph_d)
        b0t = sbd("b0t", [128, HPC, 128])
        b1t = sbd("b1t", [128, HPC, 128])
        lnTd = sbd("lnTd", [128, 8, CH])
        kf2_hi = sbd("kf2_hi", [128, S], BF16)
        kf2_lo = sbd("kf2_lo", [128, S], BF16)
        v1sb = sbd("v1sb", [128, NKT, 65])
        wq1_sb = sbd("wq1_sb", [128, 4, 8, 128])
        wk1_sb = sbd("wk1_sb", [128, 8, 64])
        wv1_sb = sbd("wv1_sb", [128, 8, 64])
        qf_hi_d = sbd("qf_hi_d", [128, 4, CH], BF16)
        qf_lo_d = sbd("qf_lo_d", [128, 4, CH], BF16)
        qscr_d = sbd("qscr_d", [128, 512])
        esb_d = sbd("esb_d", [128, 2, 2, 512])
        attnT_d = sbd("attnT_d", [128, 4, 512])
        ystg_d = sbd("ystg_d", [128, 2, 512])
        wosb_d = sbd("wosb_d", [128, 8, 4, 128])

        ds_b01.inc(32 * HPC)
        def _(s):
            for h in range(HPC):
                s.dma_start(out=b0t[:, h, :],
                            in_=strh[h:h + 1, 0:16384]
                            .rearrange("o (p n) -> (o p) n", p=128)
                            ).then_inc(ds_b01.sem, 16)
                s.dma_start(out=b1t[:, h, :],
                            in_=strh[h:h + 1, 16384:32768]
                            .rearrange("o (p n) -> (o p) n", p=128)
                            ).then_inc(ds_b01.sem, 16)
            return None
        k.sy(_)
        t_b01 = ds_b01.n

        ds_w4.inc(64)
        def _(s):
            s.dma_start(out=wq1_sb[:], in_=wq1.rearrange("a b p n -> p a b n")
                        ).then_inc(ds_w4.sem, 16)
            s.dma_start(out=wk1_sb[:], in_=wk1.rearrange("b p n -> p b n")
                        ).then_inc(ds_w4.sem, 16)
            s.dma_start(out=wv1_sb[:], in_=wv1.rearrange("b p n -> p b n")
                        ).then_inc(ds_w4.sem, 16)
            s.dma_start(out=wosb_d[:], in_=wo1.rearrange("a b p n -> p a b n")
                        ).then_inc(ds_w4.sem, 16)
            return None
        k.adma(_)
        t_p4w = ds_w4.n

        def load_ago_chunk(j):
            g = j // 4
            kkk = j % 4

            ds_x.inc(16)
            def _(s, g=g, kkk=kkk):
                s.dma_start(out=xch[:], in_=ago[kkk][g]
                            .rearrange("(t p) d -> p t d", p=128)
                            ).then_inc(ds_x.sem, 16)
                return None
            k.sy(_, waits=[(k.col, ag_ticks[kkk]), (k.dv, cell["x_free"]),
                           cell["x_dma"]])
            return ds_x.n

        for j in range(NCH):
            t_ld = load_ago_chunk(j)
            t_stt = emit_ln_stats(xch, 4, [(ds_x, t_ld)])
            t_fm = emit_ln_fm(xch, 4, t_stt, None, None, lnTd)
            cell["x_free"] = t_fm
            cell["x_dma"] = None

            tmk = mmgroup(0, lambda te, bank: [
                te.matmul(bank[0:64, 0:512], wk1_sb[:, di, :], lnTd[:, di, :],
                          start=(di == 0), stop=(di == 7))
                for di in range(8)][-1], [(k.dv, t_fm), (ds_w4, t_p4w)])
            t_kh = k.v(lambda v, j=j: v.tensor_copy(
                kf2_hi[0:64, CH * j:CH * (j + 1)], pb[0][0:64, 0:512]),
                waits=[(k.pe, tmk)])
            t_ek = k.v(lambda v, j=j: v.tensor_tensor(
                kf2_lo[0:64, CH * j:CH * (j + 1)], pb[0][0:64, 0:512],
                kf2_hi[0:64, CH * j:CH * (j + 1)], SUB),
                waits=[(k.dv, t_kh)])
            mm_free[0] = (k.dv, t_ek)
            t_sq = k.v(lambda v, j=j: v.tensor_tensor(
                vts[0:64, :], kf2_hi[0:64, CH * j:CH * (j + 1)],
                kf2_hi[0:64, CH * j:CH * (j + 1)], MULT),
                waits=[(k.dv, t_kh), (k.pe, vts_rd[0])])
            tnm = mmgroup(6, lambda te, bank: te.matmul(
                bank[0:1, 0:512], onescol[0:64, :], vts[0:64, :], start=True,
                stop=True), [(k.dv, t_sq)])
            vts_rd[0] = tnm

            trd = bank_evac(6, tnm, lambda v: v.tensor_reduce(
                sc[0:1, 48:49], pb[6][0:1, 0:512], AXL, MAXOP),
                extra=[(k.dv, t_relu)])
            k.v(lambda v: v.tensor_tensor(
                sc[0:1, 20:21], sc[0:1, 20:21], sc[0:1, 48:49], MAXOP),
                waits=[(k.dv, trd)])

            tmv = mmgroup(1, lambda te, bank: [
                te.matmul(bank[0:64, 0:512], wv1_sb[:, di, :], lnTd[:, di, :],
                          start=(di == 0), stop=(di == 7))
                for di in range(8)][-1], [(k.dv, t_fm), (ds_w4, t_p4w)])
            t_ev = bank_evac(1, tmv, lambda v: v.tensor_copy(
                vts[0:64, :], pb[1][0:64, 0:512]))
            for tt in range(4):
                ttr = k.t(lambda te, tt=tt: te.transpose(
                    pb[7][:, 0:64], vts[0:64, tt * 128:(tt + 1) * 128],
                    identsb[0:64, 0:64]),
                    waits=[(k.dv, t_ev), mm_free[7]])
                bank_evac(7, ttr, lambda v, tt=tt, j=j: v.tensor_copy(
                    v1sb[:, 4 * j + tt, 0:64], pb[7][:, 0:64]))
            k.flush()

        t_v1o = k.v(lambda v: v.memset(v1sb[:, :, 64:65], 1.0))

        ds_dup.inc(32)
        def _(g):
            g.dma_start(out=kf2_hi[64:128, :], in_=kf2_hi[0:64, :]
                        ).then_inc(ds_dup.sem, 16)
            g.dma_start(out=kf2_lo[64:128, :], in_=kf2_lo[0:64, :]
                        ).then_inc(ds_dup.sem, 16)
            return None
        k.g_(_, waits=[(k.dv, k.dv.n)])
        t_dup = ds_dup.n

        def l1_get_q(c):
            qs_tick[0] = 0
            t_ld = load_ago_chunk(c)
            t_stt = emit_ln_stats(xch, 4, [(ds_x, t_ld)])
            t_fm = emit_ln_fm(xch, 4, t_stt, None, None, lnTd)
            cell["x_free"] = t_fm
            for p in range(4):
                bi = p % 2
                tmm = mmgroup(bi, lambda te, bank, p=p: [
                    te.matmul(bank[:, 0:512], wq1_sb[:, p, di, :],
                              lnTd[:, di, :], start=(di == 0), stop=(di == 7))
                    for di in range(8)][-1], [(k.dv, t_fm), (ds_w4, t_p4w)])
                t_s = k.v(lambda v, bi=bi: v.tensor_scalar(
                    qscr_d[:], pb[bi][:, 0:512], 0.125, None, MULT),
                    waits=[(k.pe, tmm)])
                t_h = k.v(lambda v, p=p: v.tensor_copy(
                    qf_hi_d[:, p, :], qscr_d[:]), waits=[(k.dv, t_s)])
                t_ev = k.v(lambda v, p=p: v.tensor_tensor(
                    qf_lo_d[:, p, :], qscr_d[:], qf_hi_d[:, p, :], SUB),
                    waits=[(k.dv, t_h)])
                mm_free[bi] = (k.dv, t_ev)
                t_sq = k.v(lambda v, p=p: v.tensor_tensor(
                    vts[:], qf_hi_d[:, p, :], qf_hi_d[:, p, :], MULT),
                    waits=[(k.dv, t_h), (k.pe, vts_rd[0])])
                tnm = mmgroup(6, lambda te, bank: te.matmul(
                    bank[0:1, 0:512], onescol[:], vts[:], start=True,
                    stop=True), [(k.dv, t_sq)])
                vts_rd[0] = tnm
                bank_evac(6, tnm, lambda v, p=p: v.tensor_reduce(
                    sc[0:1, 40 + p:41 + p], pb[6][0:1, 0:512], AXL, MAXOP))
                qs_tick[0] = max(qs_tick[0], t_ev)
            return qs_tick[0], k.dv.n

        attention("l1", l1_get_q, (kf2_hi, kf2_lo), (v1sb, t_v1o),
                  (ds_w4, t_p4w), rs1i, rs1o, 8, lambda p: 20, True,
                  [(ds_dup, t_dup)], qf_hi_d, qf_lo_d, esb_d,
                  attnT_d, ystg_d, wosb_d, None, None, (ds_b01, t_b01),
                  rs_ticks)

        k.t(_spin(260), waits=[(k.dv, k.dv.n), (k.ac, k.ac.n)])
        k.barrier()
        ph_d.close()

        ph_e = contextlib.ExitStack()
        sbe = mksb(ph_e)
        lnTe = sbe("lnTe", [128, 8, CH])
        h1T = sbe("h1T", [128, 8, CH])
        h2T = sbe("h2T", [128, 8, CH])
        wf1_sb = sbe("wf1_sb", [128, 8, 8, 128])
        wf2_sb = sbe("wf2_sb", [128, 8, 8, 128])
        wf3_sb = sbe("wf3_sb", [128, 8, 8, 128])

        ds_w5.inc(48)
        def _(s):
            for dst, src in ((wf1_sb, wf1), (wf2_sb, wf2), (wf3_sb, wf3)):
                s.dma_start(out=dst[:], in_=src.rearrange("a b p n -> p a b n")
                            ).then_inc(ds_w5.sem, 16)
            return None
        k.adma(_)
        t_p5w = ds_w5.n

        for kk in range(4):
            ds_x.inc(16)
            def _(s, kk=kk):
                s.dma_start(out=xch[:], in_=x2h[CH * kk:CH * (kk + 1), :]
                            .rearrange("(t p) d -> p t d", p=128)
                            ).then_inc(ds_x.sem, 16)
                return None
            k.sy(_, waits=[(k.col, rs_ticks[("l1", kk)]),
                           (k.dv, cell["x_free"]), cell["x_dma"]])
            t_xld = ds_x.n
            cell["x_dma"] = None

            for dc in range(8):
                bi = 6 + (dc % 2)

                sl_ = dc % 2
                ds_stg[sl_].inc(16)
                def _(s, kk=kk, dc=dc, sl_=sl_):
                    s.dma_start(out=stg[:, sl_, :],
                                in_=rs1o[kk][128 * dc:128 * (dc + 1), :]
                                ).then_inc(ds_stg[sl_].sem, 16)
                    return None
                k.sy(_, waits=[(k.pe, stg_read[sl_]), (k.dv, stg_bo[sl_])])
                t_sld = ds_stg[sl_].n
                t_bo = k.v(lambda v, dc=dc, sl_=sl_: v.tensor_scalar(
                    stg[:, sl_, :], stg[:, sl_, :], boutsb[:, dc:dc + 1],
                    None, ADD), waits=[(ds_stg[sl_], t_sld)])
                stg_bo[sl_] = t_bo

                def _t(te, bi=bi, sl_=sl_):
                    r = None
                    for tt in range(4):
                        r = te.transpose(pb[bi][:, tt * 128:(tt + 1) * 128],
                                         stg[:, sl_, tt * 128:(tt + 1) * 128],
                                         identsb[:])
                    return r
                tmm = k.t(_t, waits=[(k.dv, t_bo), mm_free[bi]])
                stg_read[sl_] = tmm

                def _v(v, dc=dc, bi=bi):
                    r = None
                    for tt in range(4):
                        r = v.tensor_tensor(
                            xch[:, tt, 128 * dc:128 * (dc + 1)],
                            pb[bi][:, tt * 128:(tt + 1) * 128],
                            xch[:, tt, 128 * dc:128 * (dc + 1)], ADD)
                    return r
                bank_evac(bi, tmm, _v, extra=[(ds_x, t_xld)])
            t_x3 = k.dv.n

            t_stt = emit_ln_stats(xch, 4, [(k.dv, t_x3)])
            t_fm = emit_ln_fm(xch, 4, t_stt, None, None, lnTe)

            def chain(win, xin, xout, wait_in, t_w):
                t_last = 0
                for do in range(8):
                    bi = do % 2
                    tmm = mmgroup(bi, lambda te, bank, do=do: [
                        te.matmul(bank[:, 0:512], win[:, do, di, :],
                                  xin[:, di, :], start=(di == 0),
                                  stop=(di == 7)) for di in range(8)][-1],
                        [wait_in, (ds_w5, t_w)])
                    t_last = bank_evac(bi, tmm, lambda v, do=do, bi=bi:
                                       v.tensor_copy(xout[:, do, :],
                                                     pb[bi][:, 0:512]))
                return t_last
            t_h1 = chain(wf1_sb, lnTe, h1T, (k.dv, t_fm), t_p5w)
            t_h2 = chain(wf2_sb, h1T, h2T, (k.dv, t_h1), t_p5w)

            for do in range(8):
                bi = do % 2
                tmm = mmgroup(bi, lambda te, bank, do=do: [
                    te.matmul(bank[:, 0:512], wf3_sb[:, do, di, :],
                              h2T[:, di, :], start=(di == 0), stop=(di == 7))
                    for di in range(8)][-1], [(k.dv, t_h2),
                                              (ds_w5, t_p5w)])
                sl_ = do % 2
                t_y = bank_evac(bi, tmm, lambda a, do=do, bi=bi, sl_=sl_:
                    a.activation(
                        stg[:, sl_, :], pb[bi][:, 0:512], AF.Tanh,
                        bias=bias1sb[:, do:do + 1], scale=1.0), eng="a",
                    extra=[(k.pe, stg_read[sl_]), (k.dv, stg_bo[sl_])])

                bj = 6 + (do % 2)

                def _t(te, bj=bj, sl_=sl_):
                    r = None
                    for tt in range(4):
                        r = te.transpose(pb[bj][:, tt * 128:(tt + 1) * 128],
                                         stg[:, sl_, tt * 128:(tt + 1) * 128],
                                         identsb[:])
                    return r
                tmm2 = k.t(_t, waits=[(k.ac, t_y), mm_free[bj]])
                stg_read[sl_] = tmm2

                def _v(v, do=do, bj=bj):
                    r = None
                    for tt in range(4):
                        r = v.tensor_tensor(
                            xch[:, tt, 128 * do:128 * (do + 1)],
                            pb[bj][:, tt * 128:(tt + 1) * 128],
                            xch[:, tt, 128 * do:128 * (do + 1)], ADD)
                    return r
                bank_evac(bj, tmm2, _v)
            t_out = k.dv.n

            k.dout.inc(16)
            def _(s, kk=kk):
                s.dma_start(out=y_out[CH * kk:CH * (kk + 1), :]
                            .rearrange("(t p) d -> p t d", p=128),
                            in_=xch[:]).then_inc(k.dout.sem, 16)
                return None
            k.sy(_, waits=[(k.dv, t_out)])
            out_tick = k.dout.n
            cell["x_free"] = k.v(lambda v: None, waits=[(k.dout, out_tick)])
            k.flush()

        ph_e.close()

        def _(s):
            s.wait_ge(k.dout.sem, k.dout.n)
            return None
        k.sy(_)
        k.flush()

    return nc, tap_outs


_prog_cache = {}


def _get_prog(taps=()):
    key = tuple(sorted(taps))
    if key not in _prog_cache:
        _prog_cache[key] = build_program(taps=key)
    return _prog_cache[key]


def make_in_maps(inputs):
    f32c = lambda a: np.ascontiguousarray(np.asarray(a), dtype=np.float32)
    x = f32c(inputs["L0_attn_x"])
    W_dq = f32c(inputs["L0_attn_W_dq"])
    W_uq = f32c(inputs["L0_attn_W_uq"])
    W_dkv = f32c(inputs["L0_attn_W_dkv"])
    W_uk = f32c(inputs["L0_attn_W_uk"])
    W_uv = f32c(inputs["L0_attn_W_uv"])
    W_o = f32c(inputs["L0_attn_W_o"])
    scl = np.zeros((128, 136), dtype=np.float32)
    scl[:, 0:32] = f32c(inputs["L0_ffn_w1_s"]).reshape(32, 128).T
    scl[:, 32:64] = f32c(inputs["L0_ffn_w2_s"]).reshape(32, 128).T
    scl[:, 64:96] = f32c(inputs["L0_ffn_h1_s"]).reshape(32, 128).T
    scl[:, 96:128] = f32c(inputs["L0_ffn_h2_s"]).reshape(32, 128).T
    scl[:, 128:136] = f32c(inputs["L0_ffn_w3_s"]).reshape(8, 128).T
    Wq1 = f32c(inputs["L1_attn_Wq"])
    Wkv1 = f32c(inputs["L1_attn_Wkv"])
    Wout1 = f32c(inputs["L1_attn_Wout"])
    bout = f32c(inputs["L1_attn_bout"]).reshape(8, 128).T
    rel_emb = f32c(inputs["L1_attn_rel_emb"])
    bias = f32c(inputs["L1_ffn_bias"]).reshape(8, 128).T

    shared = {
        "ident": np.eye(128, dtype=np.float32),
        "onehot": _build_onehot(),
        "wdq": _tile_w(W_dq), "wdkv": _tile_w(W_dkv),
        "w1f": _tile_w(np.ascontiguousarray(f32c(inputs["L0_ffn_w1_w"]).T)),
        "w2f": _tile_w(np.ascontiguousarray(f32c(inputs["L0_ffn_w2_w"]).T)),
        "h1f": _tile_w(np.ascontiguousarray(f32c(inputs["L0_ffn_h1_w"]).T)),
        "h2f": _tile_w(np.ascontiguousarray(f32c(inputs["L0_ffn_h2_w"]).T)),
        "w3f": _tile_w(np.ascontiguousarray(f32c(inputs["L0_ffn_w3_w"]).T)),
        "scl": scl, "bout1": bout, "bias1": bias,
        "wf1": _tile_w(f32c(inputs["L1_ffn_weight"])),
        "wf2": _tile_w(f32c(inputs["L1_ffn_aux_weight1"])),
        "wf3": _tile_w(f32c(inputs["L1_ffn_aux_weight2"])),
    }
    in_maps = []
    for c in range(8):
        b, h = c // 2, c % 2
        hs = slice(512 * h, 512 * (h + 1))
        m = dict(shared)
        m["x_in"] = np.ascontiguousarray(x[b])
        m["x_my"] = np.ascontiguousarray(x[b, TOK * h:TOK * (h + 1)])
        m["wuq"] = _tile_w(np.ascontiguousarray(W_uq[:, hs]))
        m["wuk"] = _tile_w(np.ascontiguousarray(W_uk[:, hs]))
        m["wuv"] = _tile_w(np.ascontiguousarray(W_uv[:, hs]))
        m["wo"] = _tile_w(np.ascontiguousarray(W_o[hs, :]))
        m["wq1"] = _tile_w(np.ascontiguousarray(Wq1[:, hs]))
        m["wk1"] = _tile_w(np.ascontiguousarray(Wkv1[:, 0:64]))[0]
        m["wv1"] = _tile_w(np.ascontiguousarray(Wkv1[:, 64:128]))[0]
        m["wo1"] = _tile_w(np.ascontiguousarray(Wout1[hs, :]))
        m["rel"] = np.ascontiguousarray(rel_emb[:, 8 * h:8 * (h + 1)])
        in_maps.append(m)
    return in_maps


def kernel(**inputs):
    nc, _ = _get_prog()
    in_maps = make_in_maps(inputs)
    res = run_bass_kernel_spmd(nc, in_maps, core_ids=list(range(8)))
    out = np.zeros((4, S, D), dtype=np.float32)
    for c in range(8):
        b, h = c // 2, c % 2
        out[b, TOK * h:TOK * (h + 1)] = res.results[c]["y_out"]
    return out


# version 10
# speedup vs baseline: 1.0604x; 1.0248x over previous
import contextlib
import math

import numpy as np

import concourse.bass as bass
import concourse.mybir as mybir
from concourse.alu_op_type import AluOpType
from concourse.bass_utils import run_bass_kernel_spmd

F32 = mybir.dt.float32
BF16 = mybir.dt.bfloat16
AF = mybir.ActivationFunctionType
ADD = AluOpType.add
SUB = AluOpType.subtract
MULT = AluOpType.mult
MAXOP = AluOpType.max
AXL = mybir.AxisListType.X

S = 4096
D = 1024
TOK = 2048
NCH = 8
CH = 512
HPC = 8
NKT = 32
RELB = 32
RG = [[0, 1], [2, 3], [4, 5], [6, 7]]
VRING = 4
KRING = 4
CHORD = [0, 4, 1, 5, 2, 6, 3, 7]


def _tile_w(w):
    din, dout = w.shape
    dt_ = min(128, dout)
    ndi, ndo = din // 128, dout // dt_
    return np.ascontiguousarray(
        w.reshape(ndi, 128, ndo, dt_).transpose(2, 0, 1, 3))


def _rel_bucket(d):
    n = max(int(d), 0)
    me = RELB // 2
    if n < me:
        return n
    v = me + int(math.log(max(n, 1) / me) / math.log(128 / me) * (RELB - me))
    return min(v, RELB - 1)


def _build_onehot():
    n = 2 * 16384 + 2
    oh = np.zeros((RELB, n), dtype=np.float32)
    for t in range(2):
        for kr in range(128):
            base = t * 16384 + kr * 128
            for qc in range(128):
                d = 128 * t + qc - kr
                if d < 0:
                    continue
                oh[_rel_bucket(d), base + qc] += 8.0
                oh[31, base + qc] -= 8.0
    oh[31, n - 2] = 8.0
    return oh


class Ctr:
    __slots__ = ("sem", "n", "name")

    def __init__(self, sem, name):
        self.sem = sem
        self.n = 0
        self.name = name

    def inc(self, k=1):
        self.n += k
        return self.n


class K:
    ENGS = ("tensor", "vector", "scalar", "sync", "gpsimd")

    def __init__(self, nc, block, es):
        self.nc = nc
        self.block = block
        self.es = es
        self.hw = {e: {} for e in self.ENGS}
        self.bar = []
        self.allc = []
        self.pend = []
        for n in ("pe", "dv", "ac", "gp", "col", "dout"):
            c = Ctr(es.enter_context(nc.semaphore("s_" + n)), n)
            setattr(self, n, c)
            self.allc.append(c)

    def ds(self, name):
        c = Ctr(self.es.enter_context(self.nc.semaphore("d_" + name)), name)
        self.allc.append(c)
        return c

    def barrier(self):
        self.flush()
        self.bar = [(c, c.n) for c in self.allc]

    def _sect(self, eng, waits, fn, tick):
        allw = list(self.bar) + list(waits or ())
        if tick is not None and eng in ("vector", "scalar", "gpsimd") \
                and tick.n > 0:
            allw.append((tick, tick.n))
        tv = tick.inc() if tick is not None else None
        self.pend.append((eng, allw, fn, tick))
        return tv

    def flush(self):
        pend = self.pend
        self.pend = []
        for eng in self.ENGS:
            items = [it for it in pend if it[0] == eng]
            if not items:
                continue
            hw = self.hw[eng]

            def body(e, items=items, hw=hw, eng=eng):
                for _, allw, fn, tick in items:
                    for w in allw:
                        if w is None:
                            continue
                        c, v = w
                        if v is None or v <= 0 or hw.get(c.name, 0) >= v:
                            continue
                        e.wait_ge(c.sem, v)
                        hw[c.name] = v
                    last = fn(e)
                    if tick is not None:
                        if last is None:
                            assert eng in ("gpsimd", "vector", "scalar"), eng
                            last = e.nop()
                        last.then_inc(tick.sem, 1)
            getattr(self.block, eng)(body)

    def t(self, fn, waits=None):
        return self._sect("tensor", waits, fn, self.pe)

    def v(self, fn, waits=None):
        return self._sect("vector", waits, fn, self.dv)

    def a(self, fn, waits=None):
        return self._sect("scalar", waits, fn, self.ac)

    def g(self, fn, waits=None):
        return self._sect("gpsimd", waits, fn, self.gp)

    def g_(self, fn, waits=None):
        return self._sect("gpsimd", waits, fn, None)

    def sy(self, fn, waits=None):
        return self._sect("sync", waits, fn, None)

    def adma(self, fn, waits=None):
        return self._sect("sync", waits, fn, None)


def build_program(taps=()):
    nc = bass.Bass("TRN2", target_bir_lowering=False)
    dt = nc.dram_tensor
    tap_outs = {}

    x_in = dt("x_in", [S, D], F32, kind="ExternalInput").ap()
    x_my = dt("x_my", [TOK, D], F32, kind="ExternalInput").ap()
    ident = dt("ident", [128, 128], F32, kind="ExternalInput").ap()
    onehot = dt("onehot", [RELB, 2 * 16384 + 2], F32, kind="ExternalInput").ap()
    wdq = dt("wdq", [2, 8, 128, 128], F32, kind="ExternalInput").ap()
    wuq = dt("wuq", [4, 2, 128, 128], F32, kind="ExternalInput").ap()
    wdkv = dt("wdkv", [2, 8, 128, 128], F32, kind="ExternalInput").ap()
    wuk = dt("wuk", [4, 2, 128, 128], F32, kind="ExternalInput").ap()
    wuv = dt("wuv", [4, 2, 128, 128], F32, kind="ExternalInput").ap()
    wo = dt("wo", [8, 4, 128, 128], F32, kind="ExternalInput").ap()
    w1f = dt("w1f", [32, 8, 128, 128], F32, kind="ExternalInput").ap()
    w2f = dt("w2f", [32, 8, 128, 128], F32, kind="ExternalInput").ap()
    h1f = dt("h1f", [32, 32, 128, 128], F32, kind="ExternalInput").ap()
    h2f = dt("h2f", [32, 32, 128, 128], F32, kind="ExternalInput").ap()
    w3f = dt("w3f", [8, 32, 128, 128], F32, kind="ExternalInput").ap()
    scl = dt("scl", [128, 136], F32, kind="ExternalInput").ap()
    wq1 = dt("wq1", [4, 8, 128, 128], F32, kind="ExternalInput").ap()
    wk1 = dt("wk1", [8, 128, 64], F32, kind="ExternalInput").ap()
    wv1 = dt("wv1", [8, 128, 64], F32, kind="ExternalInput").ap()
    wo1 = dt("wo1", [8, 4, 128, 128], F32, kind="ExternalInput").ap()
    bout1 = dt("bout1", [128, 8], F32, kind="ExternalInput").ap()
    rel = dt("rel", [RELB, HPC], F32, kind="ExternalInput").ap()
    wf1 = dt("wf1", [8, 8, 128, 128], F32, kind="ExternalInput").ap()
    wf2 = dt("wf2", [8, 8, 128, 128], F32, kind="ExternalInput").ap()
    wf3 = dt("wf3", [8, 8, 128, 128], F32, kind="ExternalInput").ap()
    bias1 = dt("bias1", [128, 8], F32, kind="ExternalInput").ap()
    y_out = dt("y_out", [TOK, D], F32, kind="ExternalOutput").ap()

    w1b = dt("w1b", [32, 8, 128, 128], BF16, kind="Internal").ap()
    w2b = dt("w2b", [32, 8, 128, 128], BF16, kind="Internal").ap()
    h1b = dt("h1b", [32, 32, 128, 128], BF16, kind="Internal").ap()
    h2b = dt("h2b", [32, 32, 128, 128], BF16, kind="Internal").ap()
    w3b = dt("w3b", [8, 32, 128, 128], BF16, kind="Internal").ap()
    v0h = dt("v0h", [NKT, 128, HPC, 65], F32, kind="Internal").ap()
    kfh = dt("kfh", [NKT, 128, 4, 2, 128], BF16, kind="Internal").ap()
    cqh = dt("cqh", [2, 128, S], F32, kind="Internal").ap()
    strh = dt("strh", [HPC, 2 * 16384 + 2], F32, kind="Internal").ap()
    x2h = dt("x2h", [TOK, D], F32, kind="Internal").ap()
    rs0i = [dt(f"rs0i{j}", [2, D, CH], F32, kind="Internal").ap() for j in range(4)]
    rs0o = [dt(f"rs0o{j}", [D, CH], F32, kind="Internal").ap() for j in range(4)]
    agi = [dt(f"agi{j}", [CH, D], F32, kind="Internal").ap() for j in range(4)]
    ago = [dt(f"ago{j}", [2, CH, D], F32, kind="Internal").ap() for j in range(4)]
    rs1i = [dt(f"rs1i{j}", [2, D, CH], F32, kind="Internal").ap() for j in range(4)]
    rs1o = [dt(f"rs1o{j}", [D, CH], F32, kind="Internal").ap() for j in range(4)]

    es = contextlib.ExitStack()

    def mksb(stack):
        def sbuf(name, shape, d=F32):
            return stack.enter_context(nc.sbuf_tensor(name, list(shape), d))
        return sbuf

    with nc.Block() as block, es:
        sbuf = mksb(es)
        k = K(nc, block, es)
        pb = [es.enter_context(nc.psum_tensor(f"pb{i}", [128, 512], F32))
              for i in range(8)]
        mm_free = [(k.dv, 0) for _ in range(8)]
        ds_consts = k.ds("consts")
        ds_conv = k.ds("conv")
        ds_oh = [k.ds(f"oh{i}") for i in range(2)]
        ds_st = [k.ds(f"st{i}") for i in range(2)]
        ds_str = k.ds("str")
        ds_scg = k.ds("scg")
        ds_x = k.ds("x")
        ds_cq = k.ds("cq")
        ds_ks = k.ds("ks")
        ds_vst = k.ds("vst")
        ds_wa = k.ds("wa")
        ds_wo = k.ds("wo")
        ds_cqb = k.ds("cqb")
        ds_vr = [k.ds(f"vr{i}") for i in range(VRING)]
        ds_kr = [k.ds(f"kr{i}") for i in range(KRING)]
        ds_rs = [k.ds(f"rs{i}") for i in range(2)]
        ds_ag = [k.ds(f"ag{i}") for i in range(2)]
        ds_stg = [k.ds("stg0"), k.ds("stg1")]
        ds_wr = [k.ds(f"wr{i}") for i in range(2)]
        ds_b01 = k.ds("b01")
        ds_w4 = k.ds("w4")
        ds_dup = k.ds("dup")
        ds_w5 = k.ds("w5")
        ds_o2 = k.ds("o2")

        identsb = sbuf("identsb", [128, 128])
        sc = sbuf("sc", [128, 64])
        onescol = sbuf("onescol", [128, 1])
        onesrow = sbuf("onesrow", [1, 128])
        epsv = sbuf("epsv", [128, 1])
        zerov = sbuf("zerov", [1, 1])
        bvec = sbuf("bvec", [128, 8])
        boutsb = sbuf("boutsb", [128, 8])
        bias1sb = sbuf("bias1sb", [128, 8])
        sclsb = sbuf("sclsb", [128, 136])
        lnws = sbuf("lnws", [128, 128])
        relsb = sbuf("relsb", [RELB, HPC])
        maxcsb = sbuf("maxcsb", [HPC, 1])
        xch = sbuf("xch", [128, 4, D])
        vts = sbuf("vts", [128, 512])
        stg = sbuf("stg", [128, 2, 512])
        stg64 = sbuf("stg64", [128, 512])
        rcrow = sbuf("rcrow", [128, 2, 512])
        rbs = sbuf("rbs", [128, 512])
        onesr64 = sbuf("onesr64", [128, 64])
        masks = sbuf("masks", [128, 4, 512], BF16)
        wuq_sb = sbuf("wuq_sb", [128, 4, 2, 128])

        stg_read = [0, 0]
        stg_bo = [0, 0]
        vts_rd = [0]
        cell = {"x_free": 0, "x_dma": None}

        ds_consts.inc(80)
        def _(s):
            s.dma_start(out=identsb[:], in_=ident[:]).then_inc(ds_consts.sem, 16)
            s.dma_start(out=boutsb[:], in_=bout1[:]).then_inc(ds_consts.sem, 16)
            s.dma_start(out=bias1sb[:], in_=bias1[:]).then_inc(ds_consts.sem, 16)
            s.dma_start(out=sclsb[:], in_=scl[:]).then_inc(ds_consts.sem, 16)
            s.dma_start(out=relsb[:], in_=rel[:]).then_inc(ds_consts.sem, 16)
            return None
        k.sy(_)
        ld_consts = ds_consts.n

        def _spin(n_mm):
            def _(te):
                r = None
                for j in range(n_mm):
                    r = te.matmul(pb[0][:, (j % 4) * 128:(j % 4) * 128 + 128],
                                  identsb[:], identsb[:], start=True,
                                  stop=True)
                return r
            return _

        k.t(_spin(500), waits=[(ds_consts, ld_consts)])

        ds_conv.inc(80)
        def _(g):
            for src, dst in ((w1f, w1b), (w2f, w2b), (h1f, h1b), (h2f, h2b),
                             (w3f, w3b)):
                g.dma_start(out=dst[:], in_=src[:]).then_inc(ds_conv.sem, 16)
            return None
        k.g_(_)
        t_conv = ds_conv.n

        def _(v):
            v.memset(onescol[:], 1.0)
            v.memset(onesrow[:], 1.0)
            v.memset(onesr64[64:65, :], 1.0)
            v.memset(epsv[:], 1e-5)
            v.memset(zerov[:], 0.0)
            return v.memset(sc[:], 0.0)
        t_mem = k.v(_)

        def _(g):
            r = None
            for m in range(4):
                r = g.memset(masks[:, m, :], 1.0)
            return r
        t_msk0 = k.g(_)

        def _(g):
            r = None
            for m in range(4):
                r = g.affine_select(masks[:, m, :], masks[:, m, :],
                                    [[1, 512]], AluOpType.is_ge, 0.0,
                                    base=-128 * m, channel_multiplier=-1)
            return r
        t_masks = k.g(_, waits=[(k.gp, t_msk0)])

        NCOLS = 2 * 16384 + 2
        t_relu = 0

        def emit_ln_stats(xin, nt, wait_list, ws=0):
            def _(v):
                r = None
                for tt in range(nt):
                    o = ws + 16 * tt
                    v.bn_stats(lnws[:, o:o + 6], xin[:, tt, 0:512])
                    r = v.bn_stats(lnws[:, o + 6:o + 12], xin[:, tt, 512:1024])
                return r
            t1 = k.v(_, waits=wait_list)

            def _(v):
                r = None
                for tt in range(nt):
                    o = ws + 16 * tt
                    r = v.bn_aggr(lnws[:, o + 12:o + 14], lnws[:, o:o + 12])
                return r
            t2 = k.v(_, waits=[(k.dv, t1)])

            def _(v):
                r = None
                for tt in range(nt):
                    o = ws + 16 * tt
                    r = v.tensor_scalar(lnws[:, o + 14:o + 15],
                                        lnws[:, o + 12:o + 13], -1.0, None,
                                        MULT)
                return r
            t3 = k.v(_, waits=[(k.dv, t2)])

            def _(a):
                r = None
                for tt in range(nt):
                    o = ws + 16 * tt
                    r = a.activation(lnws[:, o + 15:o + 16],
                                     lnws[:, o + 13:o + 14], AF.Sqrt,
                                     bias=epsv[:], scale=1.0)
                return r
            t4 = k.a(_, waits=[(k.dv, t3)])

            def _(v):
                r = None
                for tt in range(nt):
                    o = ws + 16 * tt
                    r = v.reciprocal(lnws[:, o + 15:o + 16],
                                     lnws[:, o + 15:o + 16])
                return r
            return k.v(_, waits=[(k.ac, t4)])

        def emit_ln_fm(xin, nt, t_stats, dst_hi, dst_lo, dst_f32, ndc=8,
                       ws=0):
            last = 0
            for dc in range(ndc):
                sl_ = dc % 2

                def _sl(v, dc=dc, sl_=sl_):
                    r = None
                    for tt in range(nt):
                        o = ws + 16 * tt
                        r = v.tensor_scalar(
                            stg[:, sl_, 128 * tt:128 * (tt + 1)],
                            xin[:, tt, 128 * dc:128 * (dc + 1)],
                            lnws[:, o + 14:o + 15], lnws[:, o + 15:o + 16],
                            ADD, op1=MULT)
                    return r
                bi = 6 + (dc % 2)
                tsl = k.v(_sl, waits=[(k.dv, t_stats), (k.pe, stg_read[sl_])])

                def _t(te, bi=bi, sl_=sl_):
                    r = None
                    for tt in range(nt):
                        r = te.transpose(pb[bi][:, tt * 128:(tt + 1) * 128],
                                         stg[:, sl_, 128 * tt:128 * (tt + 1)],
                                         identsb[:])
                    return r
                tmm = k.t(_t, waits=[(k.dv, tsl), mm_free[bi]])
                stg_read[sl_] = tmm

                if dst_f32 is not None:
                    t = k.v(lambda v, dc=dc, bi=bi: v.tensor_copy(
                        dst_f32[:, dc, 0:nt * 128], pb[bi][:, 0:nt * 128]),
                        waits=[(k.pe, tmm)])
                else:
                    t0 = k.v(lambda v, dc=dc, bi=bi: v.tensor_copy(
                        dst_hi[:, dc, 0:nt * 128], pb[bi][:, 0:nt * 128]),
                        waits=[(k.pe, tmm)])
                    t = k.v(lambda v, dc=dc, bi=bi: v.tensor_tensor(
                        dst_lo[:, dc, 0:nt * 128], pb[bi][:, 0:nt * 128],
                        dst_hi[:, dc, 0:nt * 128], SUB),
                        waits=[(k.dv, t0)])
                mm_free[bi] = (k.dv, t)
                last = t
            return last

        def mmgroup(bi, emit, wait_list):
            return k.t(lambda te: emit(te, pb[bi]),
                       waits=list(wait_list) + [mm_free[bi]])

        def bank_evac(bi, t_pe, fn, eng="v", extra=None):
            f = {"v": k.v, "a": k.a}[eng]
            w = [(k.pe, t_pe)]
            if extra:
                w += list(extra)
            t = f(fn, waits=w)
            mm_free[bi] = ({"v": k.dv, "a": k.ac}[eng], t)
            return t

        ph_a = contextlib.ExitStack()
        sba = mksb(ph_a)
        lnTa = sba("lnTa", [128, 8, CH])
        lnTa2 = sba("lnTa2", [128, 8, CH])
        xch2 = sba("xch2", [128, 4, D])
        ohtile = sba("ohtile", [RELB, 2, 2048])
        sttile = sba("sttile", [HPC, 2, 2048])
        ckvT = sba("ckvT", [128, 2, CH])
        cqst = sba("cqst", [128, 2, CH])
        ksst = sba("ksst", [128, 4, 2, CH], BF16)
        vstage = sba("vstage", [128, 4, HPC, 65])
        wdq_sb = sba("wdq_sb", [128, 2, 8, 128])
        wdkv_sb = sba("wdkv_sb", [128, 2, 8, 128])
        wuk_sb = sba("wuk_sb", [128, 4, 2, 128])
        wuv_sb = sba("wuv_sb", [128, 4, 2, 128])


        BL = 2048
        strip_blocks = [(i, c0, min(BL, NCOLS - c0))
                        for i, c0 in enumerate(range(0, NCOLS, BL))]
        strip_state = {"oh_mm": {}, "st_wr": {}, "tev": 0, "idx": 0}

        def strip_round():
            if strip_state["idx"] >= len(strip_blocks):
                return
            i, c0, n = strip_blocks[strip_state["idx"]]
            strip_state["idx"] += 1
            oh_mm = strip_state["oh_mm"]
            st_wr = strip_state["st_wr"]

            ds_oh[i % 2].inc(16)
            def _(s, i=i, c0=c0, n=n):
                s.dma_start(out=ohtile[:, i % 2, 0:n],
                            in_=onehot[:, c0:c0 + n]
                            ).then_inc(ds_oh[i % 2].sem, 16)
                return None
            k.sy(_, waits=[(k.pe, oh_mm.get(i - 2, 0))])
            t_oh = ds_oh[i % 2].n
            for b0 in range(0, n, 512):
                nb = min(512, n - b0)
                tmm = k.t(lambda te, i=i, b0=b0, nb=nb: te.matmul(
                    pb[i % 2][0:HPC, 0:nb], relsb[:],
                    ohtile[:, i % 2, b0:b0 + nb], start=True, stop=True),
                    waits=[(ds_oh[i % 2], t_oh), mm_free[i % 2],
                           (ds_consts, ld_consts)])
                oh_mm[i] = tmm

                def _ev(v, i=i, b0=b0, nb=nb):
                    v.tensor_copy(sttile[:, i % 2, b0:b0 + nb],
                                  pb[i % 2][0:HPC, 0:nb])
                    return v.tensor_reduce(sc[0:8, 49:50],
                                           pb[i % 2][0:HPC, 0:nb], AXL, MAXOP)
                tev0 = k.v(_ev, waits=[(k.pe, tmm), (k.dv, t_mem),
                                       (ds_st[i % 2], st_wr.get(i % 2, 0))])
                tev = k.v(lambda v, i=i, b0=b0: v.tensor_tensor(
                    maxcsb[:, 0:1],
                    (maxcsb[:, 0:1] if (i or b0) else sc[0:8, 49:50]),
                    sc[0:8, 49:50], MAXOP), waits=[(k.dv, tev0)])
                mm_free[i % 2] = (k.dv, tev)
                strip_state["tev"] = tev

            ds_st[i % 2].inc(16)
            def _(s, i=i, c0=c0, n=n):
                s.dma_start(out=strh[:, c0:c0 + n],
                            in_=sttile[:, i % 2, 0:n]
                            ).then_inc(ds_st[i % 2].sem, 16)
                return None
            k.sy(_, waits=[(k.dv, strip_state["tev"])])
            st_wr[i % 2] = ds_st[i % 2].n

        def strip_finish():
            while strip_state["idx"] < len(strip_blocks):
                strip_round()
            ds_scg.inc(32)
            def _(g):
                g.dma_start(out=sc[0:1, 0:8],
                            in_=strh[:, NCOLS - 2:NCOLS - 1]
                            .rearrange("h o -> o h"),
                            allow_slow_non_contiguous=True
                            ).then_inc(ds_scg.sem, 16)
                g.dma_start(out=sc[0:1, 8:16], in_=maxcsb[:],
                            allow_slow_non_contiguous=True
                            ).then_inc(ds_scg.sem, 16)
                return None
            k.g_(_, waits=[(ds_st[0], ds_st[0].n), (ds_st[1], ds_st[1].n),
                           (k.dv, strip_state["tev"])])
            return k.v(lambda v: v.tensor_scalar(
                sc[0:1, 8:16], sc[0:1, 8:16], 0.0, None, MAXOP),
                waits=[(ds_scg, ds_scg.n)])

        ds_wa.inc(80)
        def _(s):
            for dst, src in ((wdq_sb, wdq), (wdkv_sb, wdkv), (wuk_sb, wuk),
                             (wuv_sb, wuv), (wuq_sb, wuq)):
                s.dma_start(out=dst[:], in_=src.rearrange("a b p n -> p a b n")
                            ).then_inc(ds_wa.sem, 16)
            return None
        k.adma(_)
        t_p1w = ds_wa.n

        v0_wr = [0] * NCH
        vst_dma = 0
        cq_dma = 0
        ks_dma = 0

        xfree_p = [0, 0]
        for j in range(NCH):
            strip_round()
            strip_round()
            par = j % 2
            xv = xch if par == 0 else xch2
            lv = lnTa if par == 0 else lnTa2
            ds_x.inc(16)
            def _(s, j=j, xv=xv):
                s.dma_start(out=xv[:],
                            in_=x_in[CH * j:CH * (j + 1), :]
                            .rearrange("(t p) d -> p t d", p=128)
                            ).then_inc(ds_x.sem, 16)
                return None
            k.sy(_, waits=[(k.dv, xfree_p[par])])
            t_x = ds_x.n

            t_st = emit_ln_stats(xv, 4, [(ds_x, t_x)], ws=64 * par)
            t_fm = emit_ln_fm(xv, 4, t_st, None, None, lv, ws=64 * par)
            xfree_p[par] = t_fm

            for do in range(2):
                tmm = mmgroup(do, lambda te, bank, do=do, lv=lv: [
                    te.matmul(bank[:, 0:512], wdq_sb[:, do, di, :],
                              lv[:, di, :], start=(di == 0), stop=(di == 7))
                    for di in range(8)][-1],
                    [(k.dv, t_fm), (ds_wa, t_p1w)])
                bank_evac(do, tmm, lambda v, do=do: v.tensor_copy(
                    cqst[:, do, :], pb[do][:, 0:512]),
                    extra=[(ds_cq, cq_dma)])
            t_cq = k.dv.n

            ds_cq.inc(16)
            def _(s, j=j):
                s.dma_start(out=cqh[:, :, CH * j:CH * (j + 1)]
                            .rearrange("a p n -> p a n"),
                            in_=cqst[:]).then_inc(ds_cq.sem, 16)
                return None
            k.sy(_, waits=[(k.dv, t_cq)])
            cq_dma = ds_cq.n

            for do in range(2):
                tmm = mmgroup(2 + do, lambda te, bank, do=do, lv=lv: [
                    te.matmul(bank[:, 0:512], wdkv_sb[:, do, di, :],
                              lv[:, di, :], start=(di == 0), stop=(di == 7))
                    for di in range(8)][-1],
                    [(k.dv, t_fm), (ds_wa, t_p1w)])
                bank_evac(2 + do, tmm, lambda v, do=do: v.tensor_copy(
                    ckvT[:, do, :], pb[2 + do][:, 0:512]))
            t_ckv = k.dv.n

            for do in range(4):
                bi = 4 + (do % 2)
                tmm = mmgroup(bi, lambda te, bank, do=do: [
                    te.matmul(bank[:, 0:512], wuk_sb[:, do, di, :],
                              ckvT[:, di, :], start=(di == 0), stop=(di == 1))
                    for di in range(2)][-1], [(k.dv, t_ckv)])
                t_hi = k.v(lambda v, do=do, bi=bi: v.tensor_copy(
                    ksst[:, do, 0, :], pb[bi][:, 0:512]),
                    waits=[(k.pe, tmm), (ds_ks, ks_dma)])
                t_ev = k.v(lambda v, do=do, bi=bi: v.tensor_tensor(
                    ksst[:, do, 1, :], pb[bi][:, 0:512],
                    ksst[:, do, 0, :], SUB), waits=[(k.dv, t_hi)])
                mm_free[bi] = (k.dv, t_ev)
                t_sq = k.v(lambda v, do=do: v.tensor_tensor(
                    vts[:], ksst[:, do, 0, :], ksst[:, do, 0, :], MULT),
                    waits=[(k.dv, t_hi), (k.pe, vts_rd[0])])
                tnm = mmgroup(6, lambda te, bank: te.matmul(
                    bank[0:1, 0:512], onescol[:], vts[:], start=True,
                    stop=True), [(k.dv, t_sq)])
                vts_rd[0] = tnm

                trd = bank_evac(6, tnm, lambda v: v.tensor_reduce(
                    sc[0:1, 48:49], pb[6][0:1, 0:512], AXL, MAXOP),
                    extra=[(k.dv, t_relu)])
                k.v(lambda v, do=do: v.tensor_tensor(
                    sc[0:1, 16 + do:17 + do], sc[0:1, 16 + do:17 + do],
                    sc[0:1, 48:49], MAXOP), waits=[(k.dv, trd)])
            t_ks = k.dv.n

            ds_ks.inc(16)
            def _(s, j=j):
                s.dma_start(out=kfh[4 * j:4 * (j + 1)]
                            .rearrange("t p a b n -> p a b t n"),
                            in_=ksst[:].rearrange("p a b (t n) -> p a b t n",
                                                  n=128)
                            ).then_inc(ds_ks.sem, 16)
                return None
            k.sy(_, waits=[(k.dv, t_ks)])
            ks_dma = ds_ks.n

            for do in range(4):
                bi = 4 + (do % 2)
                tmm = mmgroup(bi, lambda te, bank, do=do: [
                    te.matmul(bank[:, 0:512], wuv_sb[:, do, di, :],
                              ckvT[:, di, :], start=(di == 0), stop=(di == 1))
                    for di in range(2)][-1], [(k.dv, t_ckv)])
                t_ev = bank_evac(bi, tmm, lambda v, bi=bi: v.tensor_copy(
                    vts[:], pb[bi][:, 0:512]))
                for tt in range(4):
                    ttr = k.t(lambda te, tt=tt: te.transpose(
                        pb[7][:, 0:128], vts[:, tt * 128:(tt + 1) * 128],
                        identsb[:]), waits=[(k.dv, t_ev), mm_free[7]])

                    def _vv(v, tt=tt, do=do):
                        v.tensor_copy(vstage[:, tt, 2 * do, 0:64],
                                      pb[7][:, 0:64])
                        return v.tensor_copy(vstage[:, tt, 2 * do + 1, 0:64],
                                             pb[7][:, 64:128])
                    bank_evac(7, ttr, _vv, extra=[(ds_vst, vst_dma)])
            t_vst = k.v(lambda v: v.memset(vstage[:, :, :, 64:65], 1.0),
                        waits=[mm_free[7]])

            ds_vst.inc(16)
            def _(s, j=j):
                s.dma_start(out=v0h[4 * j:4 * (j + 1)]
                            .rearrange("t p h c -> p t h c"),
                            in_=vstage[:]).then_inc(ds_vst.sem, 16)
                return None
            k.sy(_, waits=[(k.dv, t_vst)])
            v0_wr[j] = ds_vst.n
            vst_dma = ds_vst.n
            k.flush()

        t_relu = strip_finish()
        k.barrier()
        ph_a.close()

        def attention(tag, get_q, kf_mode, vt_mode, wout_ds, rsi, rso,
                      mc_base, kn_col, use_bias, extra_w, qf_hi, qf_lo,
                      esb, attnT, ystg, wosb, vring, kring, b01_w,
                      rs_ticks):
            attn_guard = (k.pe, 0)
            yp_dma = None
            rs_wr = [[0, 0] for _ in range(4)]
            vr_av = {}
            kr_sc = {}
            for ci, c in enumerate(CHORD):
                t_qs, t_qn = get_q(c)
                t_q = t_qs

                def _m(v):
                    r = None
                    for p in range(4):
                        r = v.tensor_tensor(
                            sc[0:1, 44 + p:45 + p], sc[0:1, 40 + p:41 + p],
                            sc[0:1, kn_col(p):kn_col(p) + 1], MULT)
                    return r
                tm1 = k.v(_m, waits=[(k.dv, t_qn)])

                def _s(a):
                    r = None
                    for p in range(4):
                        r = a.activation(sc[0:1, 44 + p:45 + p],
                                         sc[0:1, 44 + p:45 + p], AF.Sqrt,
                                         bias=zerov[:], scale=1.0201)
                    return r
                tm2 = k.a(_s, waits=[(k.dv, tm1)])

                def _b(v):
                    r = None
                    for p in range(4):
                        r = v.tensor_scalar(
                            sc[0:1, 24 + 2 * p:26 + 2 * p],
                            sc[0:1, mc_base + 2 * p:mc_base + 2 * p + 2],
                            sc[0:1, 44 + p:45 + p], -1.0, ADD, op1=MULT)
                    return r
                tm3 = k.v(_b, waits=[(k.ac, tm2)])
                tbc = mmgroup(6, lambda te, bank: te.matmul(
                    bank[0:128, 0:8], onesrow[:], sc[0:1, 24:32],
                    start=True, stop=True), [(k.dv, tm3)])
                t_bv = bank_evac(6, tbc, lambda v: v.tensor_copy(
                    bvec[:], pb[6][0:128, 0:8]))

                nkt = 4 * c + 4
                epi_pend = [None]
                t_at = 0
                for p in range(4):
                    if epi_pend[0] is not None and nkt == 4:
                        t_at = epi_pend[0]()
                        epi_pend[0] = None
                    av_done = {}
                    av_args = {}
                    for t in range(nkt + 1):
                      if t < nkt:
                        sb0 = 2 * (t % 2)
                        if kf_mode is None:
                            sl = t % KRING

                            ds_kr[sl].inc(16)
                            def _kld(s, t=t, sl=sl, p=p):
                                s.dma_start(out=kring[:, sl, :, :],
                                            in_=kfh[t][:, p]
                                            ).then_inc(ds_kr[sl].sem, 16)
                                return None
                            k.sy(_kld, waits=[kr_sc.get(sl, (k.pe, 0))])
                            kw = [(ds_kr[sl], ds_kr[sl].n)]

                            def kf_sl(kv, po, sl=sl):
                                return kring[po:po + 64, sl,
                                             0 if kv == "hi" else 1, :]
                        else:
                            kf2h, kf2l = kf_mode
                            kw = []

                            def kf_sl(kv, po, t_=t):
                                src = kf2h if kv == "hi" else kf2l
                                return src[po:po + 64,
                                           128 * t_:128 * (t_ + 1)]

                        if vt_mode is None:
                            slv = t % VRING

                            ds_vr[slv].inc(16)
                            def _vld(s, t=t, slv=slv, p=p):
                                s.dma_start(out=vring[:, slv, :, :],
                                            in_=v0h[t][:, 2 * p:2 * p + 2, :]
                                            ).then_inc(ds_vr[slv].sem, 16)
                                return None
                            k.sy(_vld, waits=[vr_av.get(slv, (k.pe, 0))])
                            vt_get = (lambda hh, slv=slv:
                                      vring[:, slv, hh, :])
                            vw = [(ds_vr[slv], ds_vr[slv].n)]
                        else:
                            v1sb_, t_v1 = vt_mode
                            vt_get = (lambda hh, t=t: v1sb_[:, t, :])
                            vw = [(k.dv, t_v1)]

                        def _sc(te, p=p, t=t, sb0=sb0, kf_sl=kf_sl):
                            r = None
                            for hh in range(2):
                                po = 64 * hh
                                bank = pb[sb0 + hh]
                                for i, (qop, kv) in enumerate(
                                        ((qf_hi, "hi"), (qf_lo, "hi"),
                                         (qf_hi, "lo"))):
                                    r = te.matmul(
                                        bank[:, 0:512], kf_sl(kv, po),
                                        qop[po:po + 64, p, :],
                                        start=(i == 0), stop=(i == 2),
                                        tile_position=(po, 0))
                            return r
                        t_smm = k.t(_sc, [(k.dv, t_q), mm_free[sb0],
                                          mm_free[sb0 + 1]] + kw +
                                    list(extra_w))
                        if kf_mode is None:
                            kr_sc[sl] = (k.pe, t_smm)

                        t_badd = 0
                        if use_bias:
                            addv = []
                            for qs in range(4):
                                qt = 4 * c + qs
                                if t == qt:
                                    addv.append((qs, b0t))
                                elif t == qt - 1:
                                    addv.append((qs, b1t))
                            if addv:
                                def _ba(v, p=p, addv=addv, sb0=sb0):
                                    r = None
                                    for hh in range(2):
                                        h = 2 * p + hh
                                        for qs, bt in addv:
                                            sl_ = slice(128 * qs,
                                                        128 * (qs + 1))
                                            r = v.tensor_tensor(
                                                pb[sb0 + hh][:, sl_],
                                                pb[sb0 + hh][:, sl_],
                                                bt[:, h, :], ADD)
                                    return r
                                t_badd = k.v(_ba, waits=[(k.pe, t_smm),
                                                         b01_w])

                        def _ex(a, p=p, t=t, sb0=sb0):
                            r = None
                            for hh in range(2):
                                h = 2 * p + hh
                                r = a.activation(
                                    esb[:, hh, t % 2, :],
                                    pb[sb0 + hh][:, 0:512], AF.Exp,
                                    bias=bvec[:, h:h + 1], scale=1.0)
                            return r
                        wex = [(k.pe, t_smm), (k.dv, t_bv)]
                        if t_badd:
                            wex.append((k.dv, t_badd))
                        if t >= 2:
                            wex.append(av_done[t - 2])
                        t_exp = k.a(_ex, waits=wex)
                        mm_free[sb0] = (k.ac, t_exp)
                        mm_free[sb0 + 1] = (k.ac, t_exp)
                        if t == 0 and epi_pend[0] is not None:
                            t_at = epi_pend[0]()
                            epi_pend[0] = None

                        t_mask = 0
                        if t >= 4 * c:
                            mi = t - 4 * c

                            def _mk(v, mi=mi, t=t):
                                r = None
                                for hh in range(2):
                                    r = v.tensor_tensor(
                                        esb[:, hh, t % 2, :],
                                        esb[:, hh, t % 2, :],
                                        masks[:, mi, :], MULT)
                                return r
                            t_mask = k.v(_mk, waits=[(k.ac, t_exp),
                                                     (k.gp, t_masks)])

                        av_args[t] = (vt_get,
                                      list(vw) + [(k.dv, t_mask) if t_mask
                                                  else (k.ac, t_exp)],
                                      slv if vt_mode is None else None)
                      ta = t - 1
                      if ta >= 0:
                        vt_get_a, w_av, slv_a = av_args.pop(ta)

                        def _av(te, ta=ta, nkt=nkt, vt_get_a=vt_get_a):
                            r = None
                            for hh in range(2):
                                r = te.matmul(
                                    pb[4 + hh][0:65, 0:512],
                                    vt_get_a(hh),
                                    esb[:, hh, ta % 2, :],
                                    start=(ta == 0), stop=(ta == nkt - 1))
                            return r
                        if ta == 0:
                            w_av = w_av + [mm_free[4], mm_free[5]]
                        tick_av = k.t(_av, w_av)
                        av_done[ta] = (k.pe, tick_av)
                        if slv_a is not None:
                            vr_av[slv_a] = (k.pe, tick_av)

                    def epilogue(p=p, av_last=av_done[nkt - 1],
                                 ag=attn_guard):
                        t_rc = k.v(lambda v: [
                            v.reciprocal(rcrow[64:65, hh, :],
                                         pb[4 + hh][64:65, 0:512])
                            for hh in range(2)][-1],
                            waits=[av_last])
                        for hh in range(2):
                            t_bcm = mmgroup(6, lambda te, bank, hh=hh:
                                te.matmul(
                                    bank[0:64, 0:512], onesr64[64:65, :],
                                    rcrow[64:65, hh, :], start=True,
                                    stop=True, tile_position=(64, 0)),
                                [(k.dv, t_rc), (k.dv, t_mem)])
                            t_rb = bank_evac(6, t_bcm, lambda v:
                                v.tensor_copy(rbs[0:64, :],
                                              pb[6][0:64, 0:512]))
                            if hh == 0:
                                t_mu = k.v(lambda v, p=p: v.tensor_tensor(
                                    attnT[0:64, p, :], pb[4][0:64, 0:512],
                                    rbs[0:64, :], MULT),
                                    waits=[(k.dv, t_rb), ag])
                                mm_free[4] = (k.dv, t_mu)
                            else:
                                t_mu = k.v(lambda v: v.tensor_tensor(
                                    stg64[0:64, :], pb[5][0:64, 0:512],
                                    rbs[0:64, :], MULT),
                                    waits=[(k.dv, t_rb), (ds_o2, ds_o2.n)])
                                mm_free[5] = (k.dv, t_mu)

                                ds_o2.inc(16)
                                def _od(s, p=p):
                                    s.dma_start(out=attnT[64:128, p, :],
                                                in_=stg64[0:64, :]
                                                ).then_inc(ds_o2.sem, 16)
                                    return None
                                k.sy(_od, waits=[(k.dv, t_mu)])
                        return k.dv.n
                    epi_pend[0] = epilogue
                if epi_pend[0] is not None:
                    t_at = epi_pend[0]()
                    epi_pend[0] = None
                k.flush()

                for dh in range(4):
                    for do2 in range(2):
                        do = 2 * dh + do2
                        bi = do2
                        tmm = mmgroup(bi, lambda te, bank, do=do: [
                            te.matmul(bank[:, 0:512], wosb[:, do, di, :],
                                      attnT[:, di, :], start=(di == 0),
                                      stop=(di == 3)) for di in range(4)][-1],
                            [(k.dv, t_at), wout_ds, (ds_o2, ds_o2.n)])
                        bank_evac(bi, tmm, lambda v, do2=do2, bi=bi:
                                  v.tensor_copy(ystg[:, do2, :],
                                                pb[bi][:, 0:512]),
                                  extra=[yp_dma])
                    t_y2 = k.dv.n

                    ds_rs[(c % 4) % 2].inc(16)
                    def _(s, c=c, dh=dh):
                        s.dma_start(
                            out=rsi[c % 4][c // 4, 256 * dh:256 * (dh + 1), :]
                            .rearrange("(dc p) n -> p dc n", p=128),
                            in_=ystg[:]).then_inc(ds_rs[(c % 4) % 2].sem, 16)
                        return None
                    k.sy(_, waits=[(k.dv, t_y2)])
                    yp_dma = (ds_rs[(c % 4) % 2], ds_rs[(c % 4) % 2].n)
                attn_guard = (k.pe, k.pe.n)

                if ci % 2 == 1:
                    j = c % 4

                    def _(g, j=j):
                        r = g.collective_compute(
                            "ReduceScatter", ADD, ins=[rsi[j][:]],
                            outs=[rso[j][:]], replica_groups=RG)
                        r.then_inc(k.col.sem, 1)
                        return None
                    k.col.inc()
                    k.g_(_, waits=[(ds_rs[j % 2], ds_rs[j % 2].n)])
                    rs_ticks[(tag, j)] = k.col.n
                k.flush()

        ph_b = contextlib.ExitStack()
        sbb = mksb(ph_b)
        qf_hi_b = sbb("qf_hi", [128, 4, CH], BF16)
        qf_lo_b = sbb("qf_lo", [128, 4, CH], BF16)
        qscr = sbb("qscr", [128, 512])
        esb_b = sbb("esb", [128, 2, 2, 512])
        vring_b = sbb("vring", [128, VRING, 2, 65])
        kring_b = sbb("kring", [128, KRING, 2, 128], BF16)
        attnT_b = sbb("attnT", [128, 4, 512])
        ystg_b = sbb("ystg", [128, 2, 512])
        wosb_b = sbb("wosb", [128, 8, 4, 128])
        cqbuf = sbb("cqbuf", [128, 2, 512])
        rs_ticks = {}

        ds_wo.inc(16)
        def _(s):
            s.dma_start(out=wosb_b[:], in_=wo.rearrange("a b p n -> p a b n")
                        ).then_inc(ds_wo.sem, 16)
            return None
        k.adma(_)
        t_wo = ds_wo.n

        cq_rd = [0]
        qs_tick = [0]

        def l0_get_q(c):
            qs_tick[0] = 0
            ds_cqb.inc(16)
            def _(s, c=c):
                s.dma_start(out=cqbuf[:],
                            in_=cqh[:, :, CH * c:CH * (c + 1)]
                            .rearrange("a p n -> p a n")
                            ).then_inc(ds_cqb.sem, 16)
                return None
            k.sy(_, waits=[(k.pe, cq_rd[0])])
            t_cql = ds_cqb.n
            for p in range(4):
                bi = p % 2
                tmm = mmgroup(bi, lambda te, bank, p=p: [
                    te.matmul(bank[:, 0:512], wuq_sb[:, p, di, :],
                              cqbuf[:, di, :], start=(di == 0),
                              stop=(di == 1)) for di in range(2)][-1],
                    [(ds_cqb, t_cql), (ds_wa, t_p1w)])
                t_s = k.v(lambda v, bi=bi: v.tensor_scalar(
                    qscr[:], pb[bi][:, 0:512], 0.125, None, MULT),
                    waits=[(k.pe, tmm)])
                t_h = k.v(lambda v, p=p: v.tensor_copy(
                    qf_hi_b[:, p, :], qscr[:]), waits=[(k.dv, t_s)])
                t_ev = k.v(lambda v, p=p: v.tensor_tensor(
                    qf_lo_b[:, p, :], qscr[:], qf_hi_b[:, p, :], SUB),
                    waits=[(k.dv, t_h)])
                mm_free[bi] = (k.dv, t_ev)
                t_sq = k.v(lambda v, p=p: v.tensor_tensor(
                    vts[:], qf_hi_b[:, p, :], qf_hi_b[:, p, :], MULT),
                    waits=[(k.dv, t_h), (k.pe, vts_rd[0])])
                tnm = mmgroup(6, lambda te, bank: te.matmul(
                    bank[0:1, 0:512], onescol[:], vts[:], start=True,
                    stop=True), [(k.dv, t_sq)])
                vts_rd[0] = tnm
                bank_evac(6, tnm, lambda v, p=p: v.tensor_reduce(
                    sc[0:1, 40 + p:41 + p], pb[6][0:1, 0:512], AXL, MAXOP))
                qs_tick[0] = max(qs_tick[0], t_ev)
            cq_rd[0] = k.pe.n
            return qs_tick[0], k.dv.n

        attention("l0", l0_get_q, None, None, (ds_wo, t_wo), rs0i, rs0o,
                  32, lambda p: 16 + p, False, [], qf_hi_b, qf_lo_b,
                  esb_b, attnT_b, ystg_b, wosb_b, vring_b, kring_b,
                  None, rs_ticks)

        k.barrier()
        ph_b.close()

        ph_c = contextlib.ExitStack()
        sbc = mksb(ph_c)
        lnThi = sbc("lnThi", [128, 8, CH], BF16)
        lnTlo = sbc("lnTlo", [128, 8, CH], BF16)
        uhi = sbc("uhi", [128, 32, CH], BF16)
        ulo = sbc("ulo", [128, 32, CH], BF16)
        ghi = sbc("ghi", [128, 32, CH], BF16)
        glo = sbc("glo", [128, 32, CH], BF16)
        fsc = sbc("fsc", [128, 512])
        qscr2 = sbc("qscr2", [128, 512])
        wring = sbc("wring", [128, 2, 32, 128], BF16)

        wr_free = [(k.pe, 0), (k.pe, 0)]
        fsc_rd = [0]
        ag_ticks = {}

        def stream_wcol(hbm_tiled, do, ndi, slot):
            ds_wr[slot].inc(16)
            def _(s, do=do, ndi=ndi, slot=slot):
                s.dma_start(out=wring[:, slot, 0:ndi, :],
                            in_=hbm_tiled[do].rearrange("b p n -> p b n")
                            ).then_inc(ds_wr[slot].sem, 16)
                return None
            k.adma(_, waits=[wr_free[slot], (ds_conv, t_conv)])
            return (ds_wr[slot], ds_wr[slot].n)

        for kk in range(4):
            ds_x.inc(16)
            def _(s, kk=kk):
                s.dma_start(out=xch[:],
                            in_=x_my[CH * kk:CH * (kk + 1), :]
                            .rearrange("(t p) d -> p t d", p=128)
                            ).then_inc(ds_x.sem, 16)
                return None
            k.sy(_, waits=[(k.col, rs_ticks[("l0", kk)]),
                           (k.dv, cell["x_free"]), cell["x_dma"]])
            t_xld = ds_x.n

            for dc in range(8):
                bi = 6 + (dc % 2)

                sl_ = dc % 2
                ds_stg[sl_].inc(16)
                def _(s, kk=kk, dc=dc, sl_=sl_):
                    s.dma_start(out=stg[:, sl_, :],
                                in_=rs0o[kk][128 * dc:128 * (dc + 1), :]
                                ).then_inc(ds_stg[sl_].sem, 16)
                    return None
                k.sy(_, waits=[(k.pe, stg_read[sl_])])
                t_sld = ds_stg[sl_].n

                def _t(te, bi=bi, sl_=sl_):
                    r = None
                    for tt in range(4):
                        r = te.transpose(pb[bi][:, tt * 128:(tt + 1) * 128],
                                         stg[:, sl_, tt * 128:(tt + 1) * 128],
                                         identsb[:])
                    return r
                tmm = k.t(_t, waits=[(ds_stg[sl_], t_sld), mm_free[bi]])
                stg_read[sl_] = tmm

                def _v(v, dc=dc, bi=bi):
                    r = None
                    for tt in range(4):
                        r = v.tensor_tensor(
                            xch[:, tt, 128 * dc:128 * (dc + 1)],
                            pb[bi][:, tt * 128:(tt + 1) * 128],
                            xch[:, tt, 128 * dc:128 * (dc + 1)], ADD)
                    return r
                bank_evac(bi, tmm, _v, extra=[(ds_x, t_xld)])
            t_x1 = k.dv.n

            t_stt = emit_ln_stats(xch, 4, [(k.dv, t_x1)])
            t_sp = emit_ln_fm(xch, 4, t_stt, lnThi, lnTlo, None)

            for do in range(32):
                sl = do % 2
                s1 = stream_wcol(w1b, do, 8, sl)
                tm1 = mmgroup(0, lambda te, bank, sl=sl: [
                    te.matmul(bank[:, 0:512], wring[:, sl, i % 8, :],
                              (lnThi if i < 8 else lnTlo)[:, i % 8, :],
                              start=(i == 0), stop=(i == 15))
                    for i in range(16)][-1], [(k.dv, t_sp), s1])
                wr_free[sl] = (k.pe, tm1)
                s2 = stream_wcol(w2b, do, 8, 1 - sl)
                tm2 = mmgroup(1, lambda te, bank, sl=sl: [
                    te.matmul(bank[:, 0:512], wring[:, 1 - sl, i % 8, :],
                              (lnThi if i < 8 else lnTlo)[:, i % 8, :],
                              start=(i == 0), stop=(i == 15))
                    for i in range(16)][-1], [(k.dv, t_sp), s2])
                wr_free[1 - sl] = (k.pe, tm2)
                t_si = k.a(lambda a, do=do: a.activation(
                    fsc[:], pb[0][:, 0:512], AF.Silu,
                    scale=sclsb[:, do:do + 1]),
                    waits=[(k.pe, tm1), (k.dv, fsc_rd[0])])
                mm_free[0] = (k.ac, t_si)

                t_u0 = k.v(lambda v, do=do: v.scalar_tensor_tensor(
                    qscr2[:], pb[1][:, 0:512], sclsb[:, 32 + do:33 + do],
                    fsc[:], MULT, MULT), waits=[(k.ac, t_si), (k.pe, tm2)])
                t_u1 = k.v(lambda v, do=do: v.tensor_copy(
                    uhi[:, do, :], qscr2[:]), waits=[(k.dv, t_u0)])
                t_u = k.v(lambda v, do=do: v.tensor_tensor(
                    ulo[:, do, :], qscr2[:], uhi[:, do, :], SUB),
                    waits=[(k.dv, t_u1)])
                mm_free[1] = (k.dv, t_u0)
                fsc_rd[0] = t_u0
                if do % 8 == 7:
                    k.flush()
            t_uall = k.dv.n

            for do in range(32):
                sl = do % 2
                s1 = stream_wcol(h1b, do, 32, sl)
                bi = 4 + (do % 2)
                tm = mmgroup(bi, lambda te, bank, sl=sl: [
                    te.matmul(bank[:, 0:512], wring[:, sl, i % 32, :],
                              (uhi if i < 32 else ulo)[:, i % 32, :],
                              start=(i == 0), stop=(i == 63))
                    for i in range(64)][-1], [(k.dv, t_uall), s1])
                wr_free[sl] = (k.pe, tm)
                t_ge = k.a(lambda a, do=do, bi=bi: a.activation(
                    fsc[:], pb[bi][:, 0:512], AF.Gelu,
                    scale=sclsb[:, 64 + do:65 + do]),
                    waits=[(k.pe, tm), (k.dv, fsc_rd[0])])
                mm_free[bi] = (k.ac, t_ge)

                t_g0 = k.v(lambda v, do=do: v.tensor_copy(
                    ghi[:, do, :], fsc[:]), waits=[(k.ac, t_ge)])
                fsc_rd[0] = k.v(lambda v, do=do: v.tensor_tensor(
                    glo[:, do, :], fsc[:], ghi[:, do, :], SUB),
                    waits=[(k.dv, t_g0)])
                if do % 8 == 7:
                    k.flush()
            t_gall = k.dv.n

            for do in range(32):
                sl = do % 2
                s1 = stream_wcol(h2b, do, 32, sl)
                bi = 4 + (do % 2)
                tm = mmgroup(bi, lambda te, bank, sl=sl: [
                    te.matmul(bank[:, 0:512], wring[:, sl, i % 32, :],
                              (ghi if i < 32 else glo)[:, i % 32, :],
                              start=(i == 0), stop=(i == 63))
                    for i in range(64)][-1], [(k.dv, t_gall), s1])
                wr_free[sl] = (k.pe, tm)
                t_ge = k.a(lambda a, do=do, bi=bi: a.activation(
                    fsc[:], pb[bi][:, 0:512], AF.Gelu,
                    scale=sclsb[:, 96 + do:97 + do]),
                    waits=[(k.pe, tm), (k.dv, fsc_rd[0])])
                mm_free[bi] = (k.ac, t_ge)

                t_g0 = k.v(lambda v, do=do: v.tensor_copy(
                    uhi[:, do, :], fsc[:]), waits=[(k.ac, t_ge)])
                fsc_rd[0] = k.v(lambda v, do=do: v.tensor_tensor(
                    ulo[:, do, :], fsc[:], uhi[:, do, :], SUB),
                    waits=[(k.dv, t_g0)])
                if do % 8 == 7:
                    k.flush()
            t_g2 = k.dv.n

            for do in range(8):
                sl = do % 2
                s1 = stream_wcol(w3b, do, 32, sl)
                bi = do % 2
                tm = mmgroup(bi, lambda te, bank, sl=sl: [
                    te.matmul(bank[:, 0:512], wring[:, sl, i % 32, :],
                              (uhi if i < 32 else ulo)[:, i % 32, :],
                              start=(i == 0), stop=(i == 63))
                    for i in range(64)][-1], [(k.dv, t_g2), s1])
                wr_free[sl] = (k.pe, tm)
                sl_ = do % 2
                t_cp = bank_evac(bi, tm, lambda a, do=do, bi=bi, sl_=sl_:
                    a.activation(
                        stg[:, sl_, :], pb[bi][:, 0:512], AF.Copy, bias=0.0,
                        scale=sclsb[:, 128 + do:129 + do]), eng="a",
                    extra=[(k.pe, stg_read[sl_])])

                bj = 6 + (do % 2)

                def _t(te, bj=bj, sl_=sl_):
                    r = None
                    for tt in range(4):
                        r = te.transpose(pb[bj][:, tt * 128:(tt + 1) * 128],
                                         stg[:, sl_, tt * 128:(tt + 1) * 128],
                                         identsb[:])
                    return r
                tmm = k.t(_t, waits=[(k.ac, t_cp), mm_free[bj]])
                stg_read[sl_] = tmm

                def _v(v, do=do, bj=bj):
                    r = None
                    for tt in range(4):
                        r = v.tensor_tensor(
                            xch[:, tt, 128 * do:128 * (do + 1)],
                            pb[bj][:, tt * 128:(tt + 1) * 128],
                            xch[:, tt, 128 * do:128 * (do + 1)], ADD)
                    return r
                bank_evac(bj, tmm, _v)
            t_x2 = k.dv.n

            ds_ag[kk % 2].inc(32)
            def _(s, kk=kk):
                s.dma_start(out=agi[kk].rearrange("(t p) d -> p t d", p=128),
                            in_=xch[:]).then_inc(ds_ag[kk % 2].sem, 16)
                s.dma_start(out=x2h[CH * kk:CH * (kk + 1), :]
                            .rearrange("(t p) d -> p t d", p=128),
                            in_=xch[:]).then_inc(ds_ag[kk % 2].sem, 16)
                return None
            k.sy(_, waits=[(k.dv, t_x2)])
            cell["x_free"] = t_x2
            cell["x_dma"] = (ds_ag[kk % 2], ds_ag[kk % 2].n)

            def _(g, kk=kk):
                r = g.collective_compute("AllGather", AluOpType.bypass,
                                         ins=[agi[kk][:]], outs=[ago[kk][:]],
                                         replica_groups=RG)
                r.then_inc(k.col.sem, 1)
                return None
            k.col.inc()
            k.g_(_, waits=[(ds_ag[kk % 2], ds_ag[kk % 2].n)])
            ag_ticks[kk] = k.col.n
            k.flush()

        k.barrier()
        ph_c.close()

        ph_d = contextlib.ExitStack()
        sbd = mksb(ph_d)
        b0t = sbd("b0t", [128, HPC, 128])
        b1t = sbd("b1t", [128, HPC, 128])
        lnTd = sbd("lnTd", [128, 8, CH])
        kf2_hi = sbd("kf2_hi", [128, S], BF16)
        kf2_lo = sbd("kf2_lo", [128, S], BF16)
        v1sb = sbd("v1sb", [128, NKT, 65])
        wq1_sb = sbd("wq1_sb", [128, 4, 8, 128])
        wk1_sb = sbd("wk1_sb", [128, 8, 64])
        wv1_sb = sbd("wv1_sb", [128, 8, 64])
        qf_hi_d = sbd("qf_hi_d", [128, 4, CH], BF16)
        qf_lo_d = sbd("qf_lo_d", [128, 4, CH], BF16)
        qscr_d = sbd("qscr_d", [128, 512])
        esb_d = sbd("esb_d", [128, 2, 2, 512])
        attnT_d = sbd("attnT_d", [128, 4, 512])
        ystg_d = sbd("ystg_d", [128, 2, 512])
        wosb_d = sbd("wosb_d", [128, 8, 4, 128])

        ds_b01.inc(32 * HPC)
        def _(s):
            for h in range(HPC):
                s.dma_start(out=b0t[:, h, :],
                            in_=strh[h:h + 1, 0:16384]
                            .rearrange("o (p n) -> (o p) n", p=128)
                            ).then_inc(ds_b01.sem, 16)
                s.dma_start(out=b1t[:, h, :],
                            in_=strh[h:h + 1, 16384:32768]
                            .rearrange("o (p n) -> (o p) n", p=128)
                            ).then_inc(ds_b01.sem, 16)
            return None
        k.sy(_)
        t_b01 = ds_b01.n

        ds_w4.inc(64)
        def _(s):
            s.dma_start(out=wq1_sb[:], in_=wq1.rearrange("a b p n -> p a b n")
                        ).then_inc(ds_w4.sem, 16)
            s.dma_start(out=wk1_sb[:], in_=wk1.rearrange("b p n -> p b n")
                        ).then_inc(ds_w4.sem, 16)
            s.dma_start(out=wv1_sb[:], in_=wv1.rearrange("b p n -> p b n")
                        ).then_inc(ds_w4.sem, 16)
            s.dma_start(out=wosb_d[:], in_=wo1.rearrange("a b p n -> p a b n")
                        ).then_inc(ds_w4.sem, 16)
            return None
        k.adma(_)
        t_p4w = ds_w4.n

        def load_ago_chunk(j):
            g = j // 4
            kkk = j % 4

            ds_x.inc(16)
            def _(s, g=g, kkk=kkk):
                s.dma_start(out=xch[:], in_=ago[kkk][g]
                            .rearrange("(t p) d -> p t d", p=128)
                            ).then_inc(ds_x.sem, 16)
                return None
            k.sy(_, waits=[(k.col, ag_ticks[kkk]), (k.dv, cell["x_free"]),
                           cell["x_dma"]])
            return ds_x.n

        for j in range(NCH):
            t_ld = load_ago_chunk(j)
            t_stt = emit_ln_stats(xch, 4, [(ds_x, t_ld)])
            t_fm = emit_ln_fm(xch, 4, t_stt, None, None, lnTd)
            cell["x_free"] = t_fm
            cell["x_dma"] = None

            tmk = mmgroup(0, lambda te, bank: [
                te.matmul(bank[0:64, 0:512], wk1_sb[:, di, :], lnTd[:, di, :],
                          start=(di == 0), stop=(di == 7))
                for di in range(8)][-1], [(k.dv, t_fm), (ds_w4, t_p4w)])
            t_kh = k.v(lambda v, j=j: v.tensor_copy(
                kf2_hi[0:64, CH * j:CH * (j + 1)], pb[0][0:64, 0:512]),
                waits=[(k.pe, tmk)])
            t_ek = k.v(lambda v, j=j: v.tensor_tensor(
                kf2_lo[0:64, CH * j:CH * (j + 1)], pb[0][0:64, 0:512],
                kf2_hi[0:64, CH * j:CH * (j + 1)], SUB),
                waits=[(k.dv, t_kh)])
            mm_free[0] = (k.dv, t_ek)
            t_sq = k.v(lambda v, j=j: v.tensor_tensor(
                vts[0:64, :], kf2_hi[0:64, CH * j:CH * (j + 1)],
                kf2_hi[0:64, CH * j:CH * (j + 1)], MULT),
                waits=[(k.dv, t_kh), (k.pe, vts_rd[0])])
            tnm = mmgroup(6, lambda te, bank: te.matmul(
                bank[0:1, 0:512], onescol[0:64, :], vts[0:64, :], start=True,
                stop=True), [(k.dv, t_sq)])
            vts_rd[0] = tnm

            trd = bank_evac(6, tnm, lambda v: v.tensor_reduce(
                sc[0:1, 48:49], pb[6][0:1, 0:512], AXL, MAXOP),
                extra=[(k.dv, t_relu)])
            k.v(lambda v: v.tensor_tensor(
                sc[0:1, 20:21], sc[0:1, 20:21], sc[0:1, 48:49], MAXOP),
                waits=[(k.dv, trd)])

            tmv = mmgroup(1, lambda te, bank: [
                te.matmul(bank[0:64, 0:512], wv1_sb[:, di, :], lnTd[:, di, :],
                          start=(di == 0), stop=(di == 7))
                for di in range(8)][-1], [(k.dv, t_fm), (ds_w4, t_p4w)])
            t_ev = bank_evac(1, tmv, lambda v: v.tensor_copy(
                vts[0:64, :], pb[1][0:64, 0:512]))
            for tt in range(4):
                ttr = k.t(lambda te, tt=tt: te.transpose(
                    pb[7][:, 0:64], vts[0:64, tt * 128:(tt + 1) * 128],
                    identsb[0:64, 0:64]),
                    waits=[(k.dv, t_ev), mm_free[7]])
                bank_evac(7, ttr, lambda v, tt=tt, j=j: v.tensor_copy(
                    v1sb[:, 4 * j + tt, 0:64], pb[7][:, 0:64]))
            k.flush()

        t_v1o = k.v(lambda v: v.memset(v1sb[:, :, 64:65], 1.0))

        ds_dup.inc(32)
        def _(g):
            g.dma_start(out=kf2_hi[64:128, :], in_=kf2_hi[0:64, :]
                        ).then_inc(ds_dup.sem, 16)
            g.dma_start(out=kf2_lo[64:128, :], in_=kf2_lo[0:64, :]
                        ).then_inc(ds_dup.sem, 16)
            return None
        k.g_(_, waits=[(k.dv, k.dv.n)])
        t_dup = ds_dup.n

        def l1_get_q(c):
            qs_tick[0] = 0
            t_ld = load_ago_chunk(c)
            t_stt = emit_ln_stats(xch, 4, [(ds_x, t_ld)])
            t_fm = emit_ln_fm(xch, 4, t_stt, None, None, lnTd)
            cell["x_free"] = t_fm
            for p in range(4):
                bi = p % 2
                tmm = mmgroup(bi, lambda te, bank, p=p: [
                    te.matmul(bank[:, 0:512], wq1_sb[:, p, di, :],
                              lnTd[:, di, :], start=(di == 0), stop=(di == 7))
                    for di in range(8)][-1], [(k.dv, t_fm), (ds_w4, t_p4w)])
                t_s = k.v(lambda v, bi=bi: v.tensor_scalar(
                    qscr_d[:], pb[bi][:, 0:512], 0.125, None, MULT),
                    waits=[(k.pe, tmm)])
                t_h = k.v(lambda v, p=p: v.tensor_copy(
                    qf_hi_d[:, p, :], qscr_d[:]), waits=[(k.dv, t_s)])
                t_ev = k.v(lambda v, p=p: v.tensor_tensor(
                    qf_lo_d[:, p, :], qscr_d[:], qf_hi_d[:, p, :], SUB),
                    waits=[(k.dv, t_h)])
                mm_free[bi] = (k.dv, t_ev)
                t_sq = k.v(lambda v, p=p: v.tensor_tensor(
                    vts[:], qf_hi_d[:, p, :], qf_hi_d[:, p, :], MULT),
                    waits=[(k.dv, t_h), (k.pe, vts_rd[0])])
                tnm = mmgroup(6, lambda te, bank: te.matmul(
                    bank[0:1, 0:512], onescol[:], vts[:], start=True,
                    stop=True), [(k.dv, t_sq)])
                vts_rd[0] = tnm
                bank_evac(6, tnm, lambda v, p=p: v.tensor_reduce(
                    sc[0:1, 40 + p:41 + p], pb[6][0:1, 0:512], AXL, MAXOP))
                qs_tick[0] = max(qs_tick[0], t_ev)
            return qs_tick[0], k.dv.n

        attention("l1", l1_get_q, (kf2_hi, kf2_lo), (v1sb, t_v1o),
                  (ds_w4, t_p4w), rs1i, rs1o, 8, lambda p: 20, True,
                  [(ds_dup, t_dup)], qf_hi_d, qf_lo_d, esb_d,
                  attnT_d, ystg_d, wosb_d, None, None, (ds_b01, t_b01),
                  rs_ticks)

        k.barrier()
        ph_d.close()

        ph_e = contextlib.ExitStack()
        sbe = mksb(ph_e)
        lnTe = sbe("lnTe", [128, 8, CH])
        h1T = sbe("h1T", [128, 8, CH])
        h2T = sbe("h2T", [128, 8, CH])
        wf1_sb = sbe("wf1_sb", [128, 8, 8, 128])
        wf2_sb = sbe("wf2_sb", [128, 8, 8, 128])
        wf3_sb = sbe("wf3_sb", [128, 8, 8, 128])

        ds_w5.inc(48)
        def _(s):
            for dst, src in ((wf1_sb, wf1), (wf2_sb, wf2), (wf3_sb, wf3)):
                s.dma_start(out=dst[:], in_=src.rearrange("a b p n -> p a b n")
                            ).then_inc(ds_w5.sem, 16)
            return None
        k.adma(_)
        t_p5w = ds_w5.n

        for kk in range(4):
            ds_x.inc(16)
            def _(s, kk=kk):
                s.dma_start(out=xch[:], in_=x2h[CH * kk:CH * (kk + 1), :]
                            .rearrange("(t p) d -> p t d", p=128)
                            ).then_inc(ds_x.sem, 16)
                return None
            k.sy(_, waits=[(k.col, rs_ticks[("l1", kk)]),
                           (k.dv, cell["x_free"]), cell["x_dma"]])
            t_xld = ds_x.n
            cell["x_dma"] = None

            for dc in range(8):
                bi = 6 + (dc % 2)

                sl_ = dc % 2
                ds_stg[sl_].inc(16)
                def _(s, kk=kk, dc=dc, sl_=sl_):
                    s.dma_start(out=stg[:, sl_, :],
                                in_=rs1o[kk][128 * dc:128 * (dc + 1), :]
                                ).then_inc(ds_stg[sl_].sem, 16)
                    return None
                k.sy(_, waits=[(k.pe, stg_read[sl_]), (k.dv, stg_bo[sl_])])
                t_sld = ds_stg[sl_].n
                t_bo = k.v(lambda v, dc=dc, sl_=sl_: v.tensor_scalar(
                    stg[:, sl_, :], stg[:, sl_, :], boutsb[:, dc:dc + 1],
                    None, ADD), waits=[(ds_stg[sl_], t_sld)])
                stg_bo[sl_] = t_bo

                def _t(te, bi=bi, sl_=sl_):
                    r = None
                    for tt in range(4):
                        r = te.transpose(pb[bi][:, tt * 128:(tt + 1) * 128],
                                         stg[:, sl_, tt * 128:(tt + 1) * 128],
                                         identsb[:])
                    return r
                tmm = k.t(_t, waits=[(k.dv, t_bo), mm_free[bi]])
                stg_read[sl_] = tmm

                def _v(v, dc=dc, bi=bi):
                    r = None
                    for tt in range(4):
                        r = v.tensor_tensor(
                            xch[:, tt, 128 * dc:128 * (dc + 1)],
                            pb[bi][:, tt * 128:(tt + 1) * 128],
                            xch[:, tt, 128 * dc:128 * (dc + 1)], ADD)
                    return r
                bank_evac(bi, tmm, _v, extra=[(ds_x, t_xld)])
            t_x3 = k.dv.n

            t_stt = emit_ln_stats(xch, 4, [(k.dv, t_x3)])
            t_fm = emit_ln_fm(xch, 4, t_stt, None, None, lnTe)

            def chain(win, xin, xout, wait_in, t_w):
                t_last = 0
                for do in range(8):
                    bi = do % 2
                    tmm = mmgroup(bi, lambda te, bank, do=do: [
                        te.matmul(bank[:, 0:512], win[:, do, di, :],
                                  xin[:, di, :], start=(di == 0),
                                  stop=(di == 7)) for di in range(8)][-1],
                        [wait_in, (ds_w5, t_w)])
                    t_last = bank_evac(bi, tmm, lambda v, do=do, bi=bi:
                                       v.tensor_copy(xout[:, do, :],
                                                     pb[bi][:, 0:512]))
                return t_last
            t_h1 = chain(wf1_sb, lnTe, h1T, (k.dv, t_fm), t_p5w)
            t_h2 = chain(wf2_sb, h1T, h2T, (k.dv, t_h1), t_p5w)

            for do in range(8):
                bi = do % 2
                tmm = mmgroup(bi, lambda te, bank, do=do: [
                    te.matmul(bank[:, 0:512], wf3_sb[:, do, di, :],
                              h2T[:, di, :], start=(di == 0), stop=(di == 7))
                    for di in range(8)][-1], [(k.dv, t_h2),
                                              (ds_w5, t_p5w)])
                sl_ = do % 2
                t_y = bank_evac(bi, tmm, lambda a, do=do, bi=bi, sl_=sl_:
                    a.activation(
                        stg[:, sl_, :], pb[bi][:, 0:512], AF.Tanh,
                        bias=bias1sb[:, do:do + 1], scale=1.0), eng="a",
                    extra=[(k.pe, stg_read[sl_]), (k.dv, stg_bo[sl_])])

                bj = 6 + (do % 2)

                def _t(te, bj=bj, sl_=sl_):
                    r = None
                    for tt in range(4):
                        r = te.transpose(pb[bj][:, tt * 128:(tt + 1) * 128],
                                         stg[:, sl_, tt * 128:(tt + 1) * 128],
                                         identsb[:])
                    return r
                tmm2 = k.t(_t, waits=[(k.ac, t_y), mm_free[bj]])
                stg_read[sl_] = tmm2

                def _v(v, do=do, bj=bj):
                    r = None
                    for tt in range(4):
                        r = v.tensor_tensor(
                            xch[:, tt, 128 * do:128 * (do + 1)],
                            pb[bj][:, tt * 128:(tt + 1) * 128],
                            xch[:, tt, 128 * do:128 * (do + 1)], ADD)
                    return r
                bank_evac(bj, tmm2, _v)
            t_out = k.dv.n

            k.dout.inc(16)
            def _(s, kk=kk):
                s.dma_start(out=y_out[CH * kk:CH * (kk + 1), :]
                            .rearrange("(t p) d -> p t d", p=128),
                            in_=xch[:]).then_inc(k.dout.sem, 16)
                return None
            k.sy(_, waits=[(k.dv, t_out)])
            out_tick = k.dout.n
            cell["x_free"] = k.v(lambda v: None, waits=[(k.dout, out_tick)])
            k.flush()

        ph_e.close()

        def _(s):
            s.wait_ge(k.dout.sem, k.dout.n)
            return None
        k.sy(_)
        k.flush()

    return nc, tap_outs


_prog_cache = {}


def _get_prog(taps=()):
    key = tuple(sorted(taps))
    if key not in _prog_cache:
        _prog_cache[key] = build_program(taps=key)
    return _prog_cache[key]


def make_in_maps(inputs):
    f32c = lambda a: np.ascontiguousarray(np.asarray(a), dtype=np.float32)
    x = f32c(inputs["L0_attn_x"])
    W_dq = f32c(inputs["L0_attn_W_dq"])
    W_uq = f32c(inputs["L0_attn_W_uq"])
    W_dkv = f32c(inputs["L0_attn_W_dkv"])
    W_uk = f32c(inputs["L0_attn_W_uk"])
    W_uv = f32c(inputs["L0_attn_W_uv"])
    W_o = f32c(inputs["L0_attn_W_o"])
    scl = np.zeros((128, 136), dtype=np.float32)
    scl[:, 0:32] = f32c(inputs["L0_ffn_w1_s"]).reshape(32, 128).T
    scl[:, 32:64] = f32c(inputs["L0_ffn_w2_s"]).reshape(32, 128).T
    scl[:, 64:96] = f32c(inputs["L0_ffn_h1_s"]).reshape(32, 128).T
    scl[:, 96:128] = f32c(inputs["L0_ffn_h2_s"]).reshape(32, 128).T
    scl[:, 128:136] = f32c(inputs["L0_ffn_w3_s"]).reshape(8, 128).T
    Wq1 = f32c(inputs["L1_attn_Wq"])
    Wkv1 = f32c(inputs["L1_attn_Wkv"])
    Wout1 = f32c(inputs["L1_attn_Wout"])
    bout = f32c(inputs["L1_attn_bout"]).reshape(8, 128).T
    rel_emb = f32c(inputs["L1_attn_rel_emb"])
    bias = f32c(inputs["L1_ffn_bias"]).reshape(8, 128).T

    shared = {
        "ident": np.eye(128, dtype=np.float32),
        "onehot": _build_onehot(),
        "wdq": _tile_w(W_dq), "wdkv": _tile_w(W_dkv),
        "w1f": _tile_w(np.ascontiguousarray(f32c(inputs["L0_ffn_w1_w"]).T)),
        "w2f": _tile_w(np.ascontiguousarray(f32c(inputs["L0_ffn_w2_w"]).T)),
        "h1f": _tile_w(np.ascontiguousarray(f32c(inputs["L0_ffn_h1_w"]).T)),
        "h2f": _tile_w(np.ascontiguousarray(f32c(inputs["L0_ffn_h2_w"]).T)),
        "w3f": _tile_w(np.ascontiguousarray(f32c(inputs["L0_ffn_w3_w"]).T)),
        "scl": scl, "bout1": bout, "bias1": bias,
        "wf1": _tile_w(f32c(inputs["L1_ffn_weight"])),
        "wf2": _tile_w(f32c(inputs["L1_ffn_aux_weight1"])),
        "wf3": _tile_w(f32c(inputs["L1_ffn_aux_weight2"])),
    }
    in_maps = []
    for c in range(8):
        b, h = c // 2, c % 2
        hs = slice(512 * h, 512 * (h + 1))
        m = dict(shared)
        m["x_in"] = np.ascontiguousarray(x[b])
        m["x_my"] = np.ascontiguousarray(x[b, TOK * h:TOK * (h + 1)])
        m["wuq"] = _tile_w(np.ascontiguousarray(W_uq[:, hs]))
        m["wuk"] = _tile_w(np.ascontiguousarray(W_uk[:, hs]))
        m["wuv"] = _tile_w(np.ascontiguousarray(W_uv[:, hs]))
        m["wo"] = _tile_w(np.ascontiguousarray(W_o[hs, :]))
        m["wq1"] = _tile_w(np.ascontiguousarray(Wq1[:, hs]))
        m["wk1"] = _tile_w(np.ascontiguousarray(Wkv1[:, 0:64]))[0]
        m["wv1"] = _tile_w(np.ascontiguousarray(Wkv1[:, 64:128]))[0]
        m["wo1"] = _tile_w(np.ascontiguousarray(Wout1[hs, :]))
        m["rel"] = np.ascontiguousarray(rel_emb[:, 8 * h:8 * (h + 1)])
        in_maps.append(m)
    return in_maps


def kernel(**inputs):
    nc, _ = _get_prog()
    in_maps = make_in_maps(inputs)
    res = run_bass_kernel_spmd(nc, in_maps, core_ids=list(range(8)))
    out = np.zeros((4, S, D), dtype=np.float32)
    for c in range(8):
        b, h = c // 2, c % 2
        out[b, TOK * h:TOK * (h + 1)] = res.results[c]["y_out"]
    return out
